# version 20
# speedup vs baseline: 1.0225x; 1.0099x over previous
import contextlib
import numpy as np

import concourse.bass as bass
import concourse.mybir as mybir

F32 = mybir.dt.float32
BF16 = mybir.dt.bfloat16
AF = mybir.ActivationFunctionType
ALU = mybir.AluOpType
AX = mybir.AxisListType

T = 2048
D = 1024
NT = T // 128
NC = D // 128
NH = 16
DH = 64
EXP_B = 16.0

ENGINES = ("sync", "vector", "scalar", "tensor", "gpsimd")

_UNAME = [0]


def uname(base):
    _UNAME[0] += 1
    return f"{base}_n{_UNAME[0]}"


class Sem:
    def __init__(self, es, nc, name):
        self.h = es.enter_context(nc.semaphore(name))
        self.n = 0


class Prog:

    CHAINED = ("vector", "scalar", "gpsimd")

    def __init__(self, nc, es):
        self.nc = nc
        self.es = es
        self.q = {e: [] for e in ENGINES}
        self._sems = {}
        self._bar = Sem(es, nc, "bar")
        self.chain = {e: Sem(es, nc, "ch_" + e) for e in self.CHAINED}

    def sem(self, name):
        if name not in self._sems:
            self._sems[name] = Sem(self.es, self.nc, name)
        return self._sems[name]

    def on(self, eng, f, inc=None, inc_by=1, wait=None, kind="compute"):
        waits = []
        if wait:
            for tok in wait:
                if tok is None:
                    continue
                s, v = tok
                waits.append((s.h, v))
        tok_out = None
        inc_pair = None
        if eng in self.CHAINED and kind == "compute":
            assert inc is None, "chained engines use the chain sem"
            ch = self.chain[eng]
            if ch.n > 0:
                waits.append((ch.h, ch.n))
            ch.n += 1
            inc_pair = (ch.h, 1)
            tok_out = (ch, ch.n)
        elif inc is not None:
            inc.n += inc_by
            inc_pair = (inc.h, inc_by)
            tok_out = (inc, inc.n)
        def run(e, f=f, waits=waits, inc_pair=inc_pair):
            for h, v in waits:
                e.wait_ge(h, v)
            ins = f(e)
            if inc_pair is not None:
                ins.then_inc(inc_pair[0], inc_pair[1])
        self.q[eng].append(run)
        return tok_out

    def wait(self, eng, toks):
        for tok in toks:
            if tok is None:
                continue
            s, v = tok
            self.q[eng].append(lambda e, h=s.h, v=v: e.wait_ge(h, v))

    def barrier(self, engines=ENGINES, wait=None):
        b = self._bar
        target = b.n + len(engines)
        wtoks = [(tok[0].h, tok[1]) for tok in (wait or []) if tok is not None]
        for e in engines:
            b.n += 1
            chw = (self.chain[e].h, self.chain[e].n) if e in self.CHAINED and self.chain[e].n > 0 else None
            def bump(en, bh=b.h, chw=chw, wtoks=wtoks):
                if chw is not None:
                    en.wait_ge(chw[0], chw[1])
                for h, v in wtoks:
                    en.wait_ge(h, v)
                en.nop().then_inc(bh, 1)
            self.q[e].append(bump)
        for e in engines:
            self.q[e].append(lambda en, bh=b.h, t=target: en.wait_ge(bh, t))

    def finalize(self, block):
        for name in ENGINES:
            qs = self.q[name]
            getattr(block, name)(lambda e, qs=qs: [f(e) for f in qs])



def host_consts():
    ident = np.eye(128, dtype=np.float32)
    tri = (np.arange(128)[None, :] >= np.arange(128)[:, None]).astype(np.float32)
    return {"identf": ident, "tri": tri}


def split_qkv_cols(w_qkv):
    w = np.asarray(w_qkv).reshape(D, NH, 3, DH)
    return [np.ascontiguousarray(w[:, :, i, :].reshape(D, D).reshape(NC, 128, D))
            for i in range(3)]



class Ctx:

    _uid = [0]

    def __init__(self, nc, es, p, ps_es=None):
        self.nc = nc
        self.es = es
        self.ps_es = ps_es or es
        self.p = p

    def sb(self, name, shape, dtype):
        Ctx._uid[0] += 1
        t = self.es.enter_context(self.nc.sbuf_tensor(
            f"{name}_u{Ctx._uid[0]}", list(shape), dtype))
        return t

    def ps(self, name, shape, dtype=F32):
        Ctx._uid[0] += 1
        t = self.ps_es.enter_context(self.nc.psum_tensor(
            f"{name}_u{Ctx._uid[0]}", list(shape), dtype))
        return t


def load_consts(cx, lsem):
    nc, p = cx.nc, cx.p
    identf_d = nc.dram_tensor("identf", [128, 128], F32, kind="ExternalInput").ap()
    tri_d = nc.dram_tensor("tri", [128, 128], F32, kind="ExternalInput").ap()
    identf = cx.sb("identf_sb", [128, 128], F32)
    identb = cx.sb("identb_sb", [128, 128], BF16)
    trib = cx.sb("trib_sb", [128, 128], BF16)
    gsem = p.sem("gld")
    t1 = p.on("sync", lambda e: e.dma_start(out=identf[:, :], in_=identf_d), inc=lsem, inc_by=16, kind="dma")
    t2 = p.on("gpsimd", lambda e: e.dma_start(out=trib[:, :], in_=tri_d), inc=gsem, inc_by=16, kind="dma")
    t3 = p.on("gpsimd", lambda e: e.dma_start(out=identb[:, :], in_=identf_d), inc=gsem, inc_by=16, kind="dma")
    tok = p.on("gpsimd", lambda e: e.nop(), wait=[t1, t3])
    return {"identf": identf, "identb": identb, "trib": trib, "tok": tok}



def emit_ln(cx, x_dram, yT, tag, ctok=None, y_tm=None, lsem=None):
    p = cx.p
    xb = cx.sb(tag + "_xb", [128, 2, D], F32)
    sq = cx.sb(tag + "_sq", [128, D], F32)
    st = cx.sb(tag + "_st", [128, NT, 8], F32)
    yb = cx.sb(tag + "_yb", [128, 2, D], BF16)
    eps = cx.sb(tag + "_eps", [128, 1], F32)
    pst = cx.ps(tag + "_pst", [128, D], BF16)
    s_v = p.sem(tag + "v")
    s_s = p.sem(tag + "s")
    s_t = p.sem(tag + "t")
    lsems = [p.sem(tag + "ldx0"), p.sem(tag + "ldx1")]
    identb = cx.consts["identb"]
    p.on("vector", lambda v: v.memset(eps[:, :], 1e-5))
    last = None
    tp_toks = [None, None]
    done_toks = [None, None]
    for t in range(NT):
        r = t % 2
        tx = p.on("sync", lambda e, t=t, r=r: e.dma_start(
            out=xb[:, r, :], in_=x_dram.rearrange("(t p) f -> p t f", p=128)[:, t, :]),
            inc=lsems[r], inc_by=16, kind="dma", wait=[done_toks[r]])
        xt = xb[:, r, :]
        s0, s1, s2, s3, s4, s5 = (st[:, t, i:i + 1] for i in range(6))
        tv1 = p.on("vector", lambda v, xt=xt, s0=s0: v.reduce_sum(s0, xt, axis=AX.X),
                   wait=[tx] + ([ctok] if t == 0 and ctok is not None else []))
        ts1 = p.on("scalar", lambda s, xt=xt, s2=s2: s.activation(sq[:, :], xt, AF.Square, accum_out=s2),
                   wait=[tx])
        p.on("vector", lambda v, s0=s0, s1=s1: v.tensor_scalar(s1, s0, 1.0 / D, None, op0=ALU.mult))
        tv2 = p.on("vector", lambda v, s3=s3, s1=s1: v.scalar_tensor_tensor(
            s3, s1, -1.0, s1, op0=ALU.mult, op1=ALU.mult), wait=[ts1])
        tv3 = p.on("vector", lambda v, s2=s2, s3=s3: v.scalar_tensor_tensor(
            s3, s2, 1.0 / D, s3, op0=ALU.mult, op1=ALU.add))
        ts2 = p.on("scalar", lambda s, s3=s3, s4=s4: s.activation(s4, s3, AF.Sqrt, bias=eps[:, 0:1]),
                   wait=[tv3])
        tv4 = p.on("vector", lambda v, s4=s4, s5=s5: v.reciprocal(s5, s4), wait=[ts2])
        s6 = st[:, t, 6:7]
        tv5 = p.on("vector", lambda v, s1=s1, s5=s5, s6=s6: v.scalar_tensor_tensor(
            s6, s1, -1.0, s5, op0=ALU.mult, op1=ALU.mult))
        tw = [tp_toks[r]] if tp_toks[r] is not None else []
        ty = p.on("scalar", lambda s, xt=xt, r=r, s5=s5, s6=s6: s.activation(
            yb[:, r, :], xt, AF.Identity, bias=s6, scale=s5), wait=[tv4, tv5] + tw)
        done_toks[r] = ty
        if y_tm is not None:
            done_toks[r] = p.on("scalar", lambda s, xt=xt, t=t, s5=s5, s6=s6: s.activation(
                y_tm[:, t, :], xt, AF.Identity, bias=s6, scale=s5))
        tts = []
        for c in range(NC):
            tts.append(p.on("tensor", lambda te, r=r, c=c: te.transpose(
                pst[:, c * 128:(c + 1) * 128], yb[:, r, c * 128:(c + 1) * 128], identb[:, :]),
                inc=s_t, wait=([ty, cx.consts["tok"], last] if c == 0 else None)))
        tp_toks[r] = tts[-1]
        last = p.on("vector", lambda v, t=t: v.tensor_copy(
            yT[:, :, t * 128:(t + 1) * 128],
            pst[:, :].rearrange("p (c t) -> p c t", t=128)), wait=[tts[-1]])
    return last



def emit_proj_featmajor(cx, yT, w_dram, outT, tag, ytok, n_out=D):
    p = cx.p
    nm = n_out // 128
    wb = cx.shared["wb"]
    pso = cx.ps(tag + "_ps", [128, 512], F32)
    s_ls = [p.sem(tag + "l0"), p.sem(tag + "l1")]
    s_m = p.sem(tag + "m")
    s_e = p.sem(tag + "e")
    ev_tok = None
    mm_g = cx.shared.get("wb_guard", [None, None])
    for m in range(nm):
        r = m % 2
        tl = p.on("gpsimd", lambda e, m=m, r=r: e.dma_start(
            out=wb[:, r, :].rearrange("p (c k) -> p c k", k=128),
            in_=w_dram[:, :, m * 128:(m + 1) * 128].transpose([1, 0, 2])),
            inc=s_ls[r], inc_by=16, kind="dma", wait=[mm_g[r]])
        for n in range(T // 512):
            mm_tok = None
            for c in range(NC):
                w = [tl, ytok, ev_tok] if c == 0 else None
                mm_tok = p.on("tensor", lambda te, r=r, c=c, n=n: te.matmul(
                    pso[:, :], wb[:, r, c * 128:(c + 1) * 128], yT[:, c, n * 512:(n + 1) * 512],
                    start=(c == 0), stop=(c == NC - 1)), inc=s_m, wait=w)
            mm_g[r] = mm_tok
            ev_tok = p.on("vector", lambda v, m=m, n=n: v.tensor_copy(
                outT[:, m, n * 512:(n + 1) * 512], pso[:, :]), wait=[mm_tok])
    cx.shared["wb_guard"] = mm_g
    return ev_tok


def emit_v_tokmajor(cx, yT, w_dram, vp, tag, ytok, ones_tok):
    p = cx.p
    wb = cx.sb(tag + "_wb", [128, NC, D], BF16)
    psv = cx.ps(tag + "_ps", [128, 2, 512], F32)
    s_l = p.sem(tag + "l")
    s_m = p.sem(tag + "m")
    s_e = p.sem(tag + "e")
    tc = p.on("gpsimd", lambda e: e.dma_start(out=wb[:, :, :], in_=w_dram.transpose([1, 0, 2])),
              inc=s_l, inc_by=16, kind="dma")
    ev_tok = None
    last_ev = {}
    for t in range(NT):
        mm_toks = [None, None]
        for c in range(NC):
            for half in range(2):
                w = [tc, ytok] if c == 0 else None
                if c == 0 and t >= 1:
                    w = [tc, ytok, last_ev.get((t - 1, half))]
                mm_toks[half] = p.on("tensor", lambda te, t=t, c=c, half=half: te.matmul(
                    psv[:, half, :], yT[:, c, t * 128:(t + 1) * 128],
                    wb[:, c, half * 512:(half + 1) * 512],
                    start=(c == 0), stop=(c == NC - 1)), inc=s_m, wait=w)
        for half in range(2):
            last_ev[(t, half)] = p.on("vector", lambda v, t=t, half=half: v.tensor_copy(
                vp[:, t, half * 8:(half + 1) * 8, 0:64],
                psv[:, half, :].rearrange("p (h d) -> p h d", d=64)),
                wait=[mm_toks[half], ones_tok])
            ev_tok = last_ev[(t, half)]
    return ev_tok



def emit_attn(cx, qT, kT, vp, ctx, tag, scale, kts_of, toks, nqc=4,
              qc_tile0=0, extra_mask=None):
    p = cx.p
    trib = cx.consts["trib"]
    identf = cx.consts["identf"]
    NS = 4
    psS = cx.ps(tag + "_psS", [128, NS, 512], F32)
    psO = cx.ps(tag + "_psO", [65, 2, 512], F32)
    psT = cx.ps(tag + "_psT", [128, 2, 4, 128], F32)
    pT = cx.sb(tag + "_pT", [128, NS, 512], BF16)
    o65 = cx.sb(tag + "_o65", [65, 2, 512], F32)
    ctxp = cx.sb(tag + "_ctxp", [128, 4, NH, 65], F32)
    rec = cx.sb(tag + "_rec", [128, 16], F32)
    scl = cx.sb(tag + "_scl", [128, 1], F32)
    bia = cx.sb(tag + "_bia", [128, 1], F32)
    s_mm = p.sem(tag + "mm")
    s_x = p.sem(tag + "x")
    s_g = p.sem(tag + "g")
    s_e = p.sem(tag + "e")
    c_tok = p.on("vector", lambda v: v.memset(scl[:, :], scale))
    c_tok2 = p.on("vector", lambda v: v.memset(bia[:, :], -EXP_B))
    p.wait("tensor", toks)
    p.wait("scalar", toks + [c_tok, c_tok2])
    p.wait("gpsimd", toks)
    st = {
        "prev_exp": [None] * NS,
        "prev_pv": [None] * NS,
        "prev_tev_b": [None, None],
        "prev_tts_b": [None, None],
        "prev_scatter_b": [None, None],
        "pv_tok": None,
        "ctx_done": [],
    }
    items = []
    tails = {}
    for qc in range(nqc):
        q0 = (qc_tile0 + qc) * 512
        for h in range(NH):
            hc, roff = h // 2, (h % 2) * 64
            ob = h % 2
            kts = list(kts_of(qc_tile0 + qc))
            def wid(kt, q0=q0):
                qlo = max(q0, kt * 128)
                qhi = min(q0 + 512, kt * 128 + 640) if extra_mask is not None else q0 + 512
                return qhi - qlo
            kts = sorted(kts, key=lambda kt: -wid(kt))
            for j, kt in enumerate(kts):
                qlo = max(q0, kt * 128)
                qhi = min(q0 + 512, kt * 128 + 640) if extra_mask is not None else q0 + 512
                w = qhi - qlo
                assert w > 0, (tag, qc, kt)
                items.append(dict(hc=hc, roff=roff, ob=ob, kt=kt, h=h, q0=q0,
                                  qlo=qlo, w=w, first=(j == 0), last=(j == len(kts) - 1),
                                  r=len(items) % NS))
            tails[len(items) - 1] = dict(h=h, ob=ob, qc=qc, q_end=(h == NH - 1))

    def emit_S(i):
        it = items[i]
        r = it["r"]
        tmm = p.on("tensor", lambda te, it=it, r=r: te.matmul(
            psS[:, r, 0:it["w"]],
            kT[it["roff"]:it["roff"] + 64, it["hc"], it["kt"] * 128:(it["kt"] + 1) * 128],
            qT[it["roff"]:it["roff"] + 64, it["hc"], it["qlo"]:it["qlo"] + it["w"]],
            start=True, stop=True), inc=s_mm,
            wait=[st["prev_exp"][r]])
        tx = p.on("scalar", lambda s, it=it, r=r: s.activation(
            pT[:, r, 0:it["w"]], psS[:, r, 0:it["w"]], AF.Exp,
            bias=bia[:, 0:1], scale=scl[:, 0:1]),
            wait=[tmm, st["prev_pv"][r]])
        st["prev_exp"][r] = tx
        mask_tok = tx
        dq = it["kt"] * 128 - it["qlo"]
        if 0 <= dq < it["w"]:
            mask_tok = p.on("gpsimd", lambda g, r=r, dq=dq: g.tensor_tensor(
                pT[:, r, dq:dq + 128], pT[:, r, dq:dq + 128], trib[:, :], op=ALU.mult),
                wait=[tx])
        if extra_mask is not None:
            dq2 = it["kt"] * 128 + 512 - it["qlo"]
            if 0 <= dq2 < it["w"]:
                ww = min(128, it["w"] - dq2)
                mask_tok = p.on("gpsimd", lambda g, r=r, dq2=dq2, ww=ww: g.tensor_tensor(
                    pT[:, r, dq2:dq2 + ww], pT[:, r, dq2:dq2 + ww], extra_mask[:, 0:ww], op=ALU.mult),
                    wait=[mask_tok])
        it["ready"] = mask_tok

    def emit_PV(i):
        it = items[i]
        r = it["r"]
        ob = it["ob"]
        extra_w = [st["prev_tev_b"][ob]] if it["first"] else []
        tpv = p.on("tensor", lambda te, it=it, r=r, ob=ob: te.matmul(
            psO[:, ob, it["qlo"] - it["q0"]:it["qlo"] - it["q0"] + it["w"]],
            vp[:, it["kt"], it["h"], :], pT[:, r, 0:it["w"]],
            start=it["first"], stop=it["last"]), inc=s_mm,
            wait=[it["ready"]] + extra_w)
        st["prev_pv"][r] = tpv
        st["pv_tok"] = tpv
        if it["last"]:
            tev = p.on("vector", lambda v, ob=ob: v.tensor_copy(o65[:, ob, :], psO[:, ob, :]),
                       wait=[tpv, st["prev_tts_b"][ob]])
            st["prev_tev_b"][ob] = tev
            tails[i]["tev"] = tev

    def emit_tail(i):
        td = tails[i]
        h, ob, qc = td["h"], td["ob"], td["qc"]
        tev = td["tev"]
        ctxp_guard = st["ctx_done"][-1] if st["ctx_done"] else None
        tts = []
        for s4 in range(4):
            tts.append(p.on("tensor", lambda te, s4=s4, ob=ob: te.transpose(
                psT[:, ob, s4, 0:65], o65[:, ob, s4 * 128:(s4 + 1) * 128], identf[0:65, 0:65]),
                inc=s_mm, wait=[tev, st["prev_scatter_b"][ob]] if s4 == 0 else None))
        st["prev_tts_b"][ob] = tts[-1]
        st["prev_scatter_b"][ob] = p.on("vector", lambda v, h=h, ob=ob: v.tensor_copy(
            ctxp[:, :, h, :], psT[:, ob, :, 0:65]),
            wait=[tts[-1], ctxp_guard])
        if td["q_end"]:
            ctok = None
            for s4 in range(4):
                tile = (qc_tile0 + qc) * 4 + s4
                tr = p.on("vector", lambda v, s4=s4: v.reciprocal(
                    rec[:, :], ctxp[:, s4, :, 64]),
                    wait=[st["prev_scatter_b"][0], st["prev_scatter_b"][1]])
                ctok = p.on("vector", lambda v, s4=s4, tile=tile: v.tensor_tensor(
                    ctx[:, tile, :].rearrange("p (h d) -> p h d", d=64),
                    ctxp[:, s4, :, 0:64],
                    rec[:, :].unsqueeze(2).broadcast_to((128, 16, 64)), op=ALU.mult),
                    wait=[tr])
            st["ctx_done"].append(ctok)

    N = len(items)
    TD = 2
    for i in range(N):
        emit_S(i)
        if i >= 1:
            emit_PV(i - 1)
        if i - 1 - TD in tails and i >= 1 + TD:
            emit_tail(i - 1 - TD)
    emit_PV(N - 1)
    for e in sorted(tails):
        if e > N - 2 - TD:
            emit_tail(e)
    return st["ctx_done"][-1]



def emit_outproj_residual(cx, ctx, wo_dram, x_dram, xout_dram, tag, ctx_tok,
                          bias_dram=None, rfac=None, rfac_tok=None):
    p = cx.p
    wb = cx.sb(tag + "_wb", [128, NC, D], BF16)
    brow = cx.sb(tag + "_b", [1, D], BF16) if bias_dram is not None else None
    ones1 = cx.sb(tag + "_o1", [1, 128], BF16) if bias_dram is not None else None
    ctxT = cx.sb(tag + "_ctxT", [128, NC, 128], BF16)
    xr = cx.sb(tag + "_xr", [128, 2, D], F32)
    xo = cx.sb(tag + "_xo", [128, 2, D], F32)
    psT = cx.ps(tag + "_psT", [128, D], BF16)
    pso = cx.ps(tag + "_pso", [128, 2, 512], F32)
    s_l = p.sem(tag + "l")
    s_l2 = p.sem(tag + "l2")
    s_m = p.sem(tag + "m")
    s_e = p.sem(tag + "e")
    s_st = p.sem(tag + "st")
    s_st2 = p.sem(tag + "st2")
    tc = p.on("gpsimd", lambda e: e.dma_start(out=wb[:, :, :], in_=wo_dram.transpose([1, 0, 2])),
              inc=p.sem(tag + "gl"), inc_by=16, kind="dma")
    tbias = None
    to1 = None
    if bias_dram is not None:
        tbias = p.on("gpsimd", lambda e: e.dma_start(out=brow[:, :], in_=bias_dram),
                     inc=p.sem(tag + "gl2"), inc_by=16, kind="dma")
        to1 = p.on("vector", lambda v: v.memset(ones1[:, :], 1.0))
    fma_hist = {}
    st_hist = {}
    evT_prev = None
    mm_prev = [None, None]
    st_toks = []
    for t in range(NT):
        r = t % 2
        txr = p.on("sync", lambda e, t=t, r=r: e.dma_start(
            out=xr[:, r, :], in_=x_dram.rearrange("(t p) f -> p t f", p=128)[:, t, :]),
            inc=(s_l if r == 0 else s_l2), inc_by=16, kind="dma",
            wait=[fma_hist.get((t - 2, 0)), fma_hist.get((t - 2, 1))])
        tts = []
        for c in range(NC):
            tts.append(p.on("tensor", lambda te, t=t, c=c: te.transpose(
                psT[:, c * 128:(c + 1) * 128], ctx[:, t, c * 128:(c + 1) * 128],
                cx.consts["identb"][:, :]), inc=s_m,
                wait=[ctx_tok, evT_prev] if c == 0 else None))
        evT = p.on("vector", lambda v: v.tensor_copy(
            ctxT[:, :, :], psT[:, :].rearrange("p (c t) -> p c t", t=128)),
            wait=[tts[-1], mm_prev[0], mm_prev[1]])
        evT_prev = evT
        for half in range(2):
            mm = None
            for c in range(NC):
                w = [evT, tc, fma_hist.get((t - 1, half))] if c == 0 else None
                mm = p.on("tensor", lambda te, c=c, half=half: te.matmul(
                    pso[:, half, :], ctxT[:, c, :], wb[:, c, half * 512:(half + 1) * 512],
                    start=(c == 0), stop=(c == NC - 1) and bias_dram is None), inc=s_m, wait=w)
            if bias_dram is not None:
                mm = p.on("tensor", lambda te, half=half: te.matmul(
                    pso[:, half, :], ones1[:, :], brow[:, half * 512:(half + 1) * 512],
                    start=False, stop=True), inc=s_m, wait=[tbias, to1])
            mm_prev[half] = mm
            if rfac is None:
                fma_hist[(t, half)] = p.on("vector", lambda v, r=r, half=half: v.tensor_tensor(
                    xo[:, r, half * 512:(half + 1) * 512], xr[:, r, half * 512:(half + 1) * 512],
                    pso[:, half, :], op=ALU.add),
                    wait=[mm, txr, st_hist.get((t - 2,))])
            else:
                fma_hist[(t, half)] = p.on("vector", lambda v, t=t, r=r, half=half: v.scalar_tensor_tensor(
                    xo[:, r, half * 512:(half + 1) * 512], pso[:, half, :], rfac[:, t:t + 1],
                    xr[:, r, half * 512:(half + 1) * 512], op0=ALU.mult, op1=ALU.add),
                    wait=[mm, txr, st_hist.get((t - 2,)), rfac_tok])
        st_hist[(t,)] = p.on("sync", lambda e, t=t, r=r: e.dma_start(
            out=xout_dram.rearrange("(t p) f -> p t f", p=128)[:, t, :], in_=xo[:, r, :]),
            inc=(s_st if r == 0 else s_st2), inc_by=16, kind="dma", wait=[fma_hist[(t, 0)], fma_hist[(t, 1)]])
        st_toks.append(st_hist[(t,)])
    return [st_toks[-2], st_toks[-1]]



def phase_l1attn(nc, p, consts, x_d, wq_d, wk_d, wv_d, wo_d, xout_d):
    es = contextlib.ExitStack()
    with es:
        cx = Ctx(nc, es, p)
        cx.consts = consts
        qT = cx.sb("qT", [128, NC, T], BF16)
        kT = cx.sb("kT", [128, NC, T], BF16)
        vp = cx.sb("vp", [128, NT, NH, 65], BF16)
        ones_tok = p.on("vector", lambda v: v.memset(vp[:, :, :, :], 1.0))
        esA = es.enter_context(contextlib.ExitStack())
        cxA = Ctx(nc, esA, p); cxA.consts = cx.consts
        yT = cxA.sb("yT", [128, NC, T], BF16)
        cxA.shared = {"wb": cxA.sb("wbshared", [128, 2, D], BF16)}
        ytok = emit_ln(cxA, x_d, yT, "ln", ctok=cx.consts["tok"])
        tq = emit_proj_featmajor(cxA, yT, wq_d, qT, "pq", ytok)
        tk = emit_proj_featmajor(cxA, yT, wk_d, kT, "pk", ytok)
        tv = emit_v_tokmajor(cxA, yT, wv_d, vp, "pv", ytok, ones_tok)
        p.barrier()
        esA.close()
        esB = es.enter_context(contextlib.ExitStack())
        cxB = Ctx(nc, esB, p); cxB.consts = cx.consts
        ctx_t = cxB.sb("ctx", [128, NT, D], BF16)
        esAT = es.enter_context(contextlib.ExitStack())
        cxAT = Ctx(nc, esB, p, ps_es=esAT); cxAT.consts = cx.consts
        ctx_tok = emit_attn(cxAT, qT, kT, vp, ctx_t, "at", 1.0 / 32.0,
                            kts_of=lambda qt: range(0, 4 * qt + 4), toks=[tq, tk, tv])
        esAT.close()
        st_last = emit_outproj_residual(cxB, ctx_t, wo_d, x_d, xout_d, "op", ctx_tok)
        p.wait("sync", st_last)
        p.wait("gpsimd", st_last)
    return st_last


def build_prog_l1attn():
    nc = bass.Bass("TRN2", target_bir_lowering=False, debug=False)
    es = contextlib.ExitStack()
    with es:
        p = Prog(nc, es)
        cx = Ctx(nc, es, p)
        x_d = nc.dram_tensor("x", [T, D], F32, kind="ExternalInput").ap()
        wq_d = nc.dram_tensor("wq", [NC, 128, D], F32, kind="ExternalInput").ap()
        wk_d = nc.dram_tensor("wk", [NC, 128, D], F32, kind="ExternalInput").ap()
        wv_d = nc.dram_tensor("wv", [NC, 128, D], F32, kind="ExternalInput").ap()
        wo_d = nc.dram_tensor("wo", [NC, 128, D], F32, kind="ExternalInput").ap()
        xout_d = nc.dram_tensor("xout", [T, D], F32, kind="ExternalOutput").ap()
        consts = load_consts(cx, p.sem("ld"))
        phase_l1attn(nc, p, consts, x_d, wq_d, wk_d, wv_d, wo_d, xout_d)
        block = es.enter_context(nc.Block())
        p.finalize(block)
    return nc



def rope_tables_l0a():
    inv = 1.0 / (10000.0 ** (np.arange(0, 64, 2, dtype=np.float64) / 64.0))
    pos = np.arange(T, dtype=np.float64)
    fr = np.outer(pos, inv)
    cos = np.cos(fr).astype(np.float32)
    sin = np.sin(fr).astype(np.float32)
    COS = np.ones((T, 64), np.float32)
    COS[:, 0:32] = cos
    SS = np.ascontiguousarray((-sin))
    return (np.ascontiguousarray(COS.reshape(NT, 128, 64)),
            np.ascontiguousarray(SS.reshape(NT, 128, 32)))


def rope_tables_local():
    inv = 1.0 / (10000.0 ** (np.arange(0, 64, 2, dtype=np.float64) / 64.0))
    pos = np.arange(T, dtype=np.float64)
    fr = np.outer(pos, inv)
    fr2 = np.concatenate([fr, fr], axis=1)
    cos = np.cos(fr2).astype(np.float32)
    sin = np.sin(fr2).astype(np.float32)
    SS = np.empty((T, 64), np.float32)
    SS[:, 0:32] = -sin[:, 0:32]
    SS[:, 32:64] = sin[:, 32:64]
    return (np.ascontiguousarray(cos.reshape(NT, 128, 64)),
            np.ascontiguousarray(SS.reshape(NT, 128, 64)))


def emit_qk_rope_T(cx, yT, w_dram, b_dram, outT, tag, ytok, cos_d, ss_d, mode, tick=None):
    p = cx.p
    ssw = 32 if mode == "l0a" else 64
    sh = cx.shared
    if "rope_wb" not in sh:
        sh["rope_wb"] = cx.sb("rope_wb", [128, NC, D], BF16)
        sh["rope_qf"] = cx.sb("rope_qf", [128, 2, D], F32)
        sh["rope_t2"] = cx.sb("rope_t2", [128, 2, 1024], F32)
        sh["rope_qr"] = cx.sb("rope_qr", [128, 2, D], BF16)
        sh["rope_cos"] = cx.sb("rope_cos", [128, 2, 64], F32)
        sh["rope_ss"] = cx.sb("rope_ss", [128, 2, 64], F32)
        sh["rope_g"] = {"rope_done": [None, None], "tp_done": [None, None],
                        "ev_prev": None, "mm_prev": [None, None], "w_guard": None}
    wb = sh["rope_wb"]
    qf = sh["rope_qf"]
    t2 = sh["rope_t2"]
    qr = sh["rope_qr"]
    cosb = sh["rope_cos"]
    ssb = sh["rope_ss"]
    g = sh["rope_g"]
    brow = cx.sb(tag + "_b", [1, D], BF16) if b_dram is not None else None
    ones1 = cx.sb(tag + "_o1", [1, 128], BF16) if b_dram is not None else None
    if "rope_psq" not in sh:
        sh["rope_psq"] = cx.ps("rope_psq", [128, 2, 512], F32)
        sh["rope_pst"] = cx.ps("rope_pst", [128, D], BF16)
    psq = sh["rope_psq"]
    pst = sh["rope_pst"]
    s_lw = p.sem(tag + "lw")
    s_lc = [p.sem(tag + "lc0"), p.sem(tag + "lc1")]
    s_m = p.sem(tag + "m")
    s_t = p.sem(tag + "t")
    identb = cx.consts["identb"]
    tw = p.on("gpsimd", lambda e: e.dma_start(out=wb[:, :, :], in_=w_dram.transpose([1, 0, 2])),
              inc=s_lw, inc_by=16, kind="dma", wait=[g["w_guard"]])
    tb = to1 = None
    if b_dram is not None:
        tb = p.on("gpsimd", lambda e: e.dma_start(out=brow[:, :], in_=b_dram),
                  inc=p.sem(tag + "lb"), inc_by=16, kind="dma")
        to1 = p.on("vector", lambda v: v.memset(ones1[:, :], 1.0))
    rope_done = g["rope_done"]
    tp_done = g["tp_done"]
    ev_prev = g["ev_prev"]
    mm_prev = g["mm_prev"]
    for t in range(NT):
        r = t % 2
        tcs = p.on("sync", lambda e, t=t, r=r: e.dma_start(out=cosb[:, r, :], in_=cos_d[t]),
                   inc=s_lc[r], inc_by=16, kind="dma", wait=[rope_done[r]])
        tss = p.on("sync", lambda e, t=t, r=r: e.dma_start(out=ssb[:, r, 0:ssw], in_=ss_d[t]),
                   inc=s_lc[r], inc_by=16, kind="dma")
        ev_toks = []
        for half in range(2):
            mm = None
            for c in range(NC):
                w = ([tw, ytok, g.setdefault("psq_ev", [None, None])[half]]
                     if c == 0 else None)
                mm = p.on("tensor", lambda te, t=t, c=c, half=half: te.matmul(
                    psq[:, half, :], yT[:, c, t * 128:(t + 1) * 128],
                    wb[:, c, half * 512:(half + 1) * 512],
                    start=(c == 0), stop=(c == NC - 1) and b_dram is None), inc=s_m, wait=w)
            if b_dram is not None:
                mm = p.on("tensor", lambda te, half=half: te.matmul(
                    psq[:, half, :], ones1[:, :], brow[:, half * 512:(half + 1) * 512],
                    start=False, stop=True), inc=s_m, wait=[to1, tb])
            mm_prev[half] = mm
            tev_ = p.on("vector", lambda v, r=r, half=half: v.tensor_copy(
                qf[:, r, half * 512:(half + 1) * 512], psq[:, half, :]),
                wait=[mm, rope_done[r]])
            ev_toks.append(tev_)
            g.setdefault("psq_ev", [None, None])[half] = tev_
        tr1 = p.on("vector", lambda v, r=r: v.tensor_tensor(
            t2[:, r, 0:NH * ssw].rearrange("p (h j) -> p h j", j=ssw),
            _swap_view(qf, r, mode),
            ssb[:, r, 0:ssw].unsqueeze(1).broadcast_to((128, NH, ssw)), op=ALU.mult),
            wait=[ev_toks[0], ev_toks[1], tss])
        tr2 = p.on("vector", lambda v, r=r: v.tensor_tensor(
            qr[:, r, :].rearrange("p (h j) -> p h j", j=64),
            qf[:, r, :].rearrange("p (h j) -> p h j", j=64),
            cosb[:, r, :].unsqueeze(1).broadcast_to((128, NH, 64)), op=ALU.mult),
            wait=[tp_done[r]])
        if mode == "l0a":
            tr3 = p.on("vector", lambda v, r=r: v.tensor_tensor(
                qr[:, r, :].rearrange("p (h j) -> p h j", j=64)[:, :, 0:32],
                qr[:, r, :].rearrange("p (h j) -> p h j", j=64)[:, :, 0:32],
                t2[:, r, 0:512].rearrange("p (h j) -> p h j", j=32), op=ALU.add))
        else:
            tr3 = p.on("vector", lambda v, r=r: v.tensor_tensor(
                qr[:, r, :], qr[:, r, :], t2[:, r, 0:1024], op=ALU.add))
        rope_done[r] = tr3
        tts = []
        for c in range(NC):
            tts.append(p.on("tensor", lambda te, r=r, c=c: te.transpose(
                pst[:, c * 128:(c + 1) * 128], qr[:, r, c * 128:(c + 1) * 128], identb[:, :]),
                inc=s_t, wait=[tr3, cx.consts["tok"], ev_prev] if c == 0 else None))
        tp_done[r] = tts[-1]
        ev_prev = p.on("vector", lambda v, t=t: v.tensor_copy(
            outT[:, :, t * 128:(t + 1) * 128],
            pst[:, :].rearrange("p (c t) -> p c t", t=128)), wait=[tts[-1]])
        if tick is not None:
            tick()
    g["rope_done"] = rope_done
    g["tp_done"] = tp_done
    g["ev_prev"] = ev_prev
    g["mm_prev"] = mm_prev
    g["w_guard"] = mm_prev[1]
    return ev_prev


def _swap_view(qf, r, mode):
    ap = qf[:, r, :]
    if mode == "l0a":
        return bass.AP(ap.tensor, ap.offset + 16,
                       [list(ap.ap[0]), [64, NH], [-16, 2], [1, 16]])
    else:
        return bass.AP(ap.tensor, ap.offset + 32,
                       [list(ap.ap[0]), [64, NH], [-32, 2], [1, 32]])


def emit_v_bias(cx, yT, w_dram, b_dram, vp, tag, ytok, ones_tok, n_tiles=NT, tick=None):
    p = cx.p
    wguard = None
    if "rope_wb" in cx.shared:
        wb = cx.shared["rope_wb"]
        wguard = cx.shared["rope_g"]["w_guard"]
    else:
        wb = cx.sb(tag + "_wb", [128, NC, D], BF16)
    brow = cx.sb(tag + "_b", [1, D], BF16) if b_dram is not None else None
    ones1 = cx.sb(tag + "_o1", [1, 128], BF16) if b_dram is not None else None
    psq_ev = [None, None]
    if "rope_psq" in cx.shared:
        psv = cx.shared["rope_psq"]
        psq_ev = cx.shared["rope_g"].get("psq_ev", [None, None]) if False else cx.shared.get("rope_g", {}).get("psq_ev", [None, None])
    else:
        psv = cx.ps(tag + "_ps", [128, 2, 512], F32)
    s_l = p.sem(tag + "l")
    s_m = p.sem(tag + "m")
    tc = p.on("gpsimd", lambda e: e.dma_start(out=wb[:, :, :], in_=w_dram.transpose([1, 0, 2])),
              inc=s_l, inc_by=16, kind="dma", wait=[wguard])
    tb = to1 = None
    if b_dram is not None:
        tb = p.on("gpsimd", lambda e: e.dma_start(out=brow[:, :], in_=b_dram),
                  inc=p.sem(tag + "lb"), inc_by=16, kind="dma")
        to1 = p.on("vector", lambda v: v.memset(ones1[:, :], 1.0))
    ev_tok = None
    last_ev = {}
    for t in range(n_tiles):
        mm_toks = [None, None]
        for half in range(2):
            for c in range(NC):
                w = [tc, ytok, psq_ev[half]] if c == 0 else None
                if c == 0 and t >= 1:
                    w = [tc, ytok, last_ev.get((t - 1, half))]
                mmv = p.on("tensor", lambda te, t=t, c=c, half=half: te.matmul(
                    psv[:, half, :], yT[:, c, t * 128:(t + 1) * 128],
                    wb[:, c, half * 512:(half + 1) * 512],
                    start=(c == 0), stop=(c == NC - 1) and b_dram is None), inc=s_m, wait=w)
            if b_dram is not None:
                mmv = p.on("tensor", lambda te, t=t, half=half: te.matmul(
                    psv[:, half, :], ones1[:, :], brow[:, half * 512:(half + 1) * 512],
                    start=False, stop=True), inc=s_m, wait=[to1, tb])
            mm_toks[half] = mmv
        for half in range(2):
            last_ev[(t, half)] = p.on("vector", lambda v, t=t, half=half: v.tensor_copy(
                vp[:, t, half * 8:(half + 1) * 8, 0:64],
                psv[:, half, :].rearrange("p (h d) -> p h d", d=64)),
                wait=[mm_toks[half], ones_tok])
            ev_tok = last_ev[(t, half)]
        if tick is not None:
            tick()
    return ev_tok



def phase_l0attn(nc, p, consts, x_d, wq_d, wk_d, wv_d, wo_d,
                 bq_d, bk_d, bv_d, bd_d, cos_d, ss_d, xout_d):
    es = contextlib.ExitStack()
    with es:
        cx = Ctx(nc, es, p)
        cx.consts = consts
        qT = cx.sb("qT", [128, NC, T], BF16)
        kT = cx.sb("kT", [128, NC, T], BF16)
        vp = cx.sb("vp", [128, NT, NH, 65], BF16)
        ones_tok = p.on("vector", lambda v: v.memset(vp[:, :, :, :], 1.0))
        esA = es.enter_context(contextlib.ExitStack())
        cxA = Ctx(nc, esA, p); cxA.consts = cx.consts; cxA.shared = {}
        yT = cxA.sb("yT", [128, NC, T], BF16)
        ytok = emit_ln(cxA, x_d, yT, "ln", ctok=cx.consts["tok"])
        t_q = emit_qk_rope_T(cxA, yT, wq_d, bq_d, qT, "pq", ytok, cos_d, ss_d, "l0a")
        t_k = emit_qk_rope_T(cxA, yT, wk_d, bk_d, kT, "pk", ytok, cos_d, ss_d, "l0a")
        t_v = emit_v_bias(cxA, yT, wv_d, bv_d, vp, "pv", ytok, ones_tok)
        p.barrier()
        esA.close()
        esB = es.enter_context(contextlib.ExitStack())
        cxB = Ctx(nc, esB, p); cxB.consts = cx.consts
        ctx_t = cxB.sb("ctx", [128, NT, D], BF16)
        esAT = es.enter_context(contextlib.ExitStack())
        cxAT = Ctx(nc, esB, p, ps_es=esAT); cxAT.consts = cx.consts
        ctx_tok = emit_attn(cxAT, qT, kT, vp, ctx_t, "at", 1.0 / 8.0,
                            kts_of=lambda qt: range(0, 4 * qt + 4), toks=[t_q, t_k, t_v])
        esAT.close()
        st_last = emit_outproj_residual(cxB, ctx_t, wo_d, x_d, xout_d, "op", ctx_tok,
                                        bias_dram=bd_d)
        p.wait("sync", st_last)
        p.wait("gpsimd", st_last)
    return st_last


def build_prog_l0attn():
    nc = bass.Bass("TRN2", target_bir_lowering=False, debug=False)
    es = contextlib.ExitStack()
    with es:
        p = Prog(nc, es)
        cx = Ctx(nc, es, p)
        x_d = nc.dram_tensor("x", [T, D], F32, kind="ExternalInput").ap()
        wq_d = nc.dram_tensor("wq", [NC, 128, D], F32, kind="ExternalInput").ap()
        wk_d = nc.dram_tensor("wk", [NC, 128, D], F32, kind="ExternalInput").ap()
        wv_d = nc.dram_tensor("wv", [NC, 128, D], F32, kind="ExternalInput").ap()
        wo_d = nc.dram_tensor("wo", [NC, 128, D], F32, kind="ExternalInput").ap()
        bq_d = nc.dram_tensor("bq", [1, D], F32, kind="ExternalInput").ap()
        bk_d = nc.dram_tensor("bk", [1, D], F32, kind="ExternalInput").ap()
        bv_d = nc.dram_tensor("bv", [1, D], F32, kind="ExternalInput").ap()
        bd_d = nc.dram_tensor("bd", [1, D], F32, kind="ExternalInput").ap()
        cos_d = nc.dram_tensor("cosd", [NT, 128, 64], F32, kind="ExternalInput").ap()
        ss_d = nc.dram_tensor("ssd", [NT, 128, 32], F32, kind="ExternalInput").ap()
        xout_d = nc.dram_tensor("xout", [T, D], F32, kind="ExternalOutput").ap()
        consts = load_consts(cx, p.sem("ld"))
        phase_l0attn(nc, p, consts, x_d, wq_d, wk_d, wv_d, wo_d,
                     bq_d, bk_d, bv_d, bd_d, cos_d, ss_d, xout_d)
        block = es.enter_context(nc.Block())
        p.finalize(block)
    return nc



def emit_quant(cx, p, src_ap, dst_ap, U1, U3, U4, M, S, wtok, W, wait_extra=None):
    nb = W // 16
    t1 = p.on("scalar", lambda s: s.activation(U1[:, 0:W], src_ap, AF.Abs),
              wait=[wtok] + (wait_extra or []))
    t2 = p.on("vector", lambda v: v.reduce_max(
        M[:, 0:nb].rearrange("p (a b) -> p a b", b=1),
        U1[:, 0:W].rearrange("p (a b) -> p a b", b=16), axis=AX.X), wait=[t1])
    t3 = p.on("vector", lambda v: v.tensor_scalar(
        S[:, 0:nb], M[:, 0:nb], 1.0 / 6.0, 1e-12, op0=ALU.mult, op1=ALU.max), wait=[t2])
    t4 = p.on("vector", lambda v: v.reciprocal(M[:, 0:nb], S[:, 0:nb]), wait=[t3])
    bc = lambda x: x[:, 0:nb].unsqueeze(2).broadcast_to((128, nb, 16))
    t5 = p.on("vector", lambda v: v.tensor_tensor(
        U1[:, 0:W].rearrange("p (a b) -> p a b", b=16),
        U1[:, 0:W].rearrange("p (a b) -> p a b", b=16), bc(M), op=ALU.mult), wait=[t4])
    ta = p.on("vector", lambda g: g.tensor_scalar(U3[:, 0:W], U1[:, 0:W], 0.25, None, op0=ALU.is_gt), wait=[t5])
    for c in (0.75, 1.25, 1.75):
        ta = p.on("vector", lambda g, c=c: g.scalar_tensor_tensor(
            U3[:, 0:W], U1[:, 0:W], c, U3[:, 0:W], op0=ALU.is_gt, op1=ALU.add))
    tb = p.on("vector", lambda g: g.tensor_scalar(U4[:, 0:W], U1[:, 0:W], 5.0, 2.0, op0=ALU.is_gt, op1=ALU.mult))
    for c in (2.5, 3.5):
        tb = p.on("vector", lambda g, c=c: g.scalar_tensor_tensor(
            U4[:, 0:W], U1[:, 0:W], c, U4[:, 0:W], op0=ALU.is_gt, op1=ALU.add))
    tq = p.on("vector", lambda v: v.scalar_tensor_tensor(
        U3[:, 0:W], U3[:, 0:W], 0.5, U4[:, 0:W], op0=ALU.mult, op1=ALU.add), wait=[ta, tb])
    tsg = p.on("scalar", lambda s: s.activation(U1[:, 0:W], src_ap, AF.Sign), wait=[ta, tb])
    tm = p.on("vector", lambda v: v.tensor_tensor(U3[:, 0:W], U3[:, 0:W], U1[:, 0:W], op=ALU.mult),
              wait=[tq, tsg])
    tout = p.on("vector", lambda v: v.tensor_tensor(
        dst_ap.rearrange("p (a b) -> p a b", b=16),
        U3[:, 0:W].rearrange("p (a b) -> p a b", b=16), bc(S), op=ALU.mult), wait=[tm])
    return tout



II = 2816
NIC = II // 128
NSL = [(0, 512), (512, 1024), (1024, 1536), (1536, 2048), (2048, 2560), (2560, 2816)]


def phase_l1ffn(nc, p, consts, x_d, wg_d, wu_d, wd_d, xout_d, wcache=None, first=True):
    es = contextlib.ExitStack()
    with es:
        cx = Ctx(nc, es, p)
        cx.consts = consts
        identb = cx.consts["identb"]
        wgT = cx.sb("wgT", [128, NC, II], BF16)
        wuT = cx.sb("wuT", [128, NC, II], BF16)
        wdT = cx.sb("wdT", [128, NIC, D], BF16)
        U1 = cx.sb("U1", [128, II], F32)
        U3 = cx.sb("U3", [128, II], BF16)
        U4 = cx.sb("U4", [128, II], BF16)
        M = cx.sb("Mq", [128, II // 16], F32)
        S = cx.sb("Sq", [128, II // 16], F32)
        s_lw = [p.sem("lw0"), p.sem("lw1")]
        s_t = p.sem("wt")
        extra_st = []

        pst = cx.ps("pstC", [128, 512], BF16) if first else None
        esA = es.enter_context(contextlib.ExitStack())
        cxA = Ctx(nc, esA, p); cxA.consts = cx.consts
        wstage = cxA.sb("wstage", [128, 2, II], F32) if first else None
        wqb = cxA.sb("wqb", [128, II], BF16) if first else None

        carry = {"q": None, "ev": None}

        def prep_weights(w_dram, n_tiles, width, outT):
            toks = [carry["ev"]]
            prev_q = carry["q"]
            for i in range(n_tiles):
                r = i % 2
                tl = p.on("sync", lambda e, i=i, r=r, width=width, w_dram=w_dram: e.dma_start(
                    out=wstage[:, r, 0:width], in_=w_dram[i]),
                    inc=s_lw[r], inc_by=16, kind="dma", wait=[prev_q])
                tq = emit_quant(cx, p, wstage[:, r, 0:width], wqb[:, 0:width],
                                U1, U3, U4, M, S, tl, width, wait_extra=[toks[-1]])
                prev_q = tq
                nblk = width // 128
                for b0 in range(0, nblk, 4):
                    tts = []
                    for b in range(b0, min(b0 + 4, nblk)):
                        tts.append(p.on("tensor", lambda te, b=b, b0=b0: te.transpose(
                            pst[:, (b - b0) * 128:(b - b0 + 1) * 128],
                            wqb[:, b * 128:(b + 1) * 128], identb[:, :]),
                            inc=s_t, wait=[tq, cx.consts["tok"], toks[-1]] if b == b0 else None))
                    nb2 = min(b0 + 4, nblk) - b0
                    toks.append(p.on("vector", lambda v, i=i, b0=b0, nb2=nb2, outT=outT: v.tensor_copy(
                        outT[:, b0:b0 + nb2, i * 128:(i + 1) * 128],
                        pst[:, 0:nb2 * 128].rearrange("p (c t) -> p c t", t=128)),
                        wait=[tts[-1]]))
            carry["q"] = prev_q
            carry["ev"] = toks[-1]
            return toks[-1]

        if first:
            tg = prep_weights(wg_d, NIC, D, wgT)
            tu = prep_weights(wu_d, NIC, D, wuT)
            td = prep_weights(wd_d, NC, II, wdT)
            if wcache is not None:
                s_wst = p.sem("wcs")
                p.on("gpsimd", lambda e: e.dma_start(
                    out=wcache["g"], in_=wgT[:, :, :]), inc=s_wst, inc_by=16,
                    kind="dma", wait=[tg])
                p.on("gpsimd", lambda e: e.dma_start(
                    out=wcache["u"], in_=wuT[:, :, :]), inc=s_wst, inc_by=16,
                    kind="dma", wait=[tu])
                extra_st.append(p.on("gpsimd", lambda e: e.dma_start(
                    out=wcache["d"], in_=wdT[:, :, :]), inc=s_wst, inc_by=16,
                    kind="dma", wait=[td]))
            p.barrier()
        else:
            tg = p.on("sync", lambda e: e.dma_start(out=wgT[:, :, :], in_=wcache["g"]),
                      inc=p.sem("wcl"), inc_by=16, kind="dma")
            tu = p.on("gpsimd", lambda e: e.dma_start(out=wuT[:, :, :], in_=wcache["u"]),
                      inc=p.sem("wclu"), inc_by=16, kind="dma")
            td = p.on("sync", lambda e: e.dma_start(out=wdT[:, :, :], in_=wcache["d"]),
                      inc=p.sem("wcld"), inc_by=16, kind="dma")
        esA.close()

        esB = es.enter_context(contextlib.ExitStack())
        cxB = Ctx(nc, esB, p); cxB.consts = cx.consts
        xb = cxB.sb("xbC", [128, 2, D], F32)
        yf = cxB.sb("yfC", [128, 1, D], F32)
        st = cxB.sb("stC", [128, NT, 8], F32)
        epsb = cxB.sb("epsC", [128, 1], F32)
        xq = cxB.sb("xqC", [128, D], BF16)
        Ux1 = cxB.sb("Ux1", [128, D], F32)
        Ux3 = cxB.sb("Ux3", [128, D], BF16)
        Ux4 = cxB.sb("Ux4", [128, D], BF16)
        Mx = cxB.sb("MxC", [128, D // 16], F32)
        Sx = cxB.sb("SxC", [128, D // 16], F32)
        xqT = cxB.sb("xqTC", [128, NC, 128], BF16)
        HB = cxB.sb("HB", [128, II], F32)
        hq = cxB.sb("hqC", [128, II], BF16)
        hqT = cxB.sb("hqTC", [128, NIC, 128], BF16)
        psG = cxB.ps("psG", [128, 512], F32)
        psU = cxB.ps("psU", [128, 512], F32)
        psD = cxB.ps("psD", [128, 2, 512], F32)
        pstx = cxB.ps("pstx", [128, D], BF16)
        s_lx = [p.sem("lx0"), p.sem("lx1")]
        s_m = p.sem("mC")
        s_tx = p.sem("txC")
        s_st = [p.sem("stC0"), p.sem("stC1")]
        p.on("vector", lambda v: v.memset(epsb[:, :], 1e-5))
        xload_guard = [None, None]
        store_guard = [None, None]
        prev = {}
        for t in range(NT):
            r = t % 2
            tx = p.on("sync", lambda e, t=t, r=r: e.dma_start(
                out=xb[:, r, :], in_=x_d.rearrange("(t p) f -> p t f", p=128)[:, t, :]),
                inc=s_lx[r], inc_by=16, kind="dma", wait=[xload_guard[r]])
            xt = xb[:, r, :]
            s0, s1, s2, s3, s4, s5, s6 = (st[:, t, i:i + 1] for i in range(7))
            tv1 = p.on("vector", lambda v, xt=xt, s0=s0: v.reduce_sum(s0, xt, axis=AX.X),
                       wait=[tx] + ([tg, tu, td, cx.consts["tok"]] if t == 0 else []))
            ts1 = p.on("scalar", lambda s, xt=xt, s2=s2: s.activation(Ux3[:, 0:D], xt, AF.Square, accum_out=s2),
                       wait=[tx, prev.get("xq_tout")])
            p.on("vector", lambda v, s0=s0, s1=s1: v.tensor_scalar(s1, s0, 1.0 / D, None, op0=ALU.mult))
            p.on("vector", lambda v, s3=s3, s1=s1: v.scalar_tensor_tensor(
                s3, s1, -1.0, s1, op0=ALU.mult, op1=ALU.mult), wait=[ts1])
            tv3 = p.on("vector", lambda v, s2=s2, s3=s3: v.scalar_tensor_tensor(
                s3, s2, 1.0 / D, s3, op0=ALU.mult, op1=ALU.add))
            ts2 = p.on("scalar", lambda s, s3=s3, s4=s4: s.activation(s4, s3, AF.Sqrt, bias=epsb[:, 0:1]),
                       wait=[tv3])
            tv4 = p.on("vector", lambda v, s4=s4, s5=s5: v.reciprocal(s5, s4), wait=[ts2])
            tv5 = p.on("vector", lambda v, s1=s1, s5=s5, s6=s6: v.scalar_tensor_tensor(
                s6, s1, -1.0, s5, op0=ALU.mult, op1=ALU.mult))
            tyf = p.on("scalar", lambda s, xt=xt, s5=s5, s6=s6: s.activation(
                yf[:, 0, :], xt, AF.Identity, bias=s6, scale=s5),
                wait=[tv4, tv5, prev.get(("yf",)), store_guard[0], store_guard[1]])
            tqx = emit_quant(cx, p, yf[:, 0, :], xq[:, :], Ux1, Ux3, Ux4, Mx, Sx, tyf, D,
                             wait_extra=[prev.get("xq_rd"), prev.get("xq_tout")])
            prev[("yf",)] = tqx
            prev["xq_tout"] = tqx
            ttx = []
            for c in range(NC):
                ttx.append(p.on("tensor", lambda te, c=c: te.transpose(
                    pstx[:, c * 128:(c + 1) * 128], xq[:, c * 128:(c + 1) * 128], identb[:, :]),
                    inc=s_tx, wait=[tqx, prev.get("xqT_rd")] if c == 0 else None))
            tevx = p.on("vector", lambda v: v.tensor_copy(
                xqT[:, :, :], pstx[:, :].rearrange("p (c t) -> p c t", t=128)), wait=[ttx[-1]])
            prev["xq_rd"] = ttx[-1]
            h_toks = []
            last_mmG = None
            last_mmU = None
            for (a, b) in NSL:
                wmm = b - a
                mmG = None
                mmU = None
                for c in range(NC):
                    w = [tevx, prev.get(("hG",))] if c == 0 else None
                    mmG = p.on("tensor", lambda te, c=c, a=a, b=b: te.matmul(
                        psG[:, 0:b - a], xqT[:, c, :], wgT[:, c, a:b],
                        start=(c == 0), stop=(c == NC - 1)), inc=s_m, wait=w)
                for c in range(NC):
                    w = [tevx, prev.get(("hU",))] if c == 0 else None
                    mmU = p.on("tensor", lambda te, c=c, a=a, b=b: te.matmul(
                        psU[:, 0:b - a], xqT[:, c, :], wuT[:, c, a:b],
                        start=(c == 0), stop=(c == NC - 1)), inc=s_m, wait=w)
                tsil = p.on("scalar", lambda s, a=a, b=b: s.activation(
                    U1[:, a:b], psG[:, 0:b - a], AF.Silu), wait=[mmG, prev.get("hq_u1")])
                prev[("hG",)] = tsil
                th = p.on("vector", lambda v, a=a, b=b: v.tensor_tensor(
                    HB[:, a:b], U1[:, a:b], psU[:, 0:b - a], op=ALU.mult),
                    wait=[tsil, mmU, prev.get("hb_rd")])
                prev[("hU",)] = th
                h_toks.append(th)
            tqh = emit_quant(cx, p, HB[:, :], hq[:, :], U1, U3, U4, M, S, h_toks[-1],
                             II, wait_extra=[prev.get("hq_rd"), prev.get("hq_u1")] + h_toks[:-1])
            prev["hb_rd"] = tqh
            prev["hq_u1"] = tqh
            tth_ev = []
            for b0 in range(0, NIC, 4):
                tts = []
                for b in range(b0, min(b0 + 4, NIC)):
                    tts.append(p.on("tensor", lambda te, b=b, b0=b0: te.transpose(
                        pstx[:, (b - b0) * 128:(b - b0 + 1) * 128],
                        hq[:, b * 128:(b + 1) * 128], identb[:, :]),
                        inc=s_tx, wait=[tqh, tevx if b0 == 0 else None,
                                        prev.get("hqT_rd"), tth_ev[-1] if tth_ev else prev.get("xqT_rd2")] if b == b0 else None))
                nb2 = min(b0 + 4, NIC) - b0
                tth_ev.append(p.on("vector", lambda v, b0=b0, nb2=nb2: v.tensor_copy(
                    hqT[:, b0:b0 + nb2, :],
                    pstx[:, 0:nb2 * 128].rearrange("p (c t) -> p c t", t=128)),
                    wait=[tts[-1]]))
            prev["hq_rd"] = tth_ev[-1]
            prev["xqT_rd2"] = tth_ev[-1]
            fma = [None, None]
            for half in range(2):
                mm = None
                for ic in range(NIC):
                    w = [tth_ev[-1], prev.get(("dn", half))] if ic == 0 else None
                    mm = p.on("tensor", lambda te, ic=ic, half=half: te.matmul(
                        psD[:, half, :], hqT[:, ic, :], wdT[:, ic, half * 512:(half + 1) * 512],
                        start=(ic == 0), stop=(ic == NIC - 1)), inc=s_m, wait=w)
                prev["xqT_rd"] = mm if half == 1 else prev.get("xqT_rd")
                fma[half] = p.on("vector", lambda v, r=r, half=half: v.scalar_tensor_tensor(
                    yf[:, 0, half * 512:(half + 1) * 512], psD[:, half, :], 0.5,
                    xb[:, r, half * 512:(half + 1) * 512], op0=ALU.mult, op1=ALU.add),
                    wait=[mm, tx, store_guard[0], store_guard[1]])
                prev[("dn", half)] = fma[half]
            prev["hqT_rd"] = prev[("dn", 1)]
            xload_guard[r] = fma[1]
            tst = p.on("sync", lambda e, t=t, r=r: e.dma_start(
                out=xout_d.rearrange("(t p) f -> p t f", p=128)[:, t, :], in_=yf[:, 0, :]),
                inc=s_st[r], inc_by=16, kind="dma", wait=[fma[0], fma[1]])
            store_guard[r] = tst
        p.wait("sync", store_guard)
        p.wait("gpsimd", store_guard)
        st_ret = [store_guard[0], store_guard[1]] + extra_st
    return st_ret


def build_prog_l1ffn():
    nc = bass.Bass("TRN2", target_bir_lowering=False, debug=False)
    es = contextlib.ExitStack()
    with es:
        p = Prog(nc, es)
        cx = Ctx(nc, es, p)
        x_d = nc.dram_tensor("x", [T, D], F32, kind="ExternalInput").ap()
        wg_d = nc.dram_tensor("wg", [NIC, 128, D], F32, kind="ExternalInput").ap()
        wu_d = nc.dram_tensor("wu", [NIC, 128, D], F32, kind="ExternalInput").ap()
        wd_d = nc.dram_tensor("wd", [NC, 128, II], F32, kind="ExternalInput").ap()
        xout_d = nc.dram_tensor("xout", [T, D], F32, kind="ExternalOutput").ap()
        consts = load_consts(cx, p.sem("ld"))
        phase_l1ffn(nc, p, consts, x_d, wg_d, wu_d, wd_d, xout_d)
        block = es.enter_context(nc.Block())
        p.finalize(block)
    return nc



W_WIN = 512
NW = 3
NQSEL = 256
NKVSEL = 1024
KQ = 288.0
KKV = 1152.0
N_ITERS = 20
EPS_CD = 0.03
EPS_INIT = 4.0
EPS_DECAY = 0.7


def cd_eps_schedule():
    return [max(EPS_INIT * EPS_DECAY ** i, EPS_CD) for i in range(N_ITERS)]


def emit_router(cx, sQ, sKV, tags, stoks, ikv, tl2, bufs, eng="vector", res=None):
    import math
    p = cx.p
    nc = cx.nc
    mq, mkv, scr, red = bufs
    out_toks = []
    import os

    def do_router(s_t, nseg, seg, kk, offs, mask, tag):
        logk = math.log(kk)
        bneg = scr[:, 0, 0:nseg * seg]
        t1 = scr[:, 1, 0:nseg * seg]
        sv = s_t[:, 0:nseg * seg].rearrange("p (a b) -> p a b", b=seg)
        t1s = t1.rearrange("p (a b) -> p a b", b=seg)
        bnegs = bneg.rearrange("p (a b) -> p a b", b=seg)
        m_ = red[:, 0, 0:nseg]
        ssum = red[:, 1, 0:nseg]
        a_ = red[:, 2, 0:nseg]
        tmp = red[:, 3, 0:nseg]
        mb = lambda x: x.unsqueeze(2).broadcast_to((2, nseg, seg))
        tk = p.on(eng, lambda v: v.tensor_scalar(bneg, s_t[:, 0:nseg * seg], -1.0, None, op0=ALU.mult),
                  wait=stoks + [tl2])
        yield
        for it, ce in enumerate(cd_eps_schedule()):
            yield
            p.on(eng, lambda v: v.tensor_tensor(t1, s_t[:, 0:nseg * seg], bneg, op=ALU.add))
            p.on(eng, lambda v: v.reduce_max(
                m_.rearrange("p (a b) -> p a b", b=1), t1s, axis=AX.X))
            p.on(eng, lambda v, ce=ce: v.tensor_tensor(t1s, t1s, mb(m_), op=ALU.subtract))
            tsc = p.on(eng, lambda v, ce=ce: v.tensor_scalar(t1, t1, 1.0 / ce, None, op0=ALU.mult))
            tex = p.on("scalar", lambda s: s.activation(t1, t1, AF.Exp), wait=[tsc])
            trs = p.on(eng, lambda v: v.reduce_sum(
                ssum.rearrange("p (a b) -> p a b", b=1), t1s, axis=AX.X), wait=[tex])
            tln = p.on("scalar", lambda s: s.activation(tmp, ssum, AF.Ln), wait=[trs])
            p.on(eng, lambda v, ce=ce: v.tensor_scalar(tmp, tmp, ce, None, op0=ALU.mult), wait=[tln])
            p.on(eng, lambda v: v.tensor_tensor(tmp, tmp, m_, op=ALU.add))
            p.on(eng, lambda v, ce=ce, logk=logk: v.tensor_scalar(
                a_, tmp, -1.0, ce * logk, op0=ALU.mult, op1=ALU.add))
            p.on(eng, lambda v: v.tensor_tensor(t1s, sv, mb(a_), op=ALU.add))
            p.on(eng, lambda v: v.tensor_scalar(bneg, t1, 0.0, -1.0, op0=ALU.max, op1=ALU.mult))
        C0 = -3.119162312651975
        I_ = scr[:, 0, 0:nseg * seg]
        Z_ = scr[:, 2, 0:nseg * seg]
        key = s_t[:, 0:nseg * seg]
        p.on(eng, lambda v: v.tensor_tensor(t1s, sv, mb(a_), op=ALU.add))
        p.on(eng, lambda v: v.tensor_scalar(I_, t1, 0.0, None, op0=ALU.is_ge))
        p.on(eng, lambda v, C0=C0: v.tensor_scalar(Z_, t1, C0, None, op0=ALU.is_le))
        p.on(eng, lambda v: v.tensor_tensor(key, I_, Z_, op=ALU.add))
        p.on(eng, lambda v: v.tensor_scalar(key, key, -1.0, 1.0, op0=ALU.mult, op1=ALU.add))
        p.on(eng, lambda v: v.tensor_tensor(key, key, t1, op=ALU.mult))
        for w, off in enumerate(offs):
            p.on(eng, lambda v, w=w, off=off: v.tensor_scalar(
                t1[:, w * seg:(w + 1) * seg], ikv[:, w * seg:(w + 1) * seg],
                3999.0 + off, None, op0=ALU.add))
        p.on(eng, lambda v: v.tensor_tensor(I_, I_, t1, op=ALU.mult))
        p.on(eng, lambda v: v.tensor_tensor(key, key, I_, op=ALU.add))
        for w, off in enumerate(offs):
            p.on(eng, lambda v, w=w, off=off, seg=seg: v.tensor_scalar(
                t1[:, w * seg:(w + 1) * seg], ikv[:, w * seg:(w + 1) * seg],
                -(4001.0 + seg) + off, None, op0=ALU.add))
        p.on(eng, lambda v: v.tensor_tensor(Z_, Z_, t1, op=ALU.mult))
        tkey = p.on(eng, lambda v: v.tensor_tensor(key, key, Z_, op=ALU.add))
        lo = red[:, 0, 0:nseg]
        hi = red[:, 1, 0:nseg]
        cnt = red[:, 2, 0:nseg]
        mid = red[:, 3, 0:nseg]
        ge = red[:, 3, 4:4 + nseg]
        d1 = red[:, 2, 4:4 + nseg]
        d2 = red[:, 1, 4:4 + nseg]
        kp = NQSEL if tag == "q" else NKVSEL
        p.on(eng, lambda v: v.memset(lo, -6000.0))
        p.on(eng, lambda v: v.memset(hi, 6000.0))
        keys_seg = key.rearrange("p (a b) -> p a b", b=seg)
        for it in range(30):
            yield
            p.on(eng, lambda v: v.tensor_tensor(mid, lo, hi, op=ALU.add))
            p.on(eng, lambda v: v.tensor_scalar(mid, mid, 0.5, None, op0=ALU.mult))
            p.on(eng, lambda v: v.tensor_tensor(t1s, keys_seg, mb(mid), op=ALU.is_gt))
            p.on(eng, lambda v: v.reduce_sum(
                cnt.rearrange("p (a b) -> p a b", b=1), t1s, axis=AX.X))
            p.on(eng, lambda v, kp=kp: v.tensor_scalar(ge, cnt, float(kp), None, op0=ALU.is_ge))
            p.on(eng, lambda v: v.tensor_tensor(d1, mid, lo, op=ALU.subtract))
            p.on(eng, lambda v: v.tensor_tensor(d1, d1, ge, op=ALU.mult))
            p.on(eng, lambda v: v.tensor_tensor(lo, lo, d1, op=ALU.add))
            p.on(eng, lambda v: v.tensor_tensor(d2, mid, hi, op=ALU.subtract))
            p.on(eng, lambda v: v.tensor_tensor(hi, hi, d2, op=ALU.add))
            p.on(eng, lambda v: v.tensor_tensor(d2, d2, ge, op=ALU.mult))
            p.on(eng, lambda v: v.tensor_tensor(hi, hi, d2, op=ALU.subtract))
        tmask = p.on(eng, lambda v, mask=mask: v.tensor_tensor(
            mask[:, 0:nseg * seg].rearrange("p (a b) -> p a b", b=seg), keys_seg, mb(lo), op=ALU.is_gt))
        return tmask

    routers = [(sQ, 3, 512, KQ, [-1024.0, -512.0, 0.0], mq, "q"),
               (sKV, 1, 1536, KKV, [0.0], mkv, "kv")]
    if os.environ.get("RT_ONLY") == "q":
        routers = routers[:1]
    for args in routers:
        out_toks.append((yield from do_router(*args)))
    if len(out_toks) == 1:
        out_toks.append(out_toks[0])
    if res is not None:
        res["mq"], res["mkv"], res["toks"] = mq, mkv, out_toks


def router_iotas():
    iq = np.zeros((2, 1536), np.float32)
    for w in range(3):
        iq[:, w * 512:(w + 1) * 512] = 1.0 + 512.0 - np.arange(512, dtype=np.float32)
    ikv = np.zeros((2, 1536), np.float32)
    ikv[:, :] = 1.0 + 1536.0 - np.arange(1536, dtype=np.float32)
    return iq, ikv



def phase_l0ffn_a1(nc, p, consts, x_d, lng_d, lnb_d, wql_d, wkl_d, wvl_d, wlo_d,
                   rtq_d, rtkv_d, cos_d, ss_d, tril_d, iota_q_d, iota_kv_d,
                   xout_d, y_d, fgt0_d, fgt1_d, mkvt0_d, mkvt1_d):
    es = contextlib.ExitStack()
    with es:
        cx = Ctx(nc, es, p)
        cx.consts = consts
        trilb = cx.sb("trilb_sb", [128, 128], BF16)
        ttril = p.on("gpsimd", lambda e: e.dma_start(out=trilb[:, :], in_=tril_d),
                     inc=p.sem("gtr"), inc_by=16, kind="dma")
        bigbuf = cx.sb("bigbuf", [128, NC * T], BF16)
        xnT = bigbuf[:, :].rearrange("p (c t) -> p c t", t=T)
        rfacT = cx.sb("rfacT", [128, NT], F32)

        qT = cx.sb("qT", [128, NC, T], BF16)
        kT = cx.sb("kT", [128, NC, T], BF16)
        vp = cx.sb("vp", [128, NT, NH, 65], BF16)
        esR = es.enter_context(contextlib.ExitStack())
        cxR = Ctx(nc, esR, p)
        sQ = cxR.sb("sQ", [2, 1536], F32)
        sKV = cxR.sb("sKV", [2, 1536], F32)
        ikv_sb = cxR.sb("ikv", [2, 1536], F32)
        ones2 = cxR.sb("ones2", [2, 2], F32)
        FtT = cxR.sb("FtT", [128, 2, 12], F32)
        mkvT_sb = cxR.sb("mkvT_sb", [128, 2, 12], F32)
        mq_sb = sQ[:, :]
        mkv_sb2 = sKV[:, :]
        rscr = cxR.sb("rt_scr", [2, 3, 1536], F32)
        rred = cxR.sb("rt_red", [2, 4, 8], F32)
        r2 = rscr[:, 0, :]
        Ft = rscr[:, 1, :]
        rrow = rscr[0:1, 2, :]
        s_rl = p.sem("rtld")
        t_il = p.on("sync", lambda e: e.dma_start(out=ikv_sb[:, :], in_=iota_kv_d), inc=s_rl, inc_by=16, kind="dma")
        esL = es.enter_context(contextlib.ExitStack())
        cxL = Ctx(nc, esL, p); cxL.consts = cx.consts
        gtile = cxL.sb("gtile", [128, D], F32)
        btile = cxL.sb("btile", [128, D], F32)
        rtqT = cxL.sb("rtqT", [128, NC, 2], F32)
        rtkvT = cxL.sb("rtkvT", [128, NC, 2], F32)
        s_lg = p.sem("lg")
        tg3 = p.on("sync", lambda e: e.dma_start(out=rtqT[:, :, :], in_=rtq_d.transpose([1, 0, 2])), inc=p.sem("lg3"), inc_by=16, kind="dma")
        tg4 = p.on("sync", lambda e: e.dma_start(out=rtkvT[:, :, :], in_=rtkv_d.transpose([1, 0, 2])), inc=p.sem("lg4"), inc_by=16, kind="dma")
        with contextlib.ExitStack() as esG:
            cxG = Ctx(nc, esG, p)
            onesf = cxG.sb("onesf", [1, 128], F32)
            growf = cxG.sb("growf", [1, D], F32)
            browf = cxG.sb("browf", [1, D], F32)
            tg1 = p.on("sync", lambda e: e.dma_start(out=growf[:, :], in_=lng_d), inc=s_lg, inc_by=16, kind="dma")
            tg2 = p.on("sync", lambda e: e.dma_start(out=browf[:, :], in_=lnb_d), inc=s_lg, inc_by=16, kind="dma")
            to_f = p.on("vector", lambda v: v.memset(onesf[:, :], 1.0))
            with nc.psum_tensor(uname("psgb"), [128, 2, 512], F32) as psgb:
                gb_toks = []
                for half in range(2):
                    m1 = p.on("tensor", lambda te, half=half: te.matmul(
                        psgb[:, half, :], onesf[:, :], growf[:, half * 512:(half + 1) * 512],
                        start=True, stop=True), inc=p.sem("gbm"), wait=[to_f, tg2])
                    gb_toks.append(p.on("vector", lambda v, half=half: v.tensor_copy(
                        gtile[:, half * 512:(half + 1) * 512], psgb[:, half, :]), wait=[m1]))
                for half in range(2):
                    m1 = p.on("tensor", lambda te, half=half: te.matmul(
                        psgb[:, half, :], onesf[:, :], browf[:, half * 512:(half + 1) * 512],
                        start=True, stop=True), inc=p.sem("gbm"), wait=[gb_toks[-1]])
                    gb_toks.append(p.on("vector", lambda v, half=half: v.tensor_copy(
                        btile[:, half * 512:(half + 1) * 512], psgb[:, half, :]), wait=[m1]))
                gb_tok = gb_toks[-1]
        xb = cxL.sb("xbL", [128, 2, D], F32)
        yf = cxL.sb("yfL", [128, 2, D], F32)
        xnf = cxL.sb("xnfL", [128, D], F32)
        xnb = cxL.sb("xnbL", [128, 2, D], BF16)
        sqs = cxL.sb("sqsL", [128, D], F32)
        stt_ = cxL.sb("stL", [128, NT, 12], F32)
        epsb = cxL.sb("epsL", [128, 1], F32)
        ytr = cxL.sb("ytrL", [128, NC, 128], F32)
        esL2 = es.enter_context(contextlib.ExitStack())
        cxL2 = Ctx(nc, esL2, p)
        psq_r = cxL2.ps("ps_sq", [2, 3, 512], F32)
        pskv_r = cxL2.ps("ps_skv", [2, 3, 512], F32)
        pstf = cxL2.ps("pstfL", [128, 512], F32)
        pstb = cxL2.ps("pstbL", [128, D], BF16)
        s_lx = [p.sem("lxA0"), p.sem("lxA1")]
        s_sy = [p.sem("syA0"), p.sem("syA1")]
        s_t = p.sem("tA")
        s_rm = p.sem("rmA")
        p.on("vector", lambda v: v.memset(epsb[:, :], 1e-5))
        xg = [gb_tok, gb_tok]
        yg = [None, None]
        tys_last = [None, None]
        xng = [None, None]
        ytr_g = None
        pstf_g = None
        pstb_g = None
        rmm_last = None
        for t in range(NT):
            r = t % 2
            tx = p.on("sync", lambda e, t=t, r=r: e.dma_start(
                out=xb[:, r, :], in_=x_d.rearrange("(t p) f -> p t f", p=128)[:, t, :]),
                inc=s_lx[r], inc_by=16, kind="dma", wait=[xg[r]])
            xt = xb[:, r, :]
            S_ = lambda i: stt_[:, t, i:i + 1]
            tv1 = p.on("vector", lambda v, xt=xt, s0=S_(0): v.reduce_sum(s0, xt, axis=AX.X),
                       wait=[tx] + ([cx.consts["tok"], ttril, gb_tok] if t == 0 else []))
            ts1 = p.on("scalar", lambda s, xt=xt, s2=S_(2): s.activation(sqs[:, :], xt, AF.Square, accum_out=s2),
                       wait=[tx])
            p.on("vector", lambda v, s0=S_(0), s1=S_(1): v.tensor_scalar(s1, s0, 1.0 / D, None, op0=ALU.mult))
            p.on("vector", lambda v, s3=S_(3), s1=S_(1): v.scalar_tensor_tensor(
                s3, s1, -1.0, s1, op0=ALU.mult, op1=ALU.mult), wait=[ts1])
            tv3 = p.on("vector", lambda v, s2=S_(2), s3=S_(3): v.scalar_tensor_tensor(
                s3, s2, 1.0 / D, s3, op0=ALU.mult, op1=ALU.add))
            ts2 = p.on("scalar", lambda s, s3=S_(3), s4=S_(4): s.activation(s4, s3, AF.Sqrt, bias=epsb[:, 0:1]),
                       wait=[tv3])
            tv4 = p.on("vector", lambda v, s4=S_(4), s5=S_(5): v.reciprocal(s5, s4), wait=[ts2])
            tv5 = p.on("vector", lambda v, s1=S_(1), s5=S_(5), s6=S_(6): v.scalar_tensor_tensor(
                s6, s1, -1.0, s5, op0=ALU.mult, op1=ALU.mult))
            tyf = p.on("scalar", lambda s, xt=xt, r=r, s5=S_(5), s6=S_(6): s.activation(
                yf[:, r, :], xt, AF.Identity, bias=s6, scale=s5),
                wait=[tv4, tv5] + (yg[r] or []))
            xg[r] = tyf
            tys = p.on("sync", lambda e, t=t, r=r: e.dma_start(
                out=y_d.rearrange("(t p) f -> p t f", p=128)[:, t, :], in_=yf[:, r, :]),
                inc=s_sy[r], inc_by=16, kind="dma", wait=[tyf])
            tys_last[r] = tys
            rmm = None
            ytp_last = None
            for grp in range(2):
                tts = []
                for c4 in range(4):
                    c = grp * 4 + c4
                    tts.append(p.on("tensor", lambda te, r=r, c=c, c4=c4: te.transpose(
                        pstf[:, c4 * 128:(c4 + 1) * 128], yf[:, r, c * 128:(c + 1) * 128],
                        cx.consts["identf"][:, :]), inc=s_t,
                        wait=[tyf, pstf_g] if c4 == 0 else None))
                ytp_last = tts[-1]
                pstf_g = p.on("vector", lambda v, grp=grp: v.tensor_copy(
                    ytr[:, grp * 4:(grp + 1) * 4, :],
                    pstf[:, :].rearrange("p (c t) -> p c t", t=128)), wait=[tts[-1]])
            for c in range(NC):
                if t >= 4:
                    seg, col = (t - 4) // 4, ((t - 4) % 4) * 128
                    rmm = p.on("tensor", lambda te, c=c, seg=seg, col=col: te.matmul(
                        psq_r[:, seg, col:col + 128], rtqT[:, c, :], ytr[:, c, :],
                        start=(c == 0), stop=(c == NC - 1)), inc=s_rm,
                        wait=[pstf_g, tg3] if c == 0 else None)
                if t <= 11:
                    seg, col = t // 4, (t % 4) * 128
                    rmm = p.on("tensor", lambda te, c=c, seg=seg, col=col: te.matmul(
                        pskv_r[:, seg, col:col + 128], rtkvT[:, c, :], ytr[:, c, :],
                        start=(c == 0), stop=(c == NC - 1)), inc=s_rm,
                        wait=[pstf_g, tg4] if c == 0 else None)
            if t == NT - 1:
                rmm_last = rmm
            tv1b = p.on("vector", lambda v, r=r, s0=S_(7): v.reduce_sum(s0, yf[:, r, :], axis=AX.X))
            ts1b = p.on("scalar", lambda s, r=r, s2=S_(8): s.activation(sqs[:, :], yf[:, r, :], AF.Square, accum_out=s2),
                        wait=[tyf])
            p.on("vector", lambda v, s0=S_(7), s1=S_(9): v.tensor_scalar(s1, s0, 1.0 / D, None, op0=ALU.mult))
            p.on("vector", lambda v, s3=S_(10), s1=S_(9): v.scalar_tensor_tensor(
                s3, s1, -1.0, s1, op0=ALU.mult, op1=ALU.mult), wait=[ts1b])
            tv3b = p.on("vector", lambda v, s2=S_(8), s3=S_(10): v.scalar_tensor_tensor(
                s3, s2, 1.0 / D, s3, op0=ALU.mult, op1=ALU.add))
            ts2b = p.on("scalar", lambda s, s3=S_(10), s4=S_(11): s.activation(s4, s3, AF.Sqrt, bias=epsb[:, 0:1]),
                        wait=[tv3b])
            tv4b = p.on("vector", lambda v, s4=S_(11), s5=S_(4): v.reciprocal(s5, s4), wait=[ts2b])
            tv5b = p.on("vector", lambda v, s1=S_(9), s5=S_(4), s6=S_(5): v.scalar_tensor_tensor(
                s6, s1, -1.0, s5, op0=ALU.mult, op1=ALU.mult))
            txn = p.on("scalar", lambda s, r=r, s5=S_(4), s6=S_(5): s.activation(
                xnf[:, :], yf[:, r, :], AF.Identity, bias=s6, scale=s5), wait=[tv4b, tv5b, xng[0] if False else None])
            yg[r] = [txn, tys, ytp_last]
            tgn = p.on("vector", lambda v: v.tensor_tensor(xnf[:, :], xnf[:, :], gtile[:, :], op=ALU.mult),
                       wait=[txn])
            tbn = p.on("vector", lambda v, r=r: v.tensor_tensor(xnb[:, r, :], xnf[:, :], btile[:, :], op=ALU.add),
                       wait=[xng[r]])
            tts = []
            for c in range(NC):
                tts.append(p.on("tensor", lambda te, r=r, c=c: te.transpose(
                    pstb[:, c * 128:(c + 1) * 128], xnb[:, r, c * 128:(c + 1) * 128],
                    cx.consts["identb"][:, :]), inc=s_t,
                    wait=[tbn, pstb_g] if c == 0 else None))
            xng[r] = tts[-1]
            pstb_g = p.on("vector", lambda v, t=t: v.tensor_copy(
                xnT[:, :, t * 128:(t + 1) * 128],
                pstb[:, :].rearrange("p (c t) -> p c t", t=128)), wait=[tts[-1]])
        xn_tok = pstb_g
        tsq = p.on("vector", lambda v: v.tensor_copy(
            sQ[:, :].rearrange("p (a b) -> p a b", b=512), psq_r[:, :, :]), wait=[rmm_last])
        tskv = p.on("vector", lambda v: v.tensor_copy(
            sKV[:, :].rearrange("p (a b) -> p a b", b=512), pskv_r[:, :, :]))
        esL2.close()
        p.wait("vector", tys_last)
        p.barrier()
        esL.close()
        ones_tok = p.on("vector", lambda v: v.memset(vp[:, :, :, :], 1.0))
        esP = es.enter_context(contextlib.ExitStack())
        cxP = Ctx(nc, esP, p); cxP.consts = cx.consts; cxP.shared = {}
        rres = {}
        rgen = emit_router(cx, sQ, sKV, "rt", [tsq, tskv], ikv_sb, t_il,
                           (mq_sb, mkv_sb2, rscr, rred), eng="vector", res=rres)

        def tick(n=3):
            for _ in range(n):
                if next(rgen, StopIteration) is StopIteration:
                    break
        t_q = emit_qk_rope_T(cxP, xnT, wql_d, None, qT, "pq", xn_tok, cos_d, ss_d, "local", tick=tick)
        t_k = emit_qk_rope_T(cxP, xnT, wkl_d, None, kT, "pk", xn_tok, cos_d, ss_d, "local", tick=tick)
        t_v = emit_v_bias(cxP, xnT, wvl_d, None, vp, "pv", xn_tok, ones_tok, tick=tick)
        for _ in rgen:
            pass
        mq, mkv, rt_toks = rres["mq"], rres["mkv"], rres["toks"]
        with nc.psum_tensor(uname("pscnt"), [2, 3, 512], F32) as pscnt, \
             nc.psum_tensor(uname("pstr"), [128, 24], F32) as pstr:
            t_o2 = p.on("vector", lambda v: v.memset(ones2[:, :], 1.0))
            cm = []
            for sl3 in range(3):
                cm.append(p.on("tensor", lambda te, sl3=sl3: te.matmul(
                    pscnt[0:1, sl3, :], ones2[:, 0:1], mq[:, sl3 * 512:(sl3 + 1) * 512],
                    start=True, stop=True), inc=p.sem("cntm"), wait=[t_o2, rt_toks[1]]))
            tcv = p.on("vector", lambda v: v.tensor_copy(
                rrow[:, :].rearrange("p (a b) -> p a b", b=512), pscnt[0:1, :, :]),
                wait=[cm[-1], rt_toks[1]])
            p.on("vector", lambda v: v.tensor_scalar(rrow[:, :], rrow[:, :], 1.0, None, op0=ALU.add))
            trr = p.on("vector", lambda v: v.reciprocal(rrow[:, :], rrow[:, :]))
            trh = p.on("vector", lambda v: v.tensor_scalar(rrow[:, :], rrow[:, :], 0.5, None, op0=ALU.mult))
            rb = []
            for sl3 in range(3):
                mmr = p.on("tensor", lambda te, sl3=sl3: te.matmul(
                    pscnt[:, sl3, :], ones2[0:1, :], rrow[:, sl3 * 512:(sl3 + 1) * 512],
                    start=True, stop=True), inc=p.sem("cntm"), wait=[trh, tcv])
                rb.append(p.on("vector", lambda v, sl3=sl3: v.tensor_copy(
                    r2[:, sl3 * 512:(sl3 + 1) * 512], pscnt[:, sl3, :]), wait=[mmr]))
            tF = p.on("vector", lambda v: v.tensor_tensor(Ft[:, :], mq[:, :], r2[:, :], op=ALU.mult),
                      wait=[rb[-1]])
            tmem = p.on("vector", lambda v: v.memset(rfacT[:, :], 0.5))
            tr_toks = []
            for i in range(12):
                tr_toks.append(p.on("tensor", lambda te, i=i: te.transpose(
                    pstr[:, 0:1], rrow[0:1, i * 128:(i + 1) * 128],
                    cx.consts["identf"][0:1, 0:1]), inc=p.sem("trm"),
                    wait=[trh] if i == 0 else [tr_toks[-1]]))
                tr_toks.append(p.on("vector", lambda v, i=i: v.tensor_copy(
                    rfacT[:, 4 + i:5 + i], pstr[:, 0:1]), wait=[tr_toks[-1], tmem]))
            for (src_t, dst, wtok, tagn) in ((Ft, FtT, tF, "F"), (mkv, mkvT_sb, rt_toks[1], "mkv")):
                for i in range(12):
                    tt = p.on("tensor", lambda te, i=i, src_t=src_t: te.transpose(
                        pstr[:, 0:2], src_t[:, i * 128:(i + 1) * 128], cx.consts["identf"][0:2, 0:2]),
                        inc=p.sem("trm"), wait=[wtok, tr_toks[-1]])
                    tr_toks.append(p.on("vector", lambda v, i=i, dst=dst: v.tensor_copy(
                        dst[:, :, i], pstr[:, 0:2]), wait=[tt]))
            rfac_tok = tr_toks[-1]
            s_so = p.sem("soA")
            so_toks = [
                p.on("sync", lambda e: e.dma_start(out=fgt0_d, in_=FtT[:, 0, :]), inc=s_so, inc_by=16, kind="dma", wait=[rfac_tok]),
                p.on("sync", lambda e: e.dma_start(out=fgt1_d, in_=FtT[:, 1, :]), inc=s_so, inc_by=16, kind="dma"),
                p.on("sync", lambda e: e.dma_start(out=mkvt0_d, in_=mkvT_sb[:, 0, :]), inc=s_so, inc_by=16, kind="dma"),
                p.on("sync", lambda e: e.dma_start(out=mkvt1_d, in_=mkvT_sb[:, 1, :]), inc=s_so, inc_by=16, kind="dma"),
            ]
        p.wait("sync", [so_toks[-1]])
        p.barrier()
        esP.close()
        esR.close()
        esB = es.enter_context(contextlib.ExitStack())
        cxB = Ctx(nc, esB, p); cxB.consts = cx.consts
        ctx_t = bigbuf[:, :].rearrange("p (a b) -> p a b", b=D)
        esAT = es.enter_context(contextlib.ExitStack())
        cxAT = Ctx(nc, esB, p, ps_es=esAT); cxAT.consts = cx.consts
        ctx_tok = emit_attn(cxAT, qT, kT, vp, ctx_t, "at", 1.0 / 8.0,
                            kts_of=lambda qt: range(max(0, 4 * qt - 4), 4 * qt + 4),
                            toks=[t_q, t_k, t_v], extra_mask=trilb)
        esAT.close()
        st_last = emit_outproj_residual(cxB, ctx_t, wlo_d, x_d, xout_d, "op", ctx_tok,
                                        rfac=rfacT, rfac_tok=None)
        p.wait("sync", st_last)
        p.wait("gpsimd", st_last)
    return st_last


def build_prog_l0ffn_a1():
    nc = bass.Bass("TRN2", target_bir_lowering=False, debug=False)
    es = contextlib.ExitStack()
    with es:
        p = Prog(nc, es)
        cx = Ctx(nc, es, p)
        x_d = nc.dram_tensor("x", [T, D], F32, kind="ExternalInput").ap()
        lng_d = nc.dram_tensor("lng", [1, D], F32, kind="ExternalInput").ap()
        lnb_d = nc.dram_tensor("lnb", [1, D], F32, kind="ExternalInput").ap()
        wql_d = nc.dram_tensor("wql", [NC, 128, D], F32, kind="ExternalInput").ap()
        wkl_d = nc.dram_tensor("wkl", [NC, 128, D], F32, kind="ExternalInput").ap()
        wvl_d = nc.dram_tensor("wvl", [NC, 128, D], F32, kind="ExternalInput").ap()
        wlo_d = nc.dram_tensor("wlo", [NC, 128, D], F32, kind="ExternalInput").ap()
        rtq_d = nc.dram_tensor("rtq", [NC, 128, 2], F32, kind="ExternalInput").ap()
        rtkv_d = nc.dram_tensor("rtkv", [NC, 128, 2], F32, kind="ExternalInput").ap()
        cos_d = nc.dram_tensor("cosd", [NT, 128, 64], F32, kind="ExternalInput").ap()
        ss_d = nc.dram_tensor("ssd", [NT, 128, 64], F32, kind="ExternalInput").ap()
        tril_d = nc.dram_tensor("trild", [128, 128], F32, kind="ExternalInput").ap()
        iota_q_d = nc.dram_tensor("iota_q", [2, 1536], F32, kind="ExternalInput").ap()
        iota_kv_d = nc.dram_tensor("iota_kv", [2, 1536], F32, kind="ExternalInput").ap()
        xout_d = nc.dram_tensor("xout", [T, D], F32, kind="ExternalOutput").ap()
        y_d = nc.dram_tensor("yout", [T, D], F32, kind="ExternalOutput").ap()
        fgt0_d = nc.dram_tensor("fgt0", [128, 12], F32, kind="ExternalOutput").ap()
        fgt1_d = nc.dram_tensor("fgt1", [128, 12], F32, kind="ExternalOutput").ap()
        mkvt0_d = nc.dram_tensor("mkvt0", [128, 12], F32, kind="ExternalOutput").ap()
        mkvt1_d = nc.dram_tensor("mkvt1", [128, 12], F32, kind="ExternalOutput").ap()
        consts = load_consts(cx, p.sem("ld"))
        phase_l0ffn_a1(nc, p, consts, x_d, lng_d, lnb_d, wql_d, wkl_d, wvl_d, wlo_d,
                       rtq_d, rtkv_d, cos_d, ss_d, tril_d, iota_q_d, iota_kv_d,
                       xout_d, y_d, fgt0_d, fgt1_d, mkvt0_d, mkvt1_d)
        block = es.enter_context(nc.Block())
        p.finalize(block)
    return nc



def emit_attn_grouped(cx, qT, kT, vp, ctxg, tag, Fg, mkvc, toks):
    p = cx.p
    identf = cx.consts["identf"]
    NS = 4
    psS = cx.ps(tag + "_psS", [128, NS, 512], F32)
    psO = cx.ps(tag + "_psO", [65, 2, 512], F32)
    psT = cx.ps(tag + "_psT", [128, 2, 4, 128], F32)
    pT = cx.sb(tag + "_pT", [128, NS, 512], BF16)
    o65 = cx.sb(tag + "_o65", [65, 2, 512], F32)
    ctxp = cx.sb(tag + "_ctxp", [128, 4, NH, 65], F32)
    rec = cx.sb(tag + "_rec", [128, 16], F32)
    scl = cx.sb(tag + "_scl", [128, 1], F32)
    bia = cx.sb(tag + "_bia", [128, 1], F32)
    mb = cx.sb(tag + "_mb", [128, 13], F32)
    s_mm = p.sem(tag + "mm")
    p.on("vector", lambda v: v.memset(scl[:, :], 0.125))
    c2 = p.on("vector", lambda v: v.memset(bia[:, :], -EXP_B))
    c3 = p.on("vector", lambda v: v.tensor_scalar(
        mb[:, 0:12], mkvc, 300.0, -(300.0 + EXP_B), op0=ALU.mult, op1=ALU.add))
    c4 = p.on("vector", lambda v: v.memset(mb[:, 12:13], -EXP_B))
    p.wait("tensor", toks)
    p.wait("scalar", toks + [c2, c4])
    st = {
        "prev_exp": [None] * NS,
        "prev_pv": [None] * NS,
        "prev_tev_b": [None, None],
        "prev_tts_b": [None, None],
        "prev_scatter_b": [None, None],
        "ctx_done": [],
    }
    items = []
    tails = {}
    for w in (1, 2, 3):
        kts = list(range(4 * w)) + [12] if w < 3 else list(range(13))
        q0 = (w - 1) * 512
        for h in range(NH):
            hc, roff = h // 2, (h % 2) * 64
            ob = h % 2
            for j, kt in enumerate(kts):
                items.append(dict(hc=hc, roff=roff, ob=ob, kt=kt, h=h, q0=q0, w=w,
                                  first=(j == 0), last=(j == len(kts) - 1),
                                  r=len(items) % NS))
            tails[len(items) - 1] = dict(h=h, ob=ob, w=w, q_end=(h == NH - 1))

    def emit_S(i):
        it = items[i]
        r = it["r"]
        tmm = p.on("tensor", lambda te, it=it, r=r: te.matmul(
            psS[:, r, :],
            kT[it["roff"]:it["roff"] + 64, it["hc"], it["kt"] * 128:(it["kt"] + 1) * 128],
            qT[it["roff"]:it["roff"] + 64, it["hc"], it["q0"]:it["q0"] + 512],
            start=True, stop=True), inc=s_mm, wait=[st["prev_exp"][r]])
        use_mb = (it["w"] == 3 and it["kt"] < 12)
        tx = p.on("scalar", lambda s, it=it, r=r, use_mb=use_mb: s.activation(
            pT[:, r, :], psS[:, r, :], AF.Exp,
            bias=(mb[:, it["kt"]:it["kt"] + 1] if use_mb else bia[:, 0:1]),
            scale=scl[:, 0:1]),
            wait=[tmm, st["prev_pv"][r]])
        st["prev_exp"][r] = tx
        it["ready"] = tx

    def emit_PV(i):
        it = items[i]
        r = it["r"]
        ob = it["ob"]
        extra_w = [st["prev_tev_b"][ob]] if it["first"] else []
        tpv = p.on("tensor", lambda te, it=it, r=r, ob=ob: te.matmul(
            psO[:, ob, :], vp[:, it["kt"], it["h"], :], pT[:, r, :],
            start=it["first"], stop=it["last"]), inc=s_mm,
            wait=[it["ready"]] + extra_w)
        st["prev_pv"][r] = tpv
        if it["last"]:
            tev = p.on("vector", lambda v, ob=ob: v.tensor_copy(o65[:, ob, :], psO[:, ob, :]),
                       wait=[tpv, st["prev_tts_b"][ob]])
            st["prev_tev_b"][ob] = tev
            tails[i]["tev"] = tev

    def emit_tail(i):
        td = tails[i]
        h, ob, w = td["h"], td["ob"], td["w"]
        tev = td["tev"]
        ctxp_guard = st["ctx_done"][-1] if st["ctx_done"] else None
        tts = []
        for s4 in range(4):
            tts.append(p.on("tensor", lambda te, s4=s4, ob=ob: te.transpose(
                psT[:, ob, s4, 0:65], o65[:, ob, s4 * 128:(s4 + 1) * 128], identf[0:65, 0:65]),
                inc=s_mm, wait=[tev, st["prev_scatter_b"][ob]] if s4 == 0 else None))
        st["prev_tts_b"][ob] = tts[-1]
        st["prev_scatter_b"][ob] = p.on("vector", lambda v, h=h, ob=ob: v.tensor_copy(
            ctxp[:, :, h, :], psT[:, ob, :, 0:65]),
            wait=[tts[-1], ctxp_guard])
        if td["q_end"]:
            ctok = None
            for s4 in range(4):
                ti = (w - 1) * 4 + s4
                tr = p.on("vector", lambda v, s4=s4: v.reciprocal(
                    rec[:, :], ctxp[:, s4, :, 64]),
                    wait=[st["prev_scatter_b"][0], st["prev_scatter_b"][1]])
                tr2 = p.on("vector", lambda v, ti=ti: v.tensor_scalar(
                    rec[:, :], rec[:, :], Fg[:, ti:ti + 1], None, op0=ALU.mult))
                ctok = p.on("vector", lambda v, s4=s4, ti=ti: v.tensor_tensor(
                    ctxg[:, ti, :].rearrange("p (h d) -> p h d", d=64),
                    ctxp[:, s4, :, 0:64],
                    rec[:, :].unsqueeze(2).broadcast_to((128, 16, 64)), op=ALU.mult),
                    wait=[tr2])
            st["ctx_done"].append(ctok)

    N = len(items)
    LA = 2
    TD = 2
    for i in range(N):
        emit_S(i)
        if i >= LA:
            emit_PV(i - LA)
        if i - LA - TD in tails and i >= LA + TD:
            emit_tail(i - LA - TD)
    for i in range(N - LA, N):
        emit_PV(i)
    for e in sorted(tails):
        if e > N - 1 - LA - TD:
            emit_tail(e)
    return st["ctx_done"][-1]


def phase_l0ffn_a2(nc, p, consts, xp_d, y_d, fgt_d, mkvt_d, wq_d, wkv_d, wo_d,
                   qnT_d, cnT_d, nullk_d, nullv_d, xout_d):
    es = contextlib.ExitStack()
    with es:
        cx = Ctx(nc, es, p)
        cx.consts = consts
        zT = cx.sb("zT", [128, NC, T], BF16)
        Fg_sb = cx.sb("Fg_sb", [128, 2, 12], F32)
        mkv_sb = cx.sb("mkv_sb", [128, 2, 12], F32)
        qn_sb = cx.sb("qn_sb", [128, 2, NC], F32)
        cn_sb = cx.sb("cn_sb", [128, 2, NC], F32)
        s_lf = p.sem("lf")
        tmf = []
        for g in range(2):
            tmf.append(p.on("sync", lambda e, g=g: e.dma_start(out=Fg_sb[:, g, :], in_=fgt_d[g]), inc=s_lf, inc_by=16, kind="dma"))
            tmf.append(p.on("sync", lambda e, g=g: e.dma_start(out=mkv_sb[:, g, :], in_=mkvt_d[g]), inc=s_lf, inc_by=16, kind="dma"))
            tmf.append(p.on("sync", lambda e, g=g: e.dma_start(out=qn_sb[:, g, :], in_=qnT_d[g]), inc=s_lf, inc_by=16, kind="dma"))
            tmf.append(p.on("sync", lambda e, g=g: e.dma_start(out=cn_sb[:, g, :], in_=cnT_d[g]), inc=s_lf, inc_by=16, kind="dma"))
        tload_all = tmf[-1]
        esZ = es.enter_context(contextlib.ExitStack())
        cxZ = Ctx(nc, esZ, p); cxZ.consts = cx.consts
        yb = cxZ.sb("ybZ", [128, 2, D], F32)
        zb = cxZ.sb("zbZ", [128, 2, D], BF16)
        sqs = cxZ.sb("sqZ", [128, D], F32)
        stz = cxZ.sb("stZ", [128, NT, 4], F32)
        pstz = cxZ.ps("pstZ", [128, D], BF16)
        s_ly = [p.sem("lyZ0"), p.sem("lyZ1")]
        s_tz = p.sem("tZ")
        yg = [None, None]
        zg = [None, None]
        ev_prev = None
        for t in range(NT):
            r = t % 2
            ty = p.on("sync", lambda e, t=t, r=r: e.dma_start(
                out=yb[:, r, :], in_=y_d.rearrange("(t p) f -> p t f", p=128)[:, t, :]),
                inc=s_ly[r], inc_by=16, kind="dma", wait=[yg[r]])
            S_ = lambda i: stz[:, t, i:i + 1]
            ts1 = p.on("scalar", lambda s, r=r, s0=S_(0): s.activation(
                sqs[:, :], yb[:, r, :], AF.Square, accum_out=s0),
                wait=[ty] + ([cx.consts["tok"], tload_all] if t == 0 else []))
            ts2 = p.on("scalar", lambda s, s0=S_(0), s1=S_(1): s.activation(s1, s0, AF.Sqrt), wait=[ts1])
            p.on("vector", lambda v, s1=S_(1), s2=S_(2): v.tensor_scalar(s2, s1, 1e-12, None, op0=ALU.max),
                 wait=[ts2])
            tv2 = p.on("vector", lambda v, s2=S_(2), s3=S_(3): v.reciprocal(s3, s2))
            tv3 = p.on("vector", lambda v, s3=S_(3): v.tensor_scalar(s3, s3, 32.0, None, op0=ALU.mult))
            tz = p.on("scalar", lambda s, r=r, s3=S_(3): s.activation(
                zb[:, r, :], yb[:, r, :], AF.Identity, scale=s3), wait=[tv3, zg[r]])
            yg[r] = tz
            tts = []
            for c in range(NC):
                tts.append(p.on("tensor", lambda te, r=r, c=c: te.transpose(
                    pstz[:, c * 128:(c + 1) * 128], zb[:, r, c * 128:(c + 1) * 128],
                    cx.consts["identb"][:, :]), inc=s_tz,
                    wait=[tz, ev_prev] if c == 0 else None))
            zg[r] = tts[-1]
            ev_prev = p.on("vector", lambda v, t=t: v.tensor_copy(
                zT[:, :, t * 128:(t + 1) * 128],
                pstz[:, :].rearrange("p (c t) -> p c t", t=128)), wait=[tts[-1]])
        p.barrier()
        esZ.close()
        esG = es.enter_context(contextlib.ExitStack())
        cxG = Ctx(nc, esG, p); cxG.consts = cx.consts
        qTg = cxG.sb("qTg", [128, NC, 3 * 512], BF16)
        kTg = cxG.sb("kTg", [128, NC, 13 * 128], BF16)
        vpg = cxG.sb("vpg", [128, 13, NH, 65], BF16)
        ctxg = cxG.sb("ctxg", [128, 12, D], BF16)
        st_final = []
        for g in range(2):
            esW = es.enter_context(contextlib.ExitStack())
            cxW = Ctx(nc, esW, p); cxW.consts = cx.consts
            wstage = cxW.sb(f"wst{g}", [128, 2, 2 * D], F32)
            wqb = cxW.sb(f"wqb{g}", [128, NC, D], BF16)
            wkvb = cxW.sb(f"wkvb{g}", [128, NC, 2 * D], BF16)
            psp = cxW.ps(f"psp{g}", [128, 512], F32)
            psv = cxW.ps(f"psv{g}", [128, 2, 512], F32)
            s_lw = [p.sem(f"lw{g}0"), p.sem(f"lw{g}1")]
            s_mp = p.sem(f"mp{g}")
            cast_g = [None, None]
            wq_toks = []
            for c in range(NC):
                r = c % 2
                tl = p.on("sync", lambda e, g=g, c=c, r=r: e.dma_start(
                    out=wstage[:, r, 0:D], in_=wq_d[g, c]),
                    inc=s_lw[r], inc_by=16, kind="dma", wait=[cast_g[r]])
                tc = p.on("vector", lambda v, g=g, c=c, r=r: v.tensor_scalar(
                    wqb[:, c, :], wstage[:, r, 0:D], qn_sb[:, g, c:c + 1], None, op0=ALU.mult),
                    wait=[tl])
                cast_g[r] = tc
                wq_toks.append(tc)
            for c in range(NC):
                r = c % 2
                tl = p.on("sync", lambda e, g=g, c=c, r=r: e.dma_start(
                    out=wstage[:, r, :], in_=wkv_d[g, c]),
                    inc=s_lw[r], inc_by=16, kind="dma", wait=[cast_g[r]])
                tc = p.on("vector", lambda v, g=g, c=c, r=r: v.tensor_scalar(
                    wkvb[:, c, :], wstage[:, r, :], cn_sb[:, g, c:c + 1], None, op0=ALU.mult),
                    wait=[tl])
                cast_g[r] = tc
                wq_toks.append(tc)
            wtok = wq_toks[-1]
            ev_tok = None
            for m in range(NC):
                for n in range(3):
                    mm = None
                    for c in range(NC):
                        w = [wtok, ev_tok] if c == 0 else None
                        mm = p.on("tensor", lambda te, m=m, n=n, c=c: te.matmul(
                            psp[:, :], wqb[:, c, m * 128:(m + 1) * 128],
                            zT[:, c, 512 + n * 512:512 + (n + 1) * 512],
                            start=(c == 0), stop=(c == NC - 1)), inc=s_mp, wait=w)
                    ev_tok = p.on("vector", lambda v, m=m, n=n: v.tensor_copy(
                        qTg[:, m, n * 512:(n + 1) * 512], psp[:, :]), wait=[mm])
            tz1 = p.on("vector", lambda v: v.memset(kTg[:, :, 1537:], 0.0))
            s_nk = p.sem(f"nk{g}")
            tnk = p.on("gpsimd", lambda e, g=g: e.dma_start(
                out=kTg[:, :, 1536:1537], in_=nullk_d[g].unsqueeze(2), allow_slow_non_contiguous=True),
                inc=s_nk, inc_by=16, kind="dma")
            for m in range(NC):
                for n in range(3):
                    mm = None
                    for c in range(NC):
                        w = [wtok, ev_tok] if c == 0 else None
                        mm = p.on("tensor", lambda te, m=m, n=n, c=c: te.matmul(
                            psp[:, :], wkvb[:, c, m * 128:(m + 1) * 128],
                            zT[:, c, n * 512:(n + 1) * 512],
                            start=(c == 0), stop=(c == NC - 1)), inc=s_mp, wait=w)
                    ev_tok = p.on("vector", lambda v, m=m, n=n: v.tensor_copy(
                        kTg[:, m, n * 512:(n + 1) * 512], psp[:, :]), wait=[mm])
            tm1 = p.on("vector", lambda v: v.memset(vpg[:, 0:12, :, :], 1.0))
            tm2 = p.on("vector", lambda v: v.memset(vpg[:, 12, :, :], 0.0))
            tm3 = p.on("vector", lambda v: v.memset(vpg[0:1, 12, :, 64:65], 1.0))
            s_nv = p.sem(f"nv{g}")
            tnv = p.on("gpsimd", lambda e, g=g: e.dma_start(
                out=vpg[0:1, 12, :, 0:64], in_=nullv_d[g].rearrange("p (h d) -> p h d", d=64)),
                inc=s_nv, inc_by=16, kind="dma", wait=[tm3])
            last_ev = {}
            for t in range(12):
                mm_toks = [None, None]
                for half in range(2):
                    for c in range(NC):
                        w = [wtok, tm1] if c == 0 else None
                        if c == 0 and t >= 1:
                            w = [wtok, tm1, last_ev.get((t - 1, half))]
                        mm_toks[half] = p.on("tensor", lambda te, t=t, c=c, half=half: te.matmul(
                            psv[:, half, :], zT[:, c, t * 128:(t + 1) * 128],
                            wkvb[:, c, D + half * 512:D + (half + 1) * 512],
                            start=(c == 0), stop=(c == NC - 1)), inc=s_mp, wait=w)
                for half in range(2):
                    last_ev[(t, half)] = p.on("vector", lambda v, t=t, half=half: v.tensor_copy(
                        vpg[:, t, half * 8:(half + 1) * 8, 0:64],
                        psv[:, half, :].rearrange("p (h d) -> p h d", d=64)),
                        wait=[mm_toks[half]])
            p.wait("vector", [tnk, tnv])
            p.barrier()
            esW.close()
            esA = es.enter_context(contextlib.ExitStack())
            cxA2 = Ctx(nc, esA, p); cxA2.consts = cx.consts
            ctx_tok = emit_attn_grouped(cxA2, qTg, kTg, vpg, ctxg, f"ga{g}",
                                        Fg_sb[:, g, :], mkv_sb[:, g, :], [])
            p.barrier()
            esA.close()
            esO = es.enter_context(contextlib.ExitStack())
            cxO = Ctx(nc, esO, p); cxO.consts = cx.consts
            xsrc = xp_d if g == 0 else xout_d
            st_toks = emit_outproj_grouped(cxO, ctxg, wo_d[g], xsrc, xp_d, xout_d,
                                           f"go{g}", ctx_tok, first=(g == 0))
            st_final.append(st_toks)
            p.wait("vector", st_toks)
            p.barrier()
            esO.close()
        p.wait("sync", st_final[-1])
        p.wait("gpsimd", st_final[-1])
        st_ret = st_final[-1]
    return st_ret


def build_prog_l0ffn_a2():
    nc = bass.Bass("TRN2", target_bir_lowering=False, debug=False)
    es = contextlib.ExitStack()
    with es:
        p = Prog(nc, es)
        cx = Ctx(nc, es, p)
        xp_d = nc.dram_tensor("xp", [T, D], F32, kind="ExternalInput").ap()
        y_d = nc.dram_tensor("y", [T, D], F32, kind="ExternalInput").ap()
        fgt_d = [nc.dram_tensor(f"fgt{g}", [128, 12], F32, kind="ExternalInput").ap() for g in range(2)]
        mkvt_d = [nc.dram_tensor(f"mkvt{g}", [128, 12], F32, kind="ExternalInput").ap() for g in range(2)]
        wq_d = nc.dram_tensor("wqg", [2, NC, 128, D], F32, kind="ExternalInput").ap()
        wkv_d = nc.dram_tensor("wkvg", [2, NC, 128, 2 * D], F32, kind="ExternalInput").ap()
        wo_d = nc.dram_tensor("wog", [2, NC, 128, D], F32, kind="ExternalInput").ap()
        qnT_d = nc.dram_tensor("qnT", [2, 128, NC], F32, kind="ExternalInput").ap()
        cnT_d = nc.dram_tensor("cnT", [2, 128, NC], F32, kind="ExternalInput").ap()
        nullk_d = nc.dram_tensor("nullk", [2, NC, 128], F32, kind="ExternalInput").ap()
        nullv_d = nc.dram_tensor("nullv", [2, 1, D], F32, kind="ExternalInput").ap()
        xout_d = nc.dram_tensor("xout", [T, D], F32, kind="ExternalOutput").ap()
        consts = load_consts(cx, p.sem("ld"))
        phase_l0ffn_a2(nc, p, consts, xp_d, y_d, fgt_d, mkvt_d, wq_d, wkv_d, wo_d,
                       qnT_d, cnT_d, nullk_d, nullv_d, xout_d)
        block = es.enter_context(nc.Block())
        p.finalize(block)
    return nc


def emit_outproj_grouped(cx, ctxg, wo_dram, xsrc_d, xp_d, xout_d, tag, ctx_tok, first):
    p = cx.p
    wb = cx.sb(tag + "_wb", [128, NC, D], BF16)
    ctxT = cx.sb(tag + "_ctxT", [128, NC, 128], BF16)
    xr = cx.sb(tag + "_xr", [128, 2, D], F32)
    xo = cx.sb(tag + "_xo", [128, 2, D], F32)
    psT = cx.ps(tag + "_psT", [128, D], BF16)
    pso = cx.ps(tag + "_pso", [128, 2, 512], F32)
    s_l = [p.sem(tag + "l0"), p.sem(tag + "l1")]
    s_m = p.sem(tag + "m")
    s_st = [p.sem(tag + "st0"), p.sem(tag + "st1")]
    tc = p.on("gpsimd", lambda e: e.dma_start(out=wb[:, :, :], in_=wo_dram.transpose([1, 0, 2])),
              inc=p.sem(tag + "gl"), inc_by=16, kind="dma")
    cp_st = [None, None]
    if first:
        for t in range(4):
            r = t % 2
            tld = p.on("sync", lambda e, t=t, r=r: e.dma_start(
                out=xr[:, r, :], in_=xp_d.rearrange("(t p) f -> p t f", p=128)[:, t, :]),
                inc=s_l[r], inc_by=16, kind="dma", wait=[cp_st[r]])
            cp_st[r] = p.on("sync", lambda e, t=t, r=r: e.dma_start(
                out=xout_d.rearrange("(t p) f -> p t f", p=128)[:, t, :], in_=xr[:, r, :]),
                inc=s_st[r], inc_by=16, kind="dma", wait=[tld])
    fma_hist = {}
    st_hist = {}
    evT_prev = None
    mm_prev = [None, None]
    last_st = None
    for i in range(12):
        t = i + 4
        r = t % 2
        txr = p.on("sync", lambda e, t=t, r=r: e.dma_start(
            out=xr[:, r, :], in_=xsrc_d.rearrange("(t p) f -> p t f", p=128)[:, t, :]),
            inc=s_l[r], inc_by=16, kind="dma",
            wait=[fma_hist.get((i - 2, 0)), fma_hist.get((i - 2, 1)), cp_st[(i + 4) % 2]] if i >= 2 else [cp_st[(i + 4) % 2]])
        tts = []
        for c in range(NC):
            tts.append(p.on("tensor", lambda te, i=i, c=c: te.transpose(
                psT[:, c * 128:(c + 1) * 128], ctxg[:, i, c * 128:(c + 1) * 128],
                cx.consts["identb"][:, :]), inc=s_m,
                wait=[ctx_tok, evT_prev] if c == 0 else None))
        evT = p.on("vector", lambda v: v.tensor_copy(
            ctxT[:, :, :], psT[:, :].rearrange("p (c t) -> p c t", t=128)),
            wait=[tts[-1], mm_prev[0], mm_prev[1]])
        evT_prev = evT
        for half in range(2):
            mm = None
            for c in range(NC):
                w = [evT, tc, fma_hist.get((i - 1, half))] if c == 0 else None
                mm = p.on("tensor", lambda te, c=c, half=half: te.matmul(
                    pso[:, half, :], ctxT[:, c, :], wb[:, c, half * 512:(half + 1) * 512],
                    start=(c == 0), stop=(c == NC - 1)), inc=s_m, wait=w)
            mm_prev[half] = mm
            fma_hist[(i, half)] = p.on("vector", lambda v, r=r, half=half: v.tensor_tensor(
                xo[:, r, half * 512:(half + 1) * 512], xr[:, r, half * 512:(half + 1) * 512],
                pso[:, half, :], op=ALU.add),
                wait=[mm, txr, st_hist.get((i - 2,))])
        st_hist[(i,)] = p.on("sync", lambda e, t=t, r=r: e.dma_start(
            out=xout_d.rearrange("(t p) f -> p t f", p=128)[:, t, :], in_=xo[:, r, :]),
            inc=s_st[r], inc_by=16, kind="dma", wait=[fma_hist[(i, 0)], fma_hist[(i, 1)]])
    return [st_hist[(10,)], st_hist[(11,)]]



def build_prog_fused():
    nc = bass.Bass("TRN2", target_bir_lowering=False, debug=False)
    es = contextlib.ExitStack()
    with es:
        p = Prog(nc, es)
        cx = Ctx(nc, es, p)
        EI = lambda n, s: nc.dram_tensor(n, list(s), F32, kind="ExternalInput").ap()
        IT = lambda n, s: nc.dram_tensor(n, list(s), F32, kind="Internal").ap()
        x_d = EI("x", [T, D])
        xout_d = nc.dram_tensor("xout", [T, D], F32, kind="ExternalOutput").ap()
        lng_d, lnb_d = EI("lng", [1, D]), EI("lnb", [1, D])
        wql_d, wkl_d, wvl_d = EI("wql", [NC, 128, D]), EI("wkl", [NC, 128, D]), EI("wvl", [NC, 128, D])
        wlo_d = EI("wlo", [NC, 128, D])
        rtq_d, rtkv_d = EI("rtq", [NC, 128, 2]), EI("rtkv", [NC, 128, 2])
        cosl_d, ssl_d = EI("cosl", [NT, 128, 64]), EI("ssl", [NT, 128, 64])
        tril_d = EI("trild", [128, 128])
        iota_q_d, iota_kv_d = EI("iota_q", [2, 1536]), EI("iota_kv", [2, 1536])
        wqg_d = EI("wqg", [2, NC, 128, D])
        wkvg_d = EI("wkvg", [2, NC, 128, 2 * D])
        wog_d = EI("wog", [2, NC, 128, D])
        qnT_d, cnT_d = EI("qnT", [2, 128, NC]), EI("cnT", [2, 128, NC])
        nullk_d, nullv_d = EI("nullk", [2, NC, 128]), EI("nullv", [2, 1, D])
        bwq_d, bwk_d, bwv_d, bwo_d = (EI(n, [NC, 128, D]) for n in ("bwq", "bwk", "bwv", "bwo"))
        bbq_d, bbk_d, bbv_d, bbd_d = (EI(n, [1, D]) for n in ("bbq", "bbk", "bbv", "bbd"))
        cosa_d, ssa_d = EI("cosa", [NT, 128, 64]), EI("ssa", [NT, 128, 32])
        wg_d, wu_d = EI("wg", [NIC, 128, D]), EI("wu", [NIC, 128, D])
        wd_d = EI("wd", [NC, 128, II])
        dwq_d, dwk_d, dwv_d, dwo_d = (EI(n, [NC, 128, D]) for n in ("dwq", "dwk", "dwv", "dwo"))
        y_i = IT("y_i", [T, D])
        fgt_i = [IT(f"fgt{g}_i", [128, 12]) for g in range(2)]
        mkvt_i = [IT(f"mkvt{g}_i", [128, 12]) for g in range(2)]
        xs_i = IT("xs_i", [T, D])
        xa_i = IT("xa_i", [T, D])
        xb_i = IT("xb_i", [T, D])
        xc_i = IT("xc_i", [T, D])
        xd_i = IT("xd_i", [T, D])
        xe_i = IT("xe_i", [T, D])
        wcache = {
            "g": nc.dram_tensor("wgq_i", [128, NC, II], BF16, kind="Internal").ap(),
            "u": nc.dram_tensor("wuq_i", [128, NC, II], BF16, kind="Internal").ap(),
            "d": nc.dram_tensor("wdq_i", [128, NIC, D], BF16, kind="Internal").ap(),
        }
        consts = load_consts(cx, p.sem("ld"))
        a1_w = (lng_d, lnb_d, wql_d, wkl_d, wvl_d, wlo_d, rtq_d, rtkv_d,
                cosl_d, ssl_d, tril_d, iota_q_d, iota_kv_d)
        a2_w = (wqg_d, wkvg_d, wog_d, qnT_d, cnT_d, nullk_d, nullv_d)
        st = phase_l0ffn_a1(nc, p, consts, x_d, *a1_w, xs_i, y_i,
                            fgt_i[0], fgt_i[1], mkvt_i[0], mkvt_i[1])
        p.barrier(wait=st)
        st = phase_l0ffn_a2(nc, p, consts, xs_i, y_i, fgt_i, mkvt_i, *a2_w, xa_i)
        p.barrier(wait=st)
        st = phase_l0attn(nc, p, consts, xa_i, bwq_d, bwk_d, bwv_d, bwo_d,
                          bbq_d, bbk_d, bbv_d, bbd_d, cosa_d, ssa_d, xb_i)
        p.barrier(wait=st)
        st = phase_l0ffn_a1(nc, p, consts, xb_i, *a1_w, xs_i, y_i,
                            fgt_i[0], fgt_i[1], mkvt_i[0], mkvt_i[1])
        p.barrier(wait=st)
        st = phase_l0ffn_a2(nc, p, consts, xs_i, y_i, fgt_i, mkvt_i, *a2_w, xc_i)
        p.barrier(wait=st)
        st = phase_l1ffn(nc, p, consts, xc_i, wg_d, wu_d, wd_d, xd_i,
                         wcache=wcache, first=True)
        p.barrier(wait=st)
        st = phase_l1attn(nc, p, consts, xd_i, dwq_d, dwk_d, dwv_d, dwo_d, xe_i)
        p.barrier(wait=st)
        st = phase_l1ffn(nc, p, consts, xe_i, wg_d, wu_d, wd_d, xout_d,
                         wcache=wcache, first=False)
        p.wait("sync", st)
        p.wait("gpsimd", st)
        block = es.enter_context(nc.Block())
        p.finalize(block)
    return nc



_PROG_CACHE = {}


def _get_prog(name):
    if name not in _PROG_CACHE:
        _PROG_CACHE[name] = {"fused": build_prog_fused}[name]()
    return _PROG_CACHE[name]


def _run(name, in_maps):
    from concourse.bass_utils import run_bass_kernel_spmd
    nc = _get_prog(name)
    res = run_bass_kernel_spmd(nc, in_maps, core_ids=list(range(8)))
    return res.results


def kernel(L0_attn_hidden_states, L0_attn_Wq, L0_attn_bq, L0_attn_Wk, L0_attn_bk,
           L0_attn_Wv, L0_attn_bv, L0_attn_Wd, L0_attn_bd, L0_ffn_ln_g, L0_ffn_ln_b,
           L0_ffn_w_qkv, L0_ffn_w_lo, L0_ffn_rt_q, L0_ffn_rt_kv, L0_ffn_qnorm_g,
           L0_ffn_cnorm_g, L0_ffn_null_kv, L0_ffn_w_q, L0_ffn_w_kv, L0_ffn_w_out,
           L1_attn_w_qkv, L1_attn_w_o, L1_ffn_w_gate, L1_ffn_w_up, L1_ffn_w_down):
    f32 = np.float32
    cont = np.ascontiguousarray
    B = 8
    xs = [cont(np.asarray(L0_attn_hidden_states[b], f32)) for b in range(B)]
    consts = host_consts()
    identf, tri = consts["identf"], consts["tri"]
    tril = (np.arange(128)[None, :] <= np.arange(128)[:, None]).astype(f32)
    iq, ikv = router_iotas()

    wt = cont(np.asarray(L0_ffn_w_qkv, f32).T)
    mk = lambda Wm: cont(np.asarray(Wm, f32).reshape(NC, 128, -1))
    COSl, SSl = rope_tables_local()
    a1_common = {
        "lng": cont(np.asarray(L0_ffn_ln_g, f32)[None, :]),
        "lnb": cont(np.asarray(L0_ffn_ln_b, f32)[None, :]),
        "wql": mk(wt[:, 0:D]), "wkl": mk(wt[:, D:2 * D]), "wvl": mk(wt[:, 2 * D:]),
        "wlo": mk(np.asarray(L0_ffn_w_lo, f32).T),
        "rtq": cont(np.asarray(L0_ffn_rt_q, f32).T.reshape(NC, 128, 2)),
        "rtkv": cont(np.asarray(L0_ffn_rt_kv, f32).T.reshape(NC, 128, 2)),
        "cosd": COSl, "ssd": SSl, "trild": tril,
        "identf": identf, "tri": tri, "iota_q": iq, "iota_kv": ikv,
    }
    w_q = np.asarray(L0_ffn_w_q, f32)
    w_kv = np.asarray(L0_ffn_w_kv, f32)
    w_out = np.asarray(L0_ffn_w_out, f32)
    qn = np.asarray(L0_ffn_qnorm_g, f32)
    cn = np.asarray(L0_ffn_cnorm_g, f32)
    nkv = np.asarray(L0_ffn_null_kv, f32)
    a2_common = {
        "wqg": np.stack([mk(w_q[g].T) for g in range(2)]),
        "wkvg": np.stack([mk(w_kv[g].T) for g in range(2)]),
        "wog": np.stack([mk(w_out[g].T) for g in range(2)]),
        "qnT": cont(np.stack([qn[g].reshape(NC, 128).T for g in range(2)])),
        "cnT": cont(np.stack([cn[g].reshape(NC, 128).T for g in range(2)])),
        "nullk": cont(np.stack([nkv[0, g].reshape(D).reshape(NC, 128) for g in range(2)])),
        "nullv": cont(np.stack([nkv[1, g].reshape(1, D) for g in range(2)])),
        "identf": identf, "tri": tri,
    }
    COSa, SSa = rope_tables_l0a()
    b_common = {
        "wq": mk(np.asarray(L0_attn_Wq, f32)), "wk": mk(np.asarray(L0_attn_Wk, f32)),
        "wv": mk(np.asarray(L0_attn_Wv, f32)), "wo": mk(np.asarray(L0_attn_Wd, f32)),
        "bq": cont(np.asarray(L0_attn_bq, f32)[None, :]),
        "bk": cont(np.asarray(L0_attn_bk, f32)[None, :]),
        "bv": cont(np.asarray(L0_attn_bv, f32)[None, :]),
        "bd": cont(np.asarray(L0_attn_bd, f32)[None, :]),
        "cosd": COSa, "ssd": SSa, "identf": identf, "tri": tri,
    }
    c_common = {
        "wg": cont(np.asarray(L1_ffn_w_gate, f32).reshape(NIC, 128, D)),
        "wu": cont(np.asarray(L1_ffn_w_up, f32).reshape(NIC, 128, D)),
        "wd": cont(np.asarray(L1_ffn_w_down, f32).reshape(NC, 128, II)),
        "identf": identf, "tri": tri,
    }
    dq, dk, dv = split_qkv_cols(np.asarray(L1_attn_w_qkv, f32))
    d_common = {
        "wq": dq, "wk": dk, "wv": dv,
        "wo": cont(np.asarray(L1_attn_w_o, f32).reshape(NC, 128, D)),
        "identf": identf, "tri": tri,
    }

    common = {
        "identf": identf, "tri": tri,
        "lng": a1_common["lng"], "lnb": a1_common["lnb"],
        "wql": a1_common["wql"], "wkl": a1_common["wkl"], "wvl": a1_common["wvl"],
        "wlo": a1_common["wlo"], "rtq": a1_common["rtq"], "rtkv": a1_common["rtkv"],
        "cosl": a1_common["cosd"], "ssl": a1_common["ssd"], "trild": a1_common["trild"],
        "iota_q": iq, "iota_kv": ikv,
        "wqg": a2_common["wqg"], "wkvg": a2_common["wkvg"], "wog": a2_common["wog"],
        "qnT": a2_common["qnT"], "cnT": a2_common["cnT"],
        "nullk": a2_common["nullk"], "nullv": a2_common["nullv"],
        "bwq": b_common["wq"], "bwk": b_common["wk"], "bwv": b_common["wv"],
        "bwo": b_common["wo"], "bbq": b_common["bq"], "bbk": b_common["bk"],
        "bbv": b_common["bv"], "bbd": b_common["bd"],
        "cosa": b_common["cosd"], "ssa": b_common["ssd"],
        "wg": c_common["wg"], "wu": c_common["wu"], "wd": c_common["wd"],
        "dwq": d_common["wq"], "dwk": d_common["wk"], "dwv": d_common["wv"],
        "dwo": d_common["wo"],
    }
    r = _run("fused", [dict(common, x=x) for x in xs])
    return np.stack([r[b]["xout"] for b in range(B)]).astype(np.float32)


# version 21 (final)
# speedup vs baseline: 1.0625x; 1.0391x over previous
import contextlib
import numpy as np

import concourse.bass as bass
import concourse.mybir as mybir

F32 = mybir.dt.float32
BF16 = mybir.dt.bfloat16
AF = mybir.ActivationFunctionType
ALU = mybir.AluOpType
AX = mybir.AxisListType

T = 2048
D = 1024
NT = T // 128
NC = D // 128
NH = 16
DH = 64
EXP_B = 16.0

ENGINES = ("sync", "vector", "scalar", "tensor", "gpsimd")

_UNAME = [0]


def uname(base):
    _UNAME[0] += 1
    return f"{base}_n{_UNAME[0]}"


class Sem:
    def __init__(self, es, nc, name):
        self.h = es.enter_context(nc.semaphore(name))
        self.n = 0


class Prog:

    CHAINED = ("vector", "scalar", "gpsimd")

    def __init__(self, nc, es):
        self.nc = nc
        self.es = es
        self.q = {e: [] for e in ENGINES}
        self._sems = {}
        self._bar = Sem(es, nc, "bar")
        self.chain = {e: Sem(es, nc, "ch_" + e) for e in self.CHAINED}

    def sem(self, name):
        if name not in self._sems:
            self._sems[name] = Sem(self.es, self.nc, name)
        return self._sems[name]

    def on(self, eng, f, inc=None, inc_by=1, wait=None, kind="compute"):
        waits = []
        if wait:
            for tok in wait:
                if tok is None:
                    continue
                s, v = tok
                waits.append((s.h, v))
        tok_out = None
        inc_pair = None
        if eng in self.CHAINED and kind == "compute":
            assert inc is None, "chained engines use the chain sem"
            ch = self.chain[eng]
            if ch.n > 0:
                waits.append((ch.h, ch.n))
            ch.n += 1
            inc_pair = (ch.h, 1)
            tok_out = (ch, ch.n)
        elif inc is not None:
            inc.n += inc_by
            inc_pair = (inc.h, inc_by)
            tok_out = (inc, inc.n)
        def run(e, f=f, waits=waits, inc_pair=inc_pair):
            for h, v in waits:
                e.wait_ge(h, v)
            ins = f(e)
            if inc_pair is not None:
                ins.then_inc(inc_pair[0], inc_pair[1])
        self.q[eng].append(run)
        return tok_out

    def wait(self, eng, toks):
        for tok in toks:
            if tok is None:
                continue
            s, v = tok
            self.q[eng].append(lambda e, h=s.h, v=v: e.wait_ge(h, v))

    def barrier(self, engines=ENGINES, wait=None):
        b = self._bar
        target = b.n + len(engines)
        wtoks = [(tok[0].h, tok[1]) for tok in (wait or []) if tok is not None]
        for e in engines:
            b.n += 1
            chw = (self.chain[e].h, self.chain[e].n) if e in self.CHAINED and self.chain[e].n > 0 else None
            def bump(en, bh=b.h, chw=chw, wtoks=wtoks):
                if chw is not None:
                    en.wait_ge(chw[0], chw[1])
                for h, v in wtoks:
                    en.wait_ge(h, v)
                en.nop().then_inc(bh, 1)
            self.q[e].append(bump)
        for e in engines:
            self.q[e].append(lambda en, bh=b.h, t=target: en.wait_ge(bh, t))

    def finalize(self, block):
        for name in ENGINES:
            qs = self.q[name]
            getattr(block, name)(lambda e, qs=qs: [f(e) for f in qs])



def host_consts():
    ident = np.eye(128, dtype=np.float32)
    tri = (np.arange(128)[None, :] >= np.arange(128)[:, None]).astype(np.float32)
    return {"identf": ident, "tri": tri}


def split_qkv_cols(w_qkv):
    w = np.asarray(w_qkv).reshape(D, NH, 3, DH)
    return [np.ascontiguousarray(w[:, :, i, :].reshape(D, D).reshape(NC, 128, D))
            for i in range(3)]



class Ctx:

    _uid = [0]

    def __init__(self, nc, es, p, ps_es=None):
        self.nc = nc
        self.es = es
        self.ps_es = ps_es or es
        self.p = p

    def sb(self, name, shape, dtype):
        Ctx._uid[0] += 1
        t = self.es.enter_context(self.nc.sbuf_tensor(
            f"{name}_u{Ctx._uid[0]}", list(shape), dtype))
        return t

    def ps(self, name, shape, dtype=F32):
        Ctx._uid[0] += 1
        t = self.ps_es.enter_context(self.nc.psum_tensor(
            f"{name}_u{Ctx._uid[0]}", list(shape), dtype))
        return t


def load_consts(cx, lsem):
    nc, p = cx.nc, cx.p
    identf_d = nc.dram_tensor("identf", [128, 128], F32, kind="ExternalInput").ap()
    tri_d = nc.dram_tensor("tri", [128, 128], F32, kind="ExternalInput").ap()
    identf = cx.sb("identf_sb", [128, 128], F32)
    identb = cx.sb("identb_sb", [128, 128], BF16)
    trib = cx.sb("trib_sb", [128, 128], BF16)
    gsem = p.sem("gld")
    t1 = p.on("sync", lambda e: e.dma_start(out=identf[:, :], in_=identf_d), inc=lsem, inc_by=16, kind="dma")
    t2 = p.on("gpsimd", lambda e: e.dma_start(out=trib[:, :], in_=tri_d), inc=gsem, inc_by=16, kind="dma")
    t3 = p.on("gpsimd", lambda e: e.dma_start(out=identb[:, :], in_=identf_d), inc=gsem, inc_by=16, kind="dma")
    tok = p.on("gpsimd", lambda e: e.nop(), wait=[t1, t3])
    return {"identf": identf, "identb": identb, "trib": trib, "tok": tok}



def emit_ln(cx, x_dram, yT, tag, ctok=None, y_tm=None, lsem=None):
    p = cx.p
    xb = cx.sb(tag + "_xb", [128, 2, D], F32)
    sq = cx.sb(tag + "_sq", [128, D], F32)
    st = cx.sb(tag + "_st", [128, NT, 8], F32)
    yb = cx.sb(tag + "_yb", [128, 2, D], BF16)
    eps = cx.sb(tag + "_eps", [128, 1], F32)
    pst = cx.ps(tag + "_pst", [128, D], BF16)
    s_v = p.sem(tag + "v")
    s_s = p.sem(tag + "s")
    s_t = p.sem(tag + "t")
    lsems = [p.sem(tag + "ldx0"), p.sem(tag + "ldx1")]
    identb = cx.consts["identb"]
    p.on("vector", lambda v: v.memset(eps[:, :], 1e-5))
    last = None
    tp_toks = [None, None]
    done_toks = [None, None]
    for t in range(NT):
        r = t % 2
        tx = p.on("sync", lambda e, t=t, r=r: e.dma_start(
            out=xb[:, r, :], in_=x_dram.rearrange("(t p) f -> p t f", p=128)[:, t, :]),
            inc=lsems[r], inc_by=16, kind="dma", wait=[done_toks[r]])
        xt = xb[:, r, :]
        s0, s1, s2, s3, s4, s5 = (st[:, t, i:i + 1] for i in range(6))
        tv1 = p.on("vector", lambda v, xt=xt, s0=s0: v.reduce_sum(s0, xt, axis=AX.X),
                   wait=[tx] + ([ctok] if t == 0 and ctok is not None else []))
        ts1 = p.on("scalar", lambda s, xt=xt, s2=s2: s.activation(sq[:, :], xt, AF.Square, accum_out=s2),
                   wait=[tx])
        p.on("vector", lambda v, s0=s0, s1=s1: v.tensor_scalar(s1, s0, 1.0 / D, None, op0=ALU.mult))
        tv2 = p.on("vector", lambda v, s3=s3, s1=s1: v.scalar_tensor_tensor(
            s3, s1, -1.0, s1, op0=ALU.mult, op1=ALU.mult), wait=[ts1])
        tv3 = p.on("vector", lambda v, s2=s2, s3=s3: v.scalar_tensor_tensor(
            s3, s2, 1.0 / D, s3, op0=ALU.mult, op1=ALU.add))
        ts2 = p.on("scalar", lambda s, s3=s3, s4=s4: s.activation(s4, s3, AF.Sqrt, bias=eps[:, 0:1]),
                   wait=[tv3])
        tv4 = p.on("vector", lambda v, s4=s4, s5=s5: v.reciprocal(s5, s4), wait=[ts2])
        s6 = st[:, t, 6:7]
        tv5 = p.on("vector", lambda v, s1=s1, s5=s5, s6=s6: v.scalar_tensor_tensor(
            s6, s1, -1.0, s5, op0=ALU.mult, op1=ALU.mult))
        tw = [tp_toks[r]] if tp_toks[r] is not None else []
        ty = p.on("scalar", lambda s, xt=xt, r=r, s5=s5, s6=s6: s.activation(
            yb[:, r, :], xt, AF.Identity, bias=s6, scale=s5), wait=[tv4, tv5] + tw)
        done_toks[r] = ty
        if y_tm is not None:
            done_toks[r] = p.on("scalar", lambda s, xt=xt, t=t, s5=s5, s6=s6: s.activation(
                y_tm[:, t, :], xt, AF.Identity, bias=s6, scale=s5))
        tts = []
        for c in range(NC):
            tts.append(p.on("tensor", lambda te, r=r, c=c: te.transpose(
                pst[:, c * 128:(c + 1) * 128], yb[:, r, c * 128:(c + 1) * 128], identb[:, :]),
                inc=s_t, wait=([ty, cx.consts["tok"], last] if c == 0 else None)))
        tp_toks[r] = tts[-1]
        last = p.on("vector", lambda v, t=t: v.tensor_copy(
            yT[:, :, t * 128:(t + 1) * 128],
            pst[:, :].rearrange("p (c t) -> p c t", t=128)), wait=[tts[-1]])
    return last



def emit_proj_featmajor(cx, yT, w_dram, outT, tag, ytok, n_out=D):
    p = cx.p
    nm = n_out // 128
    wb = cx.shared["wb"]
    pso = cx.ps(tag + "_ps", [128, 512], F32)
    s_ls = [p.sem(tag + "l0"), p.sem(tag + "l1")]
    s_m = p.sem(tag + "m")
    s_e = p.sem(tag + "e")
    ev_tok = None
    mm_g = cx.shared.get("wb_guard", [None, None])
    for m in range(nm):
        r = m % 2
        tl = p.on("gpsimd", lambda e, m=m, r=r: e.dma_start(
            out=wb[:, r, :].rearrange("p (c k) -> p c k", k=128),
            in_=w_dram[:, :, m * 128:(m + 1) * 128].transpose([1, 0, 2])),
            inc=s_ls[r], inc_by=16, kind="dma", wait=[mm_g[r]])
        for n in range(T // 512):
            mm_tok = None
            for c in range(NC):
                w = [tl, ytok, ev_tok] if c == 0 else None
                mm_tok = p.on("tensor", lambda te, r=r, c=c, n=n: te.matmul(
                    pso[:, :], wb[:, r, c * 128:(c + 1) * 128], yT[:, c, n * 512:(n + 1) * 512],
                    start=(c == 0), stop=(c == NC - 1)), inc=s_m, wait=w)
            mm_g[r] = mm_tok
            ev_tok = p.on("vector", lambda v, m=m, n=n: v.tensor_copy(
                outT[:, m, n * 512:(n + 1) * 512], pso[:, :]), wait=[mm_tok])
    cx.shared["wb_guard"] = mm_g
    return ev_tok


def emit_v_tokmajor(cx, yT, w_dram, vp, tag, ytok, ones_tok):
    p = cx.p
    wb = cx.sb(tag + "_wb", [128, NC, D], BF16)
    psv = cx.ps(tag + "_ps", [128, 2, 512], F32)
    s_l = p.sem(tag + "l")
    s_m = p.sem(tag + "m")
    s_e = p.sem(tag + "e")
    tc = p.on("gpsimd", lambda e: e.dma_start(out=wb[:, :, :], in_=w_dram.transpose([1, 0, 2])),
              inc=s_l, inc_by=16, kind="dma")
    ev_tok = None
    last_ev = {}
    for t in range(NT):
        mm_toks = [None, None]
        for c in range(NC):
            for half in range(2):
                w = [tc, ytok] if c == 0 else None
                if c == 0 and t >= 1:
                    w = [tc, ytok, last_ev.get((t - 1, half))]
                mm_toks[half] = p.on("tensor", lambda te, t=t, c=c, half=half: te.matmul(
                    psv[:, half, :], yT[:, c, t * 128:(t + 1) * 128],
                    wb[:, c, half * 512:(half + 1) * 512],
                    start=(c == 0), stop=(c == NC - 1)), inc=s_m, wait=w)
        for half in range(2):
            last_ev[(t, half)] = p.on("vector", lambda v, t=t, half=half: v.tensor_copy(
                vp[:, t, half * 8:(half + 1) * 8, 0:64],
                psv[:, half, :].rearrange("p (h d) -> p h d", d=64)),
                wait=[mm_toks[half], ones_tok])
            ev_tok = last_ev[(t, half)]
    return ev_tok



def emit_attn(cx, qT, kT, vp, ctx, tag, scale, kts_of, toks, nqc=4,
              qc_tile0=0, extra_mask=None):
    p = cx.p
    trib = cx.consts["trib"]
    identf = cx.consts["identf"]
    NS = 4
    psS = cx.ps(tag + "_psS", [128, NS, 512], F32)
    psO = cx.ps(tag + "_psO", [65, 2, 512], F32)
    psT = cx.ps(tag + "_psT", [128, 2, 4, 128], F32)
    pT = cx.sb(tag + "_pT", [128, NS, 512], BF16)
    o65 = cx.sb(tag + "_o65", [65, 2, 512], F32)
    ctxp = cx.sb(tag + "_ctxp", [128, 4, NH, 65], F32)
    rec = cx.sb(tag + "_rec", [128, 16], F32)
    scl = cx.sb(tag + "_scl", [128, 1], F32)
    bia = cx.sb(tag + "_bia", [128, 1], F32)
    s_mm = p.sem(tag + "mm")
    s_x = p.sem(tag + "x")
    s_g = p.sem(tag + "g")
    s_e = p.sem(tag + "e")
    c_tok = p.on("vector", lambda v: v.memset(scl[:, :], scale))
    c_tok2 = p.on("vector", lambda v: v.memset(bia[:, :], -EXP_B))
    p.wait("tensor", toks)
    p.wait("scalar", toks + [c_tok, c_tok2])
    p.wait("gpsimd", toks)
    st = {
        "prev_exp": [None] * NS,
        "prev_pv": [None] * NS,
        "prev_tev_b": [None, None],
        "prev_tts_b": [None, None],
        "prev_scatter_b": [None, None],
        "pv_tok": None,
        "ctx_done": [],
    }
    items = []
    tails = {}
    for qc in range(nqc):
        q0 = (qc_tile0 + qc) * 512
        for h in range(NH):
            hc, roff = h // 2, (h % 2) * 64
            ob = h % 2
            kts = list(kts_of(qc_tile0 + qc))
            def wid(kt, q0=q0):
                qlo = max(q0, kt * 128)
                qhi = min(q0 + 512, kt * 128 + 640) if extra_mask is not None else q0 + 512
                return qhi - qlo
            kts = sorted(kts, key=lambda kt: -wid(kt))
            for j, kt in enumerate(kts):
                qlo = max(q0, kt * 128)
                qhi = min(q0 + 512, kt * 128 + 640) if extra_mask is not None else q0 + 512
                w = qhi - qlo
                assert w > 0, (tag, qc, kt)
                items.append(dict(hc=hc, roff=roff, ob=ob, kt=kt, h=h, q0=q0,
                                  qlo=qlo, w=w, first=(j == 0), last=(j == len(kts) - 1),
                                  r=len(items) % NS))
            tails[len(items) - 1] = dict(h=h, ob=ob, qc=qc, q_end=(h == NH - 1))

    def emit_S(i):
        it = items[i]
        r = it["r"]
        tmm = p.on("tensor", lambda te, it=it, r=r: te.matmul(
            psS[:, r, 0:it["w"]],
            kT[it["roff"]:it["roff"] + 64, it["hc"], it["kt"] * 128:(it["kt"] + 1) * 128],
            qT[it["roff"]:it["roff"] + 64, it["hc"], it["qlo"]:it["qlo"] + it["w"]],
            start=True, stop=True), inc=s_mm,
            wait=[st["prev_exp"][r]])
        tx = p.on("scalar", lambda s, it=it, r=r: s.activation(
            pT[:, r, 0:it["w"]], psS[:, r, 0:it["w"]], AF.Exp,
            bias=bia[:, 0:1], scale=scl[:, 0:1]),
            wait=[tmm, st["prev_pv"][r]])
        st["prev_exp"][r] = tx
        mask_tok = tx
        dq = it["kt"] * 128 - it["qlo"]
        if 0 <= dq < it["w"]:
            mask_tok = p.on("gpsimd", lambda g, r=r, dq=dq: g.tensor_tensor(
                pT[:, r, dq:dq + 128], pT[:, r, dq:dq + 128], trib[:, :], op=ALU.mult),
                wait=[tx])
        if extra_mask is not None:
            dq2 = it["kt"] * 128 + 512 - it["qlo"]
            if 0 <= dq2 < it["w"]:
                ww = min(128, it["w"] - dq2)
                mask_tok = p.on("gpsimd", lambda g, r=r, dq2=dq2, ww=ww: g.tensor_tensor(
                    pT[:, r, dq2:dq2 + ww], pT[:, r, dq2:dq2 + ww], extra_mask[:, 0:ww], op=ALU.mult),
                    wait=[mask_tok])
        it["ready"] = mask_tok

    def emit_PV(i):
        it = items[i]
        r = it["r"]
        ob = it["ob"]
        extra_w = [st["prev_tev_b"][ob]] if it["first"] else []
        tpv = p.on("tensor", lambda te, it=it, r=r, ob=ob: te.matmul(
            psO[:, ob, it["qlo"] - it["q0"]:it["qlo"] - it["q0"] + it["w"]],
            vp[:, it["kt"], it["h"], :], pT[:, r, 0:it["w"]],
            start=it["first"], stop=it["last"]), inc=s_mm,
            wait=[it["ready"]] + extra_w)
        st["prev_pv"][r] = tpv
        st["pv_tok"] = tpv
        if it["last"]:
            tev = p.on("vector", lambda v, ob=ob: v.tensor_copy(o65[:, ob, :], psO[:, ob, :]),
                       wait=[tpv, st["prev_tts_b"][ob]])
            st["prev_tev_b"][ob] = tev
            tails[i]["tev"] = tev

    def emit_tail(i):
        td = tails[i]
        h, ob, qc = td["h"], td["ob"], td["qc"]
        tev = td["tev"]
        ctxp_guard = st["ctx_done"][-1] if st["ctx_done"] else None
        tts = []
        for s4 in range(4):
            tts.append(p.on("tensor", lambda te, s4=s4, ob=ob: te.transpose(
                psT[:, ob, s4, 0:65], o65[:, ob, s4 * 128:(s4 + 1) * 128], identf[0:65, 0:65]),
                inc=s_mm, wait=[tev, st["prev_scatter_b"][ob]] if s4 == 0 else None))
        st["prev_tts_b"][ob] = tts[-1]
        st["prev_scatter_b"][ob] = p.on("vector", lambda v, h=h, ob=ob: v.tensor_copy(
            ctxp[:, :, h, :], psT[:, ob, :, 0:65]),
            wait=[tts[-1], ctxp_guard])
        if td["q_end"]:
            ctok = None
            for s4 in range(4):
                tile = (qc_tile0 + qc) * 4 + s4
                tr = p.on("vector", lambda v, s4=s4: v.reciprocal(
                    rec[:, :], ctxp[:, s4, :, 64]),
                    wait=[st["prev_scatter_b"][0], st["prev_scatter_b"][1]])
                ctok = p.on("vector", lambda v, s4=s4, tile=tile: v.tensor_tensor(
                    ctx[:, tile, :].rearrange("p (h d) -> p h d", d=64),
                    ctxp[:, s4, :, 0:64],
                    rec[:, :].unsqueeze(2).broadcast_to((128, 16, 64)), op=ALU.mult),
                    wait=[tr])
            st["ctx_done"].append(ctok)

    N = len(items)
    LA = 2
    TD = 2
    for i in range(N):
        emit_S(i)
        if i >= LA:
            emit_PV(i - LA)
        if i - LA - TD in tails and i >= LA + TD:
            emit_tail(i - LA - TD)
    for i in range(N - LA, N):
        emit_PV(i)
    for e in sorted(tails):
        if e > N - 1 - LA - TD:
            emit_tail(e)
    return st["ctx_done"][-1]



def emit_outproj_residual(cx, ctx, wo_dram, x_dram, xout_dram, tag, ctx_tok,
                          bias_dram=None, rfac=None, rfac_tok=None):
    p = cx.p
    wb = cx.sb(tag + "_wb", [128, NC, D], BF16)
    brow = cx.sb(tag + "_b", [1, D], BF16) if bias_dram is not None else None
    ones1 = cx.sb(tag + "_o1", [1, 128], BF16) if bias_dram is not None else None
    ctxT = cx.sb(tag + "_ctxT", [128, NC, 128], BF16)
    xr = cx.sb(tag + "_xr", [128, 2, D], F32)
    xo = cx.sb(tag + "_xo", [128, 2, D], F32)
    psT = cx.ps(tag + "_psT", [128, D], BF16)
    pso = cx.ps(tag + "_pso", [128, 2, 512], F32)
    s_l = p.sem(tag + "l")
    s_l2 = p.sem(tag + "l2")
    s_m = p.sem(tag + "m")
    s_e = p.sem(tag + "e")
    s_st = p.sem(tag + "st")
    s_st2 = p.sem(tag + "st2")
    tc = p.on("gpsimd", lambda e: e.dma_start(out=wb[:, :, :], in_=wo_dram.transpose([1, 0, 2])),
              inc=p.sem(tag + "gl"), inc_by=16, kind="dma")
    tbias = None
    to1 = None
    if bias_dram is not None:
        tbias = p.on("gpsimd", lambda e: e.dma_start(out=brow[:, :], in_=bias_dram),
                     inc=p.sem(tag + "gl2"), inc_by=16, kind="dma")
        to1 = p.on("vector", lambda v: v.memset(ones1[:, :], 1.0))
    fma_hist = {}
    st_hist = {}
    evT_prev = None
    mm_prev = [None, None]
    st_toks = []
    for t in range(NT):
        r = t % 2
        txr = p.on("sync", lambda e, t=t, r=r: e.dma_start(
            out=xr[:, r, :], in_=x_dram.rearrange("(t p) f -> p t f", p=128)[:, t, :]),
            inc=(s_l if r == 0 else s_l2), inc_by=16, kind="dma",
            wait=[fma_hist.get((t - 2, 0)), fma_hist.get((t - 2, 1))])
        tts = []
        for c in range(NC):
            tts.append(p.on("tensor", lambda te, t=t, c=c: te.transpose(
                psT[:, c * 128:(c + 1) * 128], ctx[:, t, c * 128:(c + 1) * 128],
                cx.consts["identb"][:, :]), inc=s_m,
                wait=[ctx_tok, evT_prev] if c == 0 else None))
        evT = p.on("vector", lambda v: v.tensor_copy(
            ctxT[:, :, :], psT[:, :].rearrange("p (c t) -> p c t", t=128)),
            wait=[tts[-1], mm_prev[0], mm_prev[1]])
        evT_prev = evT
        for half in range(2):
            mm = None
            for c in range(NC):
                w = [evT, tc, fma_hist.get((t - 1, half))] if c == 0 else None
                mm = p.on("tensor", lambda te, c=c, half=half: te.matmul(
                    pso[:, half, :], ctxT[:, c, :], wb[:, c, half * 512:(half + 1) * 512],
                    start=(c == 0), stop=(c == NC - 1) and bias_dram is None), inc=s_m, wait=w)
            if bias_dram is not None:
                mm = p.on("tensor", lambda te, half=half: te.matmul(
                    pso[:, half, :], ones1[:, :], brow[:, half * 512:(half + 1) * 512],
                    start=False, stop=True), inc=s_m, wait=[tbias, to1])
            mm_prev[half] = mm
            if rfac is None:
                fma_hist[(t, half)] = p.on("vector", lambda v, r=r, half=half: v.tensor_tensor(
                    xo[:, r, half * 512:(half + 1) * 512], xr[:, r, half * 512:(half + 1) * 512],
                    pso[:, half, :], op=ALU.add),
                    wait=[mm, txr, st_hist.get((t - 2,))])
            else:
                fma_hist[(t, half)] = p.on("vector", lambda v, t=t, r=r, half=half: v.scalar_tensor_tensor(
                    xo[:, r, half * 512:(half + 1) * 512], pso[:, half, :], rfac[:, t:t + 1],
                    xr[:, r, half * 512:(half + 1) * 512], op0=ALU.mult, op1=ALU.add),
                    wait=[mm, txr, st_hist.get((t - 2,)), rfac_tok])
        st_hist[(t,)] = p.on("sync", lambda e, t=t, r=r: e.dma_start(
            out=xout_dram.rearrange("(t p) f -> p t f", p=128)[:, t, :], in_=xo[:, r, :]),
            inc=(s_st if r == 0 else s_st2), inc_by=16, kind="dma", wait=[fma_hist[(t, 0)], fma_hist[(t, 1)]])
        st_toks.append(st_hist[(t,)])
    return [st_toks[-2], st_toks[-1]]



def phase_l1attn(nc, p, consts, x_d, wq_d, wk_d, wv_d, wo_d, xout_d):
    es = contextlib.ExitStack()
    with es:
        cx = Ctx(nc, es, p)
        cx.consts = consts
        qT = cx.sb("qT", [128, NC, T], BF16)
        kT = cx.sb("kT", [128, NC, T], BF16)
        vp = cx.sb("vp", [128, NT, NH, 65], BF16)
        ones_tok = p.on("vector", lambda v: v.memset(vp[:, :, :, :], 1.0))
        esA = es.enter_context(contextlib.ExitStack())
        cxA = Ctx(nc, esA, p); cxA.consts = cx.consts
        yT = cxA.sb("yT", [128, NC, T], BF16)
        cxA.shared = {"wb": cxA.sb("wbshared", [128, 2, D], BF16)}
        ytok = emit_ln(cxA, x_d, yT, "ln", ctok=cx.consts["tok"])
        tq = emit_proj_featmajor(cxA, yT, wq_d, qT, "pq", ytok)
        tk = emit_proj_featmajor(cxA, yT, wk_d, kT, "pk", ytok)
        tv = emit_v_tokmajor(cxA, yT, wv_d, vp, "pv", ytok, ones_tok)
        p.barrier()
        esA.close()
        esB = es.enter_context(contextlib.ExitStack())
        cxB = Ctx(nc, esB, p); cxB.consts = cx.consts
        ctx_t = cxB.sb("ctx", [128, NT, D], BF16)
        esAT = es.enter_context(contextlib.ExitStack())
        cxAT = Ctx(nc, esB, p, ps_es=esAT); cxAT.consts = cx.consts
        ctx_tok = emit_attn(cxAT, qT, kT, vp, ctx_t, "at", 1.0 / 32.0,
                            kts_of=lambda qt: range(0, 4 * qt + 4), toks=[tq, tk, tv])
        esAT.close()
        st_last = emit_outproj_residual(cxB, ctx_t, wo_d, x_d, xout_d, "op", ctx_tok)
        p.wait("sync", st_last)
        p.wait("gpsimd", st_last)
    return st_last


def build_prog_l1attn():
    nc = bass.Bass("TRN2", target_bir_lowering=False, debug=False)
    es = contextlib.ExitStack()
    with es:
        p = Prog(nc, es)
        cx = Ctx(nc, es, p)
        x_d = nc.dram_tensor("x", [T, D], F32, kind="ExternalInput").ap()
        wq_d = nc.dram_tensor("wq", [NC, 128, D], F32, kind="ExternalInput").ap()
        wk_d = nc.dram_tensor("wk", [NC, 128, D], F32, kind="ExternalInput").ap()
        wv_d = nc.dram_tensor("wv", [NC, 128, D], F32, kind="ExternalInput").ap()
        wo_d = nc.dram_tensor("wo", [NC, 128, D], F32, kind="ExternalInput").ap()
        xout_d = nc.dram_tensor("xout", [T, D], F32, kind="ExternalOutput").ap()
        consts = load_consts(cx, p.sem("ld"))
        phase_l1attn(nc, p, consts, x_d, wq_d, wk_d, wv_d, wo_d, xout_d)
        block = es.enter_context(nc.Block())
        p.finalize(block)
    return nc



def rope_tables_l0a():
    inv = 1.0 / (10000.0 ** (np.arange(0, 64, 2, dtype=np.float64) / 64.0))
    pos = np.arange(T, dtype=np.float64)
    fr = np.outer(pos, inv)
    cos = np.cos(fr).astype(np.float32)
    sin = np.sin(fr).astype(np.float32)
    COS = np.ones((T, 64), np.float32)
    COS[:, 0:32] = cos
    SS = np.ascontiguousarray((-sin))
    return (np.ascontiguousarray(COS.reshape(NT, 128, 64)),
            np.ascontiguousarray(SS.reshape(NT, 128, 32)))


def rope_tables_local():
    inv = 1.0 / (10000.0 ** (np.arange(0, 64, 2, dtype=np.float64) / 64.0))
    pos = np.arange(T, dtype=np.float64)
    fr = np.outer(pos, inv)
    fr2 = np.concatenate([fr, fr], axis=1)
    cos = np.cos(fr2).astype(np.float32)
    sin = np.sin(fr2).astype(np.float32)
    SS = np.empty((T, 64), np.float32)
    SS[:, 0:32] = -sin[:, 0:32]
    SS[:, 32:64] = sin[:, 32:64]
    return (np.ascontiguousarray(cos.reshape(NT, 128, 64)),
            np.ascontiguousarray(SS.reshape(NT, 128, 64)))


def emit_qk_rope_T(cx, yT, w_dram, b_dram, outT, tag, ytok, cos_d, ss_d, mode, tick=None):
    p = cx.p
    ssw = 32 if mode == "l0a" else 64
    sh = cx.shared
    if "rope_wb" not in sh:
        sh["rope_wb"] = cx.sb("rope_wb", [128, NC, D], BF16)
        sh["rope_qf"] = cx.sb("rope_qf", [128, 2, D], F32)
        sh["rope_t2"] = cx.sb("rope_t2", [128, 2, 1024], F32)
        sh["rope_qr"] = cx.sb("rope_qr", [128, 2, D], BF16)
        sh["rope_cos"] = cx.sb("rope_cos", [128, 2, 64], F32)
        sh["rope_ss"] = cx.sb("rope_ss", [128, 2, 64], F32)
        sh["rope_g"] = {"rope_done": [None, None], "tp_done": [None, None],
                        "ev_prev": None, "mm_prev": [None, None], "w_guard": None}
    wb = sh["rope_wb"]
    qf = sh["rope_qf"]
    t2 = sh["rope_t2"]
    qr = sh["rope_qr"]
    cosb = sh["rope_cos"]
    ssb = sh["rope_ss"]
    g = sh["rope_g"]
    brow = cx.sb(tag + "_b", [1, D], BF16) if b_dram is not None else None
    ones1 = cx.sb(tag + "_o1", [1, 128], BF16) if b_dram is not None else None
    if "rope_psq" not in sh:
        sh["rope_psq"] = cx.ps("rope_psq", [128, 2, 512], F32)
        sh["rope_pst"] = cx.ps("rope_pst", [128, D], BF16)
    psq = sh["rope_psq"]
    pst = sh["rope_pst"]
    s_lw = p.sem(tag + "lw")
    s_lc = [p.sem(tag + "lc0"), p.sem(tag + "lc1")]
    s_m = p.sem(tag + "m")
    s_t = p.sem(tag + "t")
    identb = cx.consts["identb"]
    tw = p.on("gpsimd", lambda e: e.dma_start(out=wb[:, :, :], in_=w_dram.transpose([1, 0, 2])),
              inc=s_lw, inc_by=16, kind="dma", wait=[g["w_guard"]])
    tb = to1 = None
    if b_dram is not None:
        tb = p.on("gpsimd", lambda e: e.dma_start(out=brow[:, :], in_=b_dram),
                  inc=p.sem(tag + "lb"), inc_by=16, kind="dma")
        to1 = p.on("vector", lambda v: v.memset(ones1[:, :], 1.0))
    rope_done = g["rope_done"]
    tp_done = g["tp_done"]
    ev_prev = g["ev_prev"]
    mm_prev = g["mm_prev"]
    for t in range(NT):
        r = t % 2
        tcs = p.on("sync", lambda e, t=t, r=r: e.dma_start(out=cosb[:, r, :], in_=cos_d[t]),
                   inc=s_lc[r], inc_by=16, kind="dma", wait=[rope_done[r]])
        tss = p.on("sync", lambda e, t=t, r=r: e.dma_start(out=ssb[:, r, 0:ssw], in_=ss_d[t]),
                   inc=s_lc[r], inc_by=16, kind="dma")
        ev_toks = []
        for half in range(2):
            mm = None
            for c in range(NC):
                w = ([tw, ytok, g.setdefault("psq_ev", [None, None])[half]]
                     if c == 0 else None)
                mm = p.on("tensor", lambda te, t=t, c=c, half=half: te.matmul(
                    psq[:, half, :], yT[:, c, t * 128:(t + 1) * 128],
                    wb[:, c, half * 512:(half + 1) * 512],
                    start=(c == 0), stop=(c == NC - 1) and b_dram is None), inc=s_m, wait=w)
            if b_dram is not None:
                mm = p.on("tensor", lambda te, half=half: te.matmul(
                    psq[:, half, :], ones1[:, :], brow[:, half * 512:(half + 1) * 512],
                    start=False, stop=True), inc=s_m, wait=[to1, tb])
            mm_prev[half] = mm
            tev_ = p.on("vector", lambda v, r=r, half=half: v.tensor_copy(
                qf[:, r, half * 512:(half + 1) * 512], psq[:, half, :]),
                wait=[mm, rope_done[r]])
            ev_toks.append(tev_)
            g.setdefault("psq_ev", [None, None])[half] = tev_
        tr1 = p.on("vector", lambda v, r=r: v.tensor_tensor(
            t2[:, r, 0:NH * ssw].rearrange("p (h j) -> p h j", j=ssw),
            _swap_view(qf, r, mode),
            ssb[:, r, 0:ssw].unsqueeze(1).broadcast_to((128, NH, ssw)), op=ALU.mult),
            wait=[ev_toks[0], ev_toks[1], tss])
        tr2 = p.on("vector", lambda v, r=r: v.tensor_tensor(
            qr[:, r, :].rearrange("p (h j) -> p h j", j=64),
            qf[:, r, :].rearrange("p (h j) -> p h j", j=64),
            cosb[:, r, :].unsqueeze(1).broadcast_to((128, NH, 64)), op=ALU.mult),
            wait=[tp_done[r]])
        if mode == "l0a":
            tr3 = p.on("vector", lambda v, r=r: v.tensor_tensor(
                qr[:, r, :].rearrange("p (h j) -> p h j", j=64)[:, :, 0:32],
                qr[:, r, :].rearrange("p (h j) -> p h j", j=64)[:, :, 0:32],
                t2[:, r, 0:512].rearrange("p (h j) -> p h j", j=32), op=ALU.add))
        else:
            tr3 = p.on("vector", lambda v, r=r: v.tensor_tensor(
                qr[:, r, :], qr[:, r, :], t2[:, r, 0:1024], op=ALU.add))
        rope_done[r] = tr3
        tts = []
        for c in range(NC):
            tts.append(p.on("tensor", lambda te, r=r, c=c: te.transpose(
                pst[:, c * 128:(c + 1) * 128], qr[:, r, c * 128:(c + 1) * 128], identb[:, :]),
                inc=s_t, wait=[tr3, cx.consts["tok"], ev_prev] if c == 0 else None))
        tp_done[r] = tts[-1]
        ev_prev = p.on("vector", lambda v, t=t: v.tensor_copy(
            outT[:, :, t * 128:(t + 1) * 128],
            pst[:, :].rearrange("p (c t) -> p c t", t=128)), wait=[tts[-1]])
        if tick is not None:
            tick()
    g["rope_done"] = rope_done
    g["tp_done"] = tp_done
    g["ev_prev"] = ev_prev
    g["mm_prev"] = mm_prev
    g["w_guard"] = mm_prev[1]
    return ev_prev


def _swap_view(qf, r, mode):
    ap = qf[:, r, :]
    if mode == "l0a":
        return bass.AP(ap.tensor, ap.offset + 16,
                       [list(ap.ap[0]), [64, NH], [-16, 2], [1, 16]])
    else:
        return bass.AP(ap.tensor, ap.offset + 32,
                       [list(ap.ap[0]), [64, NH], [-32, 2], [1, 32]])


def emit_v_bias(cx, yT, w_dram, b_dram, vp, tag, ytok, ones_tok, n_tiles=NT, tick=None):
    p = cx.p
    wguard = None
    if "rope_wb" in cx.shared:
        wb = cx.shared["rope_wb"]
        wguard = cx.shared["rope_g"]["w_guard"]
    else:
        wb = cx.sb(tag + "_wb", [128, NC, D], BF16)
    brow = cx.sb(tag + "_b", [1, D], BF16) if b_dram is not None else None
    ones1 = cx.sb(tag + "_o1", [1, 128], BF16) if b_dram is not None else None
    psq_ev = [None, None]
    if "rope_psq" in cx.shared:
        psv = cx.shared["rope_psq"]
        psq_ev = cx.shared["rope_g"].get("psq_ev", [None, None]) if False else cx.shared.get("rope_g", {}).get("psq_ev", [None, None])
    else:
        psv = cx.ps(tag + "_ps", [128, 2, 512], F32)
    s_l = p.sem(tag + "l")
    s_m = p.sem(tag + "m")
    tc = p.on("gpsimd", lambda e: e.dma_start(out=wb[:, :, :], in_=w_dram.transpose([1, 0, 2])),
              inc=s_l, inc_by=16, kind="dma", wait=[wguard])
    tb = to1 = None
    if b_dram is not None:
        tb = p.on("gpsimd", lambda e: e.dma_start(out=brow[:, :], in_=b_dram),
                  inc=p.sem(tag + "lb"), inc_by=16, kind="dma")
        to1 = p.on("vector", lambda v: v.memset(ones1[:, :], 1.0))
    ev_tok = None
    last_ev = {}
    for t in range(n_tiles):
        mm_toks = [None, None]
        for half in range(2):
            for c in range(NC):
                w = [tc, ytok, psq_ev[half]] if c == 0 else None
                if c == 0 and t >= 1:
                    w = [tc, ytok, last_ev.get((t - 1, half))]
                mmv = p.on("tensor", lambda te, t=t, c=c, half=half: te.matmul(
                    psv[:, half, :], yT[:, c, t * 128:(t + 1) * 128],
                    wb[:, c, half * 512:(half + 1) * 512],
                    start=(c == 0), stop=(c == NC - 1) and b_dram is None), inc=s_m, wait=w)
            if b_dram is not None:
                mmv = p.on("tensor", lambda te, t=t, half=half: te.matmul(
                    psv[:, half, :], ones1[:, :], brow[:, half * 512:(half + 1) * 512],
                    start=False, stop=True), inc=s_m, wait=[to1, tb])
            mm_toks[half] = mmv
        for half in range(2):
            last_ev[(t, half)] = p.on("vector", lambda v, t=t, half=half: v.tensor_copy(
                vp[:, t, half * 8:(half + 1) * 8, 0:64],
                psv[:, half, :].rearrange("p (h d) -> p h d", d=64)),
                wait=[mm_toks[half], ones_tok])
            ev_tok = last_ev[(t, half)]
        if tick is not None:
            tick()
    return ev_tok



def phase_l0attn(nc, p, consts, x_d, wq_d, wk_d, wv_d, wo_d,
                 bq_d, bk_d, bv_d, bd_d, cos_d, ss_d, xout_d):
    es = contextlib.ExitStack()
    with es:
        cx = Ctx(nc, es, p)
        cx.consts = consts
        qT = cx.sb("qT", [128, NC, T], BF16)
        kT = cx.sb("kT", [128, NC, T], BF16)
        vp = cx.sb("vp", [128, NT, NH, 65], BF16)
        ones_tok = p.on("vector", lambda v: v.memset(vp[:, :, :, :], 1.0))
        esA = es.enter_context(contextlib.ExitStack())
        cxA = Ctx(nc, esA, p); cxA.consts = cx.consts; cxA.shared = {}
        yT = cxA.sb("yT", [128, NC, T], BF16)
        ytok = emit_ln(cxA, x_d, yT, "ln", ctok=cx.consts["tok"])
        t_q = emit_qk_rope_T(cxA, yT, wq_d, bq_d, qT, "pq", ytok, cos_d, ss_d, "l0a")
        t_k = emit_qk_rope_T(cxA, yT, wk_d, bk_d, kT, "pk", ytok, cos_d, ss_d, "l0a")
        t_v = emit_v_bias(cxA, yT, wv_d, bv_d, vp, "pv", ytok, ones_tok)
        p.barrier()
        esA.close()
        esB = es.enter_context(contextlib.ExitStack())
        cxB = Ctx(nc, esB, p); cxB.consts = cx.consts
        ctx_t = cxB.sb("ctx", [128, NT, D], BF16)
        esAT = es.enter_context(contextlib.ExitStack())
        cxAT = Ctx(nc, esB, p, ps_es=esAT); cxAT.consts = cx.consts
        ctx_tok = emit_attn(cxAT, qT, kT, vp, ctx_t, "at", 1.0 / 8.0,
                            kts_of=lambda qt: range(0, 4 * qt + 4), toks=[t_q, t_k, t_v])
        esAT.close()
        st_last = emit_outproj_residual(cxB, ctx_t, wo_d, x_d, xout_d, "op", ctx_tok,
                                        bias_dram=bd_d)
        p.wait("sync", st_last)
        p.wait("gpsimd", st_last)
    return st_last


def build_prog_l0attn():
    nc = bass.Bass("TRN2", target_bir_lowering=False, debug=False)
    es = contextlib.ExitStack()
    with es:
        p = Prog(nc, es)
        cx = Ctx(nc, es, p)
        x_d = nc.dram_tensor("x", [T, D], F32, kind="ExternalInput").ap()
        wq_d = nc.dram_tensor("wq", [NC, 128, D], F32, kind="ExternalInput").ap()
        wk_d = nc.dram_tensor("wk", [NC, 128, D], F32, kind="ExternalInput").ap()
        wv_d = nc.dram_tensor("wv", [NC, 128, D], F32, kind="ExternalInput").ap()
        wo_d = nc.dram_tensor("wo", [NC, 128, D], F32, kind="ExternalInput").ap()
        bq_d = nc.dram_tensor("bq", [1, D], F32, kind="ExternalInput").ap()
        bk_d = nc.dram_tensor("bk", [1, D], F32, kind="ExternalInput").ap()
        bv_d = nc.dram_tensor("bv", [1, D], F32, kind="ExternalInput").ap()
        bd_d = nc.dram_tensor("bd", [1, D], F32, kind="ExternalInput").ap()
        cos_d = nc.dram_tensor("cosd", [NT, 128, 64], F32, kind="ExternalInput").ap()
        ss_d = nc.dram_tensor("ssd", [NT, 128, 32], F32, kind="ExternalInput").ap()
        xout_d = nc.dram_tensor("xout", [T, D], F32, kind="ExternalOutput").ap()
        consts = load_consts(cx, p.sem("ld"))
        phase_l0attn(nc, p, consts, x_d, wq_d, wk_d, wv_d, wo_d,
                     bq_d, bk_d, bv_d, bd_d, cos_d, ss_d, xout_d)
        block = es.enter_context(nc.Block())
        p.finalize(block)
    return nc



def emit_quant(cx, p, src_ap, dst_ap, U1, U3, U4, M, S, wtok, W, wait_extra=None):
    nb = W // 16
    t1 = p.on("scalar", lambda s: s.activation(U1[:, 0:W], src_ap, AF.Abs),
              wait=[wtok] + (wait_extra or []))
    t2 = p.on("vector", lambda v: v.reduce_max(
        M[:, 0:nb].rearrange("p (a b) -> p a b", b=1),
        U1[:, 0:W].rearrange("p (a b) -> p a b", b=16), axis=AX.X), wait=[t1])
    t3 = p.on("vector", lambda v: v.tensor_scalar(
        S[:, 0:nb], M[:, 0:nb], 1.0 / 6.0, 1e-12, op0=ALU.mult, op1=ALU.max), wait=[t2])
    t4 = p.on("vector", lambda v: v.reciprocal(M[:, 0:nb], S[:, 0:nb]), wait=[t3])
    bc = lambda x: x[:, 0:nb].unsqueeze(2).broadcast_to((128, nb, 16))
    t5 = p.on("vector", lambda v: v.tensor_tensor(
        U1[:, 0:W].rearrange("p (a b) -> p a b", b=16),
        U1[:, 0:W].rearrange("p (a b) -> p a b", b=16), bc(M), op=ALU.mult), wait=[t4])
    ta = p.on("vector", lambda g: g.tensor_scalar(U3[:, 0:W], U1[:, 0:W], 0.25, None, op0=ALU.is_gt), wait=[t5])
    for c in (0.75, 1.25, 1.75):
        ta = p.on("vector", lambda g, c=c: g.scalar_tensor_tensor(
            U3[:, 0:W], U1[:, 0:W], c, U3[:, 0:W], op0=ALU.is_gt, op1=ALU.add))
    tb = p.on("vector", lambda g: g.tensor_scalar(U4[:, 0:W], U1[:, 0:W], 5.0, 2.0, op0=ALU.is_gt, op1=ALU.mult))
    for c in (2.5, 3.5):
        tb = p.on("vector", lambda g, c=c: g.scalar_tensor_tensor(
            U4[:, 0:W], U1[:, 0:W], c, U4[:, 0:W], op0=ALU.is_gt, op1=ALU.add))
    tq = p.on("vector", lambda v: v.scalar_tensor_tensor(
        U3[:, 0:W], U3[:, 0:W], 0.5, U4[:, 0:W], op0=ALU.mult, op1=ALU.add), wait=[ta, tb])
    tsg = p.on("scalar", lambda s: s.activation(U1[:, 0:W], src_ap, AF.Sign), wait=[ta, tb])
    tm = p.on("vector", lambda v: v.tensor_tensor(U3[:, 0:W], U3[:, 0:W], U1[:, 0:W], op=ALU.mult),
              wait=[tq, tsg])
    tout = p.on("vector", lambda v: v.tensor_tensor(
        dst_ap.rearrange("p (a b) -> p a b", b=16),
        U3[:, 0:W].rearrange("p (a b) -> p a b", b=16), bc(S), op=ALU.mult), wait=[tm])
    return tout



II = 2816
NIC = II // 128
NSL = [(0, 512), (512, 1024), (1024, 1536), (1536, 2048), (2048, 2560), (2560, 2816)]


def phase_l1ffn(nc, p, consts, x_d, wg_d, wu_d, wd_d, xout_d, wcache=None, first=True):
    es = contextlib.ExitStack()
    with es:
        cx = Ctx(nc, es, p)
        cx.consts = consts
        identb = cx.consts["identb"]
        wgT = cx.sb("wgT", [128, NC, II], BF16)
        wuT = cx.sb("wuT", [128, NC, II], BF16)
        wdT = cx.sb("wdT", [128, NIC, D], BF16)
        U1 = cx.sb("U1", [128, II], F32)
        U3 = cx.sb("U3", [128, II], BF16)
        U4 = cx.sb("U4", [128, II], BF16)
        M = cx.sb("Mq", [128, II // 16], F32)
        S = cx.sb("Sq", [128, II // 16], F32)
        s_lw = [p.sem("lw0"), p.sem("lw1")]
        s_t = p.sem("wt")
        extra_st = []

        pst = cx.ps("pstC", [128, 512], BF16) if first else None
        esA = es.enter_context(contextlib.ExitStack())
        cxA = Ctx(nc, esA, p); cxA.consts = cx.consts
        wstage = cxA.sb("wstage", [128, 2, II], F32) if first else None
        wqb = cxA.sb("wqb", [128, II], BF16) if first else None

        carry = {"q": None, "ev": None}

        def prep_weights(w_dram, n_tiles, width, outT):
            toks = [carry["ev"]]
            prev_q = carry["q"]
            for i in range(n_tiles):
                r = i % 2
                tl = p.on("sync", lambda e, i=i, r=r, width=width, w_dram=w_dram: e.dma_start(
                    out=wstage[:, r, 0:width], in_=w_dram[i]),
                    inc=s_lw[r], inc_by=16, kind="dma", wait=[prev_q])
                tq = emit_quant(cx, p, wstage[:, r, 0:width], wqb[:, 0:width],
                                U1, U3, U4, M, S, tl, width, wait_extra=[toks[-1]])
                prev_q = tq
                nblk = width // 128
                for b0 in range(0, nblk, 4):
                    tts = []
                    for b in range(b0, min(b0 + 4, nblk)):
                        tts.append(p.on("tensor", lambda te, b=b, b0=b0: te.transpose(
                            pst[:, (b - b0) * 128:(b - b0 + 1) * 128],
                            wqb[:, b * 128:(b + 1) * 128], identb[:, :]),
                            inc=s_t, wait=[tq, cx.consts["tok"], toks[-1]] if b == b0 else None))
                    nb2 = min(b0 + 4, nblk) - b0
                    toks.append(p.on("vector", lambda v, i=i, b0=b0, nb2=nb2, outT=outT: v.tensor_copy(
                        outT[:, b0:b0 + nb2, i * 128:(i + 1) * 128],
                        pst[:, 0:nb2 * 128].rearrange("p (c t) -> p c t", t=128)),
                        wait=[tts[-1]]))
            carry["q"] = prev_q
            carry["ev"] = toks[-1]
            return toks[-1]

        if first:
            tg = prep_weights(wg_d, NIC, D, wgT)
            tu = prep_weights(wu_d, NIC, D, wuT)
            td = prep_weights(wd_d, NC, II, wdT)
            if wcache is not None:
                s_wst = p.sem("wcs")
                p.on("gpsimd", lambda e: e.dma_start(
                    out=wcache["g"], in_=wgT[:, :, :]), inc=s_wst, inc_by=16,
                    kind="dma", wait=[tg])
                p.on("gpsimd", lambda e: e.dma_start(
                    out=wcache["u"], in_=wuT[:, :, :]), inc=s_wst, inc_by=16,
                    kind="dma", wait=[tu])
                extra_st.append(p.on("gpsimd", lambda e: e.dma_start(
                    out=wcache["d"], in_=wdT[:, :, :]), inc=s_wst, inc_by=16,
                    kind="dma", wait=[td]))
            p.barrier()
        else:
            tg = p.on("sync", lambda e: e.dma_start(out=wgT[:, :, :], in_=wcache["g"]),
                      inc=p.sem("wcl"), inc_by=16, kind="dma")
            tu = p.on("gpsimd", lambda e: e.dma_start(out=wuT[:, :, :], in_=wcache["u"]),
                      inc=p.sem("wclu"), inc_by=16, kind="dma")
            td = p.on("sync", lambda e: e.dma_start(out=wdT[:, :, :], in_=wcache["d"]),
                      inc=p.sem("wcld"), inc_by=16, kind="dma")
        esA.close()

        esB = es.enter_context(contextlib.ExitStack())
        cxB = Ctx(nc, esB, p); cxB.consts = cx.consts
        xb = cxB.sb("xbC", [128, 2, D], F32)
        yf = cxB.sb("yfC", [128, 1, D], F32)
        st = cxB.sb("stC", [128, NT, 8], F32)
        epsb = cxB.sb("epsC", [128, 1], F32)
        xq = cxB.sb("xqC", [128, D], BF16)
        Ux1 = cxB.sb("Ux1", [128, D], F32)
        Ux3 = cxB.sb("Ux3", [128, D], BF16)
        Ux4 = cxB.sb("Ux4", [128, D], BF16)
        Mx = cxB.sb("MxC", [128, D // 16], F32)
        Sx = cxB.sb("SxC", [128, D // 16], F32)
        xqT = cxB.sb("xqTC", [128, NC, 128], BF16)
        HB = cxB.sb("HB", [128, II], F32)
        hq = cxB.sb("hqC", [128, II], BF16)
        hqT = cxB.sb("hqTC", [128, NIC, 128], BF16)
        psG = cxB.ps("psG", [128, 512], F32)
        psU = cxB.ps("psU", [128, 512], F32)
        psD = cxB.ps("psD", [128, 2, 512], F32)
        pstx = cxB.ps("pstx", [128, D], BF16)
        s_lx = [p.sem("lx0"), p.sem("lx1")]
        s_m = p.sem("mC")
        s_tx = p.sem("txC")
        s_st = [p.sem("stC0"), p.sem("stC1")]
        p.on("vector", lambda v: v.memset(epsb[:, :], 1e-5))
        xload_guard = [None, None]
        store_guard = [None, None]
        prev = {}
        for t in range(NT):
            r = t % 2
            tx = p.on("sync", lambda e, t=t, r=r: e.dma_start(
                out=xb[:, r, :], in_=x_d.rearrange("(t p) f -> p t f", p=128)[:, t, :]),
                inc=s_lx[r], inc_by=16, kind="dma", wait=[xload_guard[r]])
            xt = xb[:, r, :]
            s0, s1, s2, s3, s4, s5, s6 = (st[:, t, i:i + 1] for i in range(7))
            tv1 = p.on("vector", lambda v, xt=xt, s0=s0: v.reduce_sum(s0, xt, axis=AX.X),
                       wait=[tx] + ([tg, tu, td, cx.consts["tok"]] if t == 0 else []))
            ts1 = p.on("scalar", lambda s, xt=xt, s2=s2: s.activation(Ux3[:, 0:D], xt, AF.Square, accum_out=s2),
                       wait=[tx, prev.get("xq_tout")])
            p.on("vector", lambda v, s0=s0, s1=s1: v.tensor_scalar(s1, s0, 1.0 / D, None, op0=ALU.mult))
            p.on("vector", lambda v, s3=s3, s1=s1: v.scalar_tensor_tensor(
                s3, s1, -1.0, s1, op0=ALU.mult, op1=ALU.mult), wait=[ts1])
            tv3 = p.on("vector", lambda v, s2=s2, s3=s3: v.scalar_tensor_tensor(
                s3, s2, 1.0 / D, s3, op0=ALU.mult, op1=ALU.add))
            ts2 = p.on("scalar", lambda s, s3=s3, s4=s4: s.activation(s4, s3, AF.Sqrt, bias=epsb[:, 0:1]),
                       wait=[tv3])
            tv4 = p.on("vector", lambda v, s4=s4, s5=s5: v.reciprocal(s5, s4), wait=[ts2])
            tv5 = p.on("vector", lambda v, s1=s1, s5=s5, s6=s6: v.scalar_tensor_tensor(
                s6, s1, -1.0, s5, op0=ALU.mult, op1=ALU.mult))
            tyf = p.on("scalar", lambda s, xt=xt, s5=s5, s6=s6: s.activation(
                yf[:, 0, :], xt, AF.Identity, bias=s6, scale=s5),
                wait=[tv4, tv5, prev.get(("yf",)), store_guard[0], store_guard[1]])
            tqx = emit_quant(cx, p, yf[:, 0, :], xq[:, :], Ux1, Ux3, Ux4, Mx, Sx, tyf, D,
                             wait_extra=[prev.get("xq_rd"), prev.get("xq_tout")])
            prev[("yf",)] = tqx
            prev["xq_tout"] = tqx
            ttx = []
            for c in range(NC):
                ttx.append(p.on("tensor", lambda te, c=c: te.transpose(
                    pstx[:, c * 128:(c + 1) * 128], xq[:, c * 128:(c + 1) * 128], identb[:, :]),
                    inc=s_tx, wait=[tqx, prev.get("xqT_rd")] if c == 0 else None))
            tevx = p.on("vector", lambda v: v.tensor_copy(
                xqT[:, :, :], pstx[:, :].rearrange("p (c t) -> p c t", t=128)), wait=[ttx[-1]])
            prev["xq_rd"] = ttx[-1]
            h_toks = []
            last_mmG = None
            last_mmU = None
            for (a, b) in NSL:
                wmm = b - a
                mmG = None
                mmU = None
                for c in range(NC):
                    w = [tevx, prev.get(("hG",))] if c == 0 else None
                    mmG = p.on("tensor", lambda te, c=c, a=a, b=b: te.matmul(
                        psG[:, 0:b - a], xqT[:, c, :], wgT[:, c, a:b],
                        start=(c == 0), stop=(c == NC - 1)), inc=s_m, wait=w)
                for c in range(NC):
                    w = [tevx, prev.get(("hU",))] if c == 0 else None
                    mmU = p.on("tensor", lambda te, c=c, a=a, b=b: te.matmul(
                        psU[:, 0:b - a], xqT[:, c, :], wuT[:, c, a:b],
                        start=(c == 0), stop=(c == NC - 1)), inc=s_m, wait=w)
                tsil = p.on("scalar", lambda s, a=a, b=b: s.activation(
                    U1[:, a:b], psG[:, 0:b - a], AF.Silu), wait=[mmG, prev.get("hq_u1")])
                prev[("hG",)] = tsil
                th = p.on("vector", lambda v, a=a, b=b: v.tensor_tensor(
                    HB[:, a:b], U1[:, a:b], psU[:, 0:b - a], op=ALU.mult),
                    wait=[tsil, mmU, prev.get("hb_rd")])
                prev[("hU",)] = th
                h_toks.append(th)
            tqh = emit_quant(cx, p, HB[:, :], hq[:, :], U1, U3, U4, M, S, h_toks[-1],
                             II, wait_extra=[prev.get("hq_rd"), prev.get("hq_u1")] + h_toks[:-1])
            prev["hb_rd"] = tqh
            prev["hq_u1"] = tqh
            tth_ev = []
            for b0 in range(0, NIC, 4):
                tts = []
                for b in range(b0, min(b0 + 4, NIC)):
                    tts.append(p.on("tensor", lambda te, b=b, b0=b0: te.transpose(
                        pstx[:, (b - b0) * 128:(b - b0 + 1) * 128],
                        hq[:, b * 128:(b + 1) * 128], identb[:, :]),
                        inc=s_tx, wait=[tqh, tevx if b0 == 0 else None,
                                        prev.get("hqT_rd"), tth_ev[-1] if tth_ev else prev.get("xqT_rd2")] if b == b0 else None))
                nb2 = min(b0 + 4, NIC) - b0
                tth_ev.append(p.on("vector", lambda v, b0=b0, nb2=nb2: v.tensor_copy(
                    hqT[:, b0:b0 + nb2, :],
                    pstx[:, 0:nb2 * 128].rearrange("p (c t) -> p c t", t=128)),
                    wait=[tts[-1]]))
            prev["hq_rd"] = tth_ev[-1]
            prev["xqT_rd2"] = tth_ev[-1]
            fma = [None, None]
            for half in range(2):
                mm = None
                for ic in range(NIC):
                    w = [tth_ev[-1], prev.get(("dn", half))] if ic == 0 else None
                    mm = p.on("tensor", lambda te, ic=ic, half=half: te.matmul(
                        psD[:, half, :], hqT[:, ic, :], wdT[:, ic, half * 512:(half + 1) * 512],
                        start=(ic == 0), stop=(ic == NIC - 1)), inc=s_m, wait=w)
                prev["xqT_rd"] = mm if half == 1 else prev.get("xqT_rd")
                fma[half] = p.on("vector", lambda v, r=r, half=half: v.scalar_tensor_tensor(
                    yf[:, 0, half * 512:(half + 1) * 512], psD[:, half, :], 0.5,
                    xb[:, r, half * 512:(half + 1) * 512], op0=ALU.mult, op1=ALU.add),
                    wait=[mm, tx, store_guard[0], store_guard[1]])
                prev[("dn", half)] = fma[half]
            prev["hqT_rd"] = prev[("dn", 1)]
            xload_guard[r] = fma[1]
            tst = p.on("sync", lambda e, t=t, r=r: e.dma_start(
                out=xout_d.rearrange("(t p) f -> p t f", p=128)[:, t, :], in_=yf[:, 0, :]),
                inc=s_st[r], inc_by=16, kind="dma", wait=[fma[0], fma[1]])
            store_guard[r] = tst
        p.wait("sync", store_guard)
        p.wait("gpsimd", store_guard)
        st_ret = [store_guard[0], store_guard[1]] + extra_st
    return st_ret


def build_prog_l1ffn():
    nc = bass.Bass("TRN2", target_bir_lowering=False, debug=False)
    es = contextlib.ExitStack()
    with es:
        p = Prog(nc, es)
        cx = Ctx(nc, es, p)
        x_d = nc.dram_tensor("x", [T, D], F32, kind="ExternalInput").ap()
        wg_d = nc.dram_tensor("wg", [NIC, 128, D], F32, kind="ExternalInput").ap()
        wu_d = nc.dram_tensor("wu", [NIC, 128, D], F32, kind="ExternalInput").ap()
        wd_d = nc.dram_tensor("wd", [NC, 128, II], F32, kind="ExternalInput").ap()
        xout_d = nc.dram_tensor("xout", [T, D], F32, kind="ExternalOutput").ap()
        consts = load_consts(cx, p.sem("ld"))
        phase_l1ffn(nc, p, consts, x_d, wg_d, wu_d, wd_d, xout_d)
        block = es.enter_context(nc.Block())
        p.finalize(block)
    return nc



W_WIN = 512
NW = 3
NQSEL = 256
NKVSEL = 1024
KQ = 288.0
KKV = 1152.0
N_ITERS = 20
EPS_CD = 0.03
EPS_INIT = 4.0
EPS_DECAY = 0.7


def cd_eps_schedule():
    return [max(EPS_INIT * EPS_DECAY ** i, EPS_CD) for i in range(N_ITERS)]


def emit_router(cx, sQ, sKV, tags, stoks, ikv, tl2, bufs, eng="vector", res=None):
    import math
    p = cx.p
    nc = cx.nc
    mq, mkv, scr, red = bufs
    out_toks = []
    import os

    def do_router(s_t, nseg, seg, kk, offs, mask, tag):
        logk = math.log(kk)
        bneg = scr[:, 0, 0:nseg * seg]
        t1 = scr[:, 1, 0:nseg * seg]
        sv = s_t[:, 0:nseg * seg].rearrange("p (a b) -> p a b", b=seg)
        t1s = t1.rearrange("p (a b) -> p a b", b=seg)
        bnegs = bneg.rearrange("p (a b) -> p a b", b=seg)
        m_ = red[:, 0, 0:nseg]
        ssum = red[:, 1, 0:nseg]
        a_ = red[:, 2, 0:nseg]
        tmp = red[:, 3, 0:nseg]
        mb = lambda x: x.unsqueeze(2).broadcast_to((2, nseg, seg))
        tk = p.on(eng, lambda v: v.tensor_scalar(bneg, s_t[:, 0:nseg * seg], -1.0, None, op0=ALU.mult),
                  wait=stoks + [tl2])
        yield
        for it, ce in enumerate(cd_eps_schedule()):
            yield
            p.on(eng, lambda v: v.tensor_tensor(t1, s_t[:, 0:nseg * seg], bneg, op=ALU.add))
            p.on(eng, lambda v: v.reduce_max(
                m_.rearrange("p (a b) -> p a b", b=1), t1s, axis=AX.X))
            p.on(eng, lambda v, ce=ce: v.tensor_tensor(t1s, t1s, mb(m_), op=ALU.subtract))
            tsc = p.on(eng, lambda v, ce=ce: v.tensor_scalar(t1, t1, 1.0 / ce, None, op0=ALU.mult))
            tex = p.on("scalar", lambda s: s.activation(t1, t1, AF.Exp), wait=[tsc])
            trs = p.on(eng, lambda v: v.reduce_sum(
                ssum.rearrange("p (a b) -> p a b", b=1), t1s, axis=AX.X), wait=[tex])
            tln = p.on("scalar", lambda s: s.activation(tmp, ssum, AF.Ln), wait=[trs])
            p.on(eng, lambda v, ce=ce: v.tensor_scalar(tmp, tmp, ce, None, op0=ALU.mult), wait=[tln])
            p.on(eng, lambda v: v.tensor_tensor(tmp, tmp, m_, op=ALU.add))
            p.on(eng, lambda v, ce=ce, logk=logk: v.tensor_scalar(
                a_, tmp, -1.0, ce * logk, op0=ALU.mult, op1=ALU.add))
            p.on(eng, lambda v: v.tensor_tensor(t1s, sv, mb(a_), op=ALU.add))
            p.on(eng, lambda v: v.tensor_scalar(bneg, t1, 0.0, -1.0, op0=ALU.max, op1=ALU.mult))
        C0 = -3.119162312651975
        I_ = scr[:, 0, 0:nseg * seg]
        Z_ = scr[:, 2, 0:nseg * seg]
        key = s_t[:, 0:nseg * seg]
        p.on(eng, lambda v: v.tensor_tensor(t1s, sv, mb(a_), op=ALU.add))
        p.on(eng, lambda v: v.tensor_scalar(I_, t1, 0.0, None, op0=ALU.is_ge))
        p.on(eng, lambda v, C0=C0: v.tensor_scalar(Z_, t1, C0, None, op0=ALU.is_le))
        p.on(eng, lambda v: v.tensor_tensor(key, I_, Z_, op=ALU.add))
        p.on(eng, lambda v: v.tensor_scalar(key, key, -1.0, 1.0, op0=ALU.mult, op1=ALU.add))
        p.on(eng, lambda v: v.tensor_tensor(key, key, t1, op=ALU.mult))
        for w, off in enumerate(offs):
            p.on(eng, lambda v, w=w, off=off: v.tensor_scalar(
                t1[:, w * seg:(w + 1) * seg], ikv[:, w * seg:(w + 1) * seg],
                3999.0 + off, None, op0=ALU.add))
        p.on(eng, lambda v: v.tensor_tensor(I_, I_, t1, op=ALU.mult))
        p.on(eng, lambda v: v.tensor_tensor(key, key, I_, op=ALU.add))
        for w, off in enumerate(offs):
            p.on(eng, lambda v, w=w, off=off, seg=seg: v.tensor_scalar(
                t1[:, w * seg:(w + 1) * seg], ikv[:, w * seg:(w + 1) * seg],
                -(4001.0 + seg) + off, None, op0=ALU.add))
        p.on(eng, lambda v: v.tensor_tensor(Z_, Z_, t1, op=ALU.mult))
        tkey = p.on(eng, lambda v: v.tensor_tensor(key, key, Z_, op=ALU.add))
        lo = red[:, 0, 0:nseg]
        hi = red[:, 1, 0:nseg]
        cnt = red[:, 2, 0:nseg]
        mid = red[:, 3, 0:nseg]
        ge = red[:, 3, 4:4 + nseg]
        d1 = red[:, 2, 4:4 + nseg]
        d2 = red[:, 1, 4:4 + nseg]
        kp = NQSEL if tag == "q" else NKVSEL
        p.on(eng, lambda v: v.memset(lo, -6000.0))
        p.on(eng, lambda v: v.memset(hi, 6000.0))
        keys_seg = key.rearrange("p (a b) -> p a b", b=seg)
        for it in range(30):
            yield
            p.on(eng, lambda v: v.tensor_tensor(mid, lo, hi, op=ALU.add))
            p.on(eng, lambda v: v.tensor_scalar(mid, mid, 0.5, None, op0=ALU.mult))
            p.on(eng, lambda v: v.tensor_tensor(t1s, keys_seg, mb(mid), op=ALU.is_gt))
            p.on(eng, lambda v: v.reduce_sum(
                cnt.rearrange("p (a b) -> p a b", b=1), t1s, axis=AX.X))
            p.on(eng, lambda v, kp=kp: v.tensor_scalar(ge, cnt, float(kp), None, op0=ALU.is_ge))
            p.on(eng, lambda v: v.tensor_tensor(d1, mid, lo, op=ALU.subtract))
            p.on(eng, lambda v: v.tensor_tensor(d1, d1, ge, op=ALU.mult))
            p.on(eng, lambda v: v.tensor_tensor(lo, lo, d1, op=ALU.add))
            p.on(eng, lambda v: v.tensor_tensor(d2, mid, hi, op=ALU.subtract))
            p.on(eng, lambda v: v.tensor_tensor(hi, hi, d2, op=ALU.add))
            p.on(eng, lambda v: v.tensor_tensor(d2, d2, ge, op=ALU.mult))
            p.on(eng, lambda v: v.tensor_tensor(hi, hi, d2, op=ALU.subtract))
        tmask = p.on(eng, lambda v, mask=mask: v.tensor_tensor(
            mask[:, 0:nseg * seg].rearrange("p (a b) -> p a b", b=seg), keys_seg, mb(lo), op=ALU.is_gt))
        return tmask

    routers = [(sQ, 3, 512, KQ, [-1024.0, -512.0, 0.0], mq, "q"),
               (sKV, 1, 1536, KKV, [0.0], mkv, "kv")]
    if os.environ.get("RT_ONLY") == "q":
        routers = routers[:1]
    for args in routers:
        out_toks.append((yield from do_router(*args)))
    if len(out_toks) == 1:
        out_toks.append(out_toks[0])
    if res is not None:
        res["mq"], res["mkv"], res["toks"] = mq, mkv, out_toks


def router_iotas():
    iq = np.zeros((2, 1536), np.float32)
    for w in range(3):
        iq[:, w * 512:(w + 1) * 512] = 1.0 + 512.0 - np.arange(512, dtype=np.float32)
    ikv = np.zeros((2, 1536), np.float32)
    ikv[:, :] = 1.0 + 1536.0 - np.arange(1536, dtype=np.float32)
    return iq, ikv



def phase_l0ffn_a1(nc, p, consts, x_d, lng_d, lnb_d, wql_d, wkl_d, wvl_d, wlo_d,
                   rtq_d, rtkv_d, cos_d, ss_d, tril_d, iota_q_d, iota_kv_d,
                   xout_d, y_d, fgt0_d, fgt1_d, mkvt0_d, mkvt1_d):
    es = contextlib.ExitStack()
    with es:
        cx = Ctx(nc, es, p)
        cx.consts = consts
        trilb = cx.sb("trilb_sb", [128, 128], BF16)
        ttril = p.on("gpsimd", lambda e: e.dma_start(out=trilb[:, :], in_=tril_d),
                     inc=p.sem("gtr"), inc_by=16, kind="dma")
        bigbuf = cx.sb("bigbuf", [128, NC * T], BF16)
        xnT = bigbuf[:, :].rearrange("p (c t) -> p c t", t=T)
        rfacT = cx.sb("rfacT", [128, NT], F32)

        qT = cx.sb("qT", [128, NC, T], BF16)
        kT = cx.sb("kT", [128, NC, T], BF16)
        vp = cx.sb("vp", [128, NT, NH, 65], BF16)
        esR = es.enter_context(contextlib.ExitStack())
        cxR = Ctx(nc, esR, p)
        sQ = cxR.sb("sQ", [2, 1536], F32)
        sKV = cxR.sb("sKV", [2, 1536], F32)
        ikv_sb = cxR.sb("ikv", [2, 1536], F32)
        ones2 = cxR.sb("ones2", [2, 2], F32)
        FtT = cxR.sb("FtT", [128, 2, 12], F32)
        mkvT_sb = cxR.sb("mkvT_sb", [128, 2, 12], F32)
        mq_sb = sQ[:, :]
        mkv_sb2 = sKV[:, :]
        rscr = cxR.sb("rt_scr", [2, 3, 1536], F32)
        rred = cxR.sb("rt_red", [2, 4, 8], F32)
        r2 = rscr[:, 0, :]
        Ft = rscr[:, 1, :]
        rrow = rscr[0:1, 2, :]
        s_rl = p.sem("rtld")
        t_il = p.on("sync", lambda e: e.dma_start(out=ikv_sb[:, :], in_=iota_kv_d), inc=s_rl, inc_by=16, kind="dma")
        esL = es.enter_context(contextlib.ExitStack())
        cxL = Ctx(nc, esL, p); cxL.consts = cx.consts
        gtile = cxL.sb("gtile", [128, D], F32)
        btile = cxL.sb("btile", [128, D], F32)
        rtqT = cxL.sb("rtqT", [128, NC, 2], F32)
        rtkvT = cxL.sb("rtkvT", [128, NC, 2], F32)
        s_lg = p.sem("lg")
        tg3 = p.on("sync", lambda e: e.dma_start(out=rtqT[:, :, :], in_=rtq_d.transpose([1, 0, 2])), inc=p.sem("lg3"), inc_by=16, kind="dma")
        tg4 = p.on("sync", lambda e: e.dma_start(out=rtkvT[:, :, :], in_=rtkv_d.transpose([1, 0, 2])), inc=p.sem("lg4"), inc_by=16, kind="dma")
        with contextlib.ExitStack() as esG:
            cxG = Ctx(nc, esG, p)
            onesf = cxG.sb("onesf", [1, 128], F32)
            growf = cxG.sb("growf", [1, D], F32)
            browf = cxG.sb("browf", [1, D], F32)
            tg1 = p.on("sync", lambda e: e.dma_start(out=growf[:, :], in_=lng_d), inc=s_lg, inc_by=16, kind="dma")
            tg2 = p.on("sync", lambda e: e.dma_start(out=browf[:, :], in_=lnb_d), inc=s_lg, inc_by=16, kind="dma")
            to_f = p.on("vector", lambda v: v.memset(onesf[:, :], 1.0))
            with nc.psum_tensor(uname("psgb"), [128, 2, 512], F32) as psgb:
                gb_toks = []
                for half in range(2):
                    m1 = p.on("tensor", lambda te, half=half: te.matmul(
                        psgb[:, half, :], onesf[:, :], growf[:, half * 512:(half + 1) * 512],
                        start=True, stop=True), inc=p.sem("gbm"), wait=[to_f, tg2])
                    gb_toks.append(p.on("vector", lambda v, half=half: v.tensor_copy(
                        gtile[:, half * 512:(half + 1) * 512], psgb[:, half, :]), wait=[m1]))
                for half in range(2):
                    m1 = p.on("tensor", lambda te, half=half: te.matmul(
                        psgb[:, half, :], onesf[:, :], browf[:, half * 512:(half + 1) * 512],
                        start=True, stop=True), inc=p.sem("gbm"), wait=[gb_toks[-1]])
                    gb_toks.append(p.on("vector", lambda v, half=half: v.tensor_copy(
                        btile[:, half * 512:(half + 1) * 512], psgb[:, half, :]), wait=[m1]))
                gb_tok = gb_toks[-1]
        xb = cxL.sb("xbL", [128, 2, D], F32)
        yf = cxL.sb("yfL", [128, 2, D], F32)
        xnf = cxL.sb("xnfL", [128, D], F32)
        xnb = cxL.sb("xnbL", [128, 2, D], BF16)
        sqs = cxL.sb("sqsL", [128, D], F32)
        stt_ = cxL.sb("stL", [128, NT, 12], F32)
        epsb = cxL.sb("epsL", [128, 1], F32)
        ytr = cxL.sb("ytrL", [128, NC, 128], F32)
        esL2 = es.enter_context(contextlib.ExitStack())
        cxL2 = Ctx(nc, esL2, p)
        psq_r = cxL2.ps("ps_sq", [2, 3, 512], F32)
        pskv_r = cxL2.ps("ps_skv", [2, 3, 512], F32)
        pstf = cxL2.ps("pstfL", [128, 512], F32)
        pstb = cxL2.ps("pstbL", [128, D], BF16)
        s_lx = [p.sem("lxA0"), p.sem("lxA1")]
        s_sy = [p.sem("syA0"), p.sem("syA1")]
        s_t = p.sem("tA")
        s_rm = p.sem("rmA")
        p.on("vector", lambda v: v.memset(epsb[:, :], 1e-5))
        xg = [gb_tok, gb_tok]
        yg = [None, None]
        tys_last = [None, None]
        xng = [None, None]
        ytr_g = None
        pstf_g = None
        pstb_g = None
        rmm_last = None
        for t in range(NT):
            r = t % 2
            tx = p.on("sync", lambda e, t=t, r=r: e.dma_start(
                out=xb[:, r, :], in_=x_d.rearrange("(t p) f -> p t f", p=128)[:, t, :]),
                inc=s_lx[r], inc_by=16, kind="dma", wait=[xg[r]])
            xt = xb[:, r, :]
            S_ = lambda i: stt_[:, t, i:i + 1]
            tv1 = p.on("vector", lambda v, xt=xt, s0=S_(0): v.reduce_sum(s0, xt, axis=AX.X),
                       wait=[tx] + ([cx.consts["tok"], ttril, gb_tok] if t == 0 else []))
            ts1 = p.on("scalar", lambda s, xt=xt, s2=S_(2): s.activation(sqs[:, :], xt, AF.Square, accum_out=s2),
                       wait=[tx])
            p.on("vector", lambda v, s0=S_(0), s1=S_(1): v.tensor_scalar(s1, s0, 1.0 / D, None, op0=ALU.mult))
            p.on("vector", lambda v, s3=S_(3), s1=S_(1): v.scalar_tensor_tensor(
                s3, s1, -1.0, s1, op0=ALU.mult, op1=ALU.mult), wait=[ts1])
            tv3 = p.on("vector", lambda v, s2=S_(2), s3=S_(3): v.scalar_tensor_tensor(
                s3, s2, 1.0 / D, s3, op0=ALU.mult, op1=ALU.add))
            ts2 = p.on("scalar", lambda s, s3=S_(3), s4=S_(4): s.activation(s4, s3, AF.Sqrt, bias=epsb[:, 0:1]),
                       wait=[tv3])
            tv4 = p.on("vector", lambda v, s4=S_(4), s5=S_(5): v.reciprocal(s5, s4), wait=[ts2])
            tv5 = p.on("vector", lambda v, s1=S_(1), s5=S_(5), s6=S_(6): v.scalar_tensor_tensor(
                s6, s1, -1.0, s5, op0=ALU.mult, op1=ALU.mult))
            tyf = p.on("scalar", lambda s, xt=xt, r=r, s5=S_(5), s6=S_(6): s.activation(
                yf[:, r, :], xt, AF.Identity, bias=s6, scale=s5),
                wait=[tv4, tv5] + (yg[r] or []))
            xg[r] = tyf
            tys = p.on("sync", lambda e, t=t, r=r: e.dma_start(
                out=y_d.rearrange("(t p) f -> p t f", p=128)[:, t, :], in_=yf[:, r, :]),
                inc=s_sy[r], inc_by=16, kind="dma", wait=[tyf])
            tys_last[r] = tys
            rmm = None
            ytp_last = None
            for grp in range(2):
                tts = []
                for c4 in range(4):
                    c = grp * 4 + c4
                    tts.append(p.on("tensor", lambda te, r=r, c=c, c4=c4: te.transpose(
                        pstf[:, c4 * 128:(c4 + 1) * 128], yf[:, r, c * 128:(c + 1) * 128],
                        cx.consts["identf"][:, :]), inc=s_t,
                        wait=[tyf, pstf_g] if c4 == 0 else None))
                ytp_last = tts[-1]
                pstf_g = p.on("vector", lambda v, grp=grp: v.tensor_copy(
                    ytr[:, grp * 4:(grp + 1) * 4, :],
                    pstf[:, :].rearrange("p (c t) -> p c t", t=128)), wait=[tts[-1]])
            for c in range(NC):
                if t >= 4:
                    seg, col = (t - 4) // 4, ((t - 4) % 4) * 128
                    rmm = p.on("tensor", lambda te, c=c, seg=seg, col=col: te.matmul(
                        psq_r[:, seg, col:col + 128], rtqT[:, c, :], ytr[:, c, :],
                        start=(c == 0), stop=(c == NC - 1)), inc=s_rm,
                        wait=[pstf_g, tg3] if c == 0 else None)
                if t <= 11:
                    seg, col = t // 4, (t % 4) * 128
                    rmm = p.on("tensor", lambda te, c=c, seg=seg, col=col: te.matmul(
                        pskv_r[:, seg, col:col + 128], rtkvT[:, c, :], ytr[:, c, :],
                        start=(c == 0), stop=(c == NC - 1)), inc=s_rm,
                        wait=[pstf_g, tg4] if c == 0 else None)
            if t == NT - 1:
                rmm_last = rmm
            tv1b = p.on("vector", lambda v, r=r, s0=S_(7): v.reduce_sum(s0, yf[:, r, :], axis=AX.X))
            ts1b = p.on("scalar", lambda s, r=r, s2=S_(8): s.activation(sqs[:, :], yf[:, r, :], AF.Square, accum_out=s2),
                        wait=[tyf])
            p.on("vector", lambda v, s0=S_(7), s1=S_(9): v.tensor_scalar(s1, s0, 1.0 / D, None, op0=ALU.mult))
            p.on("vector", lambda v, s3=S_(10), s1=S_(9): v.scalar_tensor_tensor(
                s3, s1, -1.0, s1, op0=ALU.mult, op1=ALU.mult), wait=[ts1b])
            tv3b = p.on("vector", lambda v, s2=S_(8), s3=S_(10): v.scalar_tensor_tensor(
                s3, s2, 1.0 / D, s3, op0=ALU.mult, op1=ALU.add))
            ts2b = p.on("scalar", lambda s, s3=S_(10), s4=S_(11): s.activation(s4, s3, AF.Sqrt, bias=epsb[:, 0:1]),
                        wait=[tv3b])
            tv4b = p.on("vector", lambda v, s4=S_(11), s5=S_(4): v.reciprocal(s5, s4), wait=[ts2b])
            tv5b = p.on("vector", lambda v, s1=S_(9), s5=S_(4), s6=S_(5): v.scalar_tensor_tensor(
                s6, s1, -1.0, s5, op0=ALU.mult, op1=ALU.mult))
            txn = p.on("scalar", lambda s, r=r, s5=S_(4), s6=S_(5): s.activation(
                xnf[:, :], yf[:, r, :], AF.Identity, bias=s6, scale=s5), wait=[tv4b, tv5b, xng[0] if False else None])
            yg[r] = [txn, tys, ytp_last]
            tgn = p.on("vector", lambda v: v.tensor_tensor(xnf[:, :], xnf[:, :], gtile[:, :], op=ALU.mult),
                       wait=[txn])
            tbn = p.on("vector", lambda v, r=r: v.tensor_tensor(xnb[:, r, :], xnf[:, :], btile[:, :], op=ALU.add),
                       wait=[xng[r]])
            tts = []
            for c in range(NC):
                tts.append(p.on("tensor", lambda te, r=r, c=c: te.transpose(
                    pstb[:, c * 128:(c + 1) * 128], xnb[:, r, c * 128:(c + 1) * 128],
                    cx.consts["identb"][:, :]), inc=s_t,
                    wait=[tbn, pstb_g] if c == 0 else None))
            xng[r] = tts[-1]
            pstb_g = p.on("vector", lambda v, t=t: v.tensor_copy(
                xnT[:, :, t * 128:(t + 1) * 128],
                pstb[:, :].rearrange("p (c t) -> p c t", t=128)), wait=[tts[-1]])
        xn_tok = pstb_g
        tsq = p.on("vector", lambda v: v.tensor_copy(
            sQ[:, :].rearrange("p (a b) -> p a b", b=512), psq_r[:, :, :]), wait=[rmm_last])
        tskv = p.on("vector", lambda v: v.tensor_copy(
            sKV[:, :].rearrange("p (a b) -> p a b", b=512), pskv_r[:, :, :]))
        esL2.close()
        p.wait("vector", tys_last)
        p.barrier()
        esL.close()
        ones_tok = p.on("vector", lambda v: v.memset(vp[:, :, :, :], 1.0))
        esP = es.enter_context(contextlib.ExitStack())
        cxP = Ctx(nc, esP, p); cxP.consts = cx.consts; cxP.shared = {}
        rres = {}
        rgen = emit_router(cx, sQ, sKV, "rt", [tsq, tskv], ikv_sb, t_il,
                           (mq_sb, mkv_sb2, rscr, rred), eng="vector", res=rres)

        def tick(n=3):
            for _ in range(n):
                if next(rgen, StopIteration) is StopIteration:
                    break
        t_q = emit_qk_rope_T(cxP, xnT, wql_d, None, qT, "pq", xn_tok, cos_d, ss_d, "local", tick=tick)
        t_k = emit_qk_rope_T(cxP, xnT, wkl_d, None, kT, "pk", xn_tok, cos_d, ss_d, "local", tick=tick)
        t_v = emit_v_bias(cxP, xnT, wvl_d, None, vp, "pv", xn_tok, ones_tok, tick=tick)
        for _ in rgen:
            pass
        mq, mkv, rt_toks = rres["mq"], rres["mkv"], rres["toks"]
        with nc.psum_tensor(uname("pscnt"), [2, 3, 512], F32) as pscnt, \
             nc.psum_tensor(uname("pstr"), [128, 24], F32) as pstr:
            t_o2 = p.on("vector", lambda v: v.memset(ones2[:, :], 1.0))
            cm = []
            for sl3 in range(3):
                cm.append(p.on("tensor", lambda te, sl3=sl3: te.matmul(
                    pscnt[0:1, sl3, :], ones2[:, 0:1], mq[:, sl3 * 512:(sl3 + 1) * 512],
                    start=True, stop=True), inc=p.sem("cntm"), wait=[t_o2, rt_toks[1]]))
            tcv = p.on("vector", lambda v: v.tensor_copy(
                rrow[:, :].rearrange("p (a b) -> p a b", b=512), pscnt[0:1, :, :]),
                wait=[cm[-1], rt_toks[1]])
            p.on("vector", lambda v: v.tensor_scalar(rrow[:, :], rrow[:, :], 1.0, None, op0=ALU.add))
            trr = p.on("vector", lambda v: v.reciprocal(rrow[:, :], rrow[:, :]))
            trh = p.on("vector", lambda v: v.tensor_scalar(rrow[:, :], rrow[:, :], 0.5, None, op0=ALU.mult))
            rb = []
            for sl3 in range(3):
                mmr = p.on("tensor", lambda te, sl3=sl3: te.matmul(
                    pscnt[:, sl3, :], ones2[0:1, :], rrow[:, sl3 * 512:(sl3 + 1) * 512],
                    start=True, stop=True), inc=p.sem("cntm"), wait=[trh, tcv])
                rb.append(p.on("vector", lambda v, sl3=sl3: v.tensor_copy(
                    r2[:, sl3 * 512:(sl3 + 1) * 512], pscnt[:, sl3, :]), wait=[mmr]))
            tF = p.on("vector", lambda v: v.tensor_tensor(Ft[:, :], mq[:, :], r2[:, :], op=ALU.mult),
                      wait=[rb[-1]])
            tmem = p.on("vector", lambda v: v.memset(rfacT[:, :], 0.5))
            tr_toks = []
            for i in range(12):
                tr_toks.append(p.on("tensor", lambda te, i=i: te.transpose(
                    pstr[:, 0:1], rrow[0:1, i * 128:(i + 1) * 128],
                    cx.consts["identf"][0:1, 0:1]), inc=p.sem("trm"),
                    wait=[trh] if i == 0 else [tr_toks[-1]]))
                tr_toks.append(p.on("vector", lambda v, i=i: v.tensor_copy(
                    rfacT[:, 4 + i:5 + i], pstr[:, 0:1]), wait=[tr_toks[-1], tmem]))
            for (src_t, dst, wtok, tagn) in ((Ft, FtT, tF, "F"), (mkv, mkvT_sb, rt_toks[1], "mkv")):
                for i in range(12):
                    tt = p.on("tensor", lambda te, i=i, src_t=src_t: te.transpose(
                        pstr[:, 0:2], src_t[:, i * 128:(i + 1) * 128], cx.consts["identf"][0:2, 0:2]),
                        inc=p.sem("trm"), wait=[wtok, tr_toks[-1]])
                    tr_toks.append(p.on("vector", lambda v, i=i, dst=dst: v.tensor_copy(
                        dst[:, :, i], pstr[:, 0:2]), wait=[tt]))
            rfac_tok = tr_toks[-1]
            s_so = p.sem("soA")
            so_toks = [
                p.on("sync", lambda e: e.dma_start(out=fgt0_d, in_=FtT[:, 0, :]), inc=s_so, inc_by=16, kind="dma", wait=[rfac_tok]),
                p.on("sync", lambda e: e.dma_start(out=fgt1_d, in_=FtT[:, 1, :]), inc=s_so, inc_by=16, kind="dma"),
                p.on("sync", lambda e: e.dma_start(out=mkvt0_d, in_=mkvT_sb[:, 0, :]), inc=s_so, inc_by=16, kind="dma"),
                p.on("sync", lambda e: e.dma_start(out=mkvt1_d, in_=mkvT_sb[:, 1, :]), inc=s_so, inc_by=16, kind="dma"),
            ]
        p.wait("sync", [so_toks[-1]])
        p.barrier()
        esP.close()
        esR.close()
        esB = es.enter_context(contextlib.ExitStack())
        cxB = Ctx(nc, esB, p); cxB.consts = cx.consts
        ctx_t = bigbuf[:, :].rearrange("p (a b) -> p a b", b=D)
        esAT = es.enter_context(contextlib.ExitStack())
        cxAT = Ctx(nc, esB, p, ps_es=esAT); cxAT.consts = cx.consts
        ctx_tok = emit_attn(cxAT, qT, kT, vp, ctx_t, "at", 1.0 / 8.0,
                            kts_of=lambda qt: range(max(0, 4 * qt - 4), 4 * qt + 4),
                            toks=[t_q, t_k, t_v], extra_mask=trilb)
        esAT.close()
        st_last = emit_outproj_residual(cxB, ctx_t, wlo_d, x_d, xout_d, "op", ctx_tok,
                                        rfac=rfacT, rfac_tok=None)
        p.wait("sync", st_last)
        p.wait("gpsimd", st_last)
    return st_last


def build_prog_l0ffn_a1():
    nc = bass.Bass("TRN2", target_bir_lowering=False, debug=False)
    es = contextlib.ExitStack()
    with es:
        p = Prog(nc, es)
        cx = Ctx(nc, es, p)
        x_d = nc.dram_tensor("x", [T, D], F32, kind="ExternalInput").ap()
        lng_d = nc.dram_tensor("lng", [1, D], F32, kind="ExternalInput").ap()
        lnb_d = nc.dram_tensor("lnb", [1, D], F32, kind="ExternalInput").ap()
        wql_d = nc.dram_tensor("wql", [NC, 128, D], F32, kind="ExternalInput").ap()
        wkl_d = nc.dram_tensor("wkl", [NC, 128, D], F32, kind="ExternalInput").ap()
        wvl_d = nc.dram_tensor("wvl", [NC, 128, D], F32, kind="ExternalInput").ap()
        wlo_d = nc.dram_tensor("wlo", [NC, 128, D], F32, kind="ExternalInput").ap()
        rtq_d = nc.dram_tensor("rtq", [NC, 128, 2], F32, kind="ExternalInput").ap()
        rtkv_d = nc.dram_tensor("rtkv", [NC, 128, 2], F32, kind="ExternalInput").ap()
        cos_d = nc.dram_tensor("cosd", [NT, 128, 64], F32, kind="ExternalInput").ap()
        ss_d = nc.dram_tensor("ssd", [NT, 128, 64], F32, kind="ExternalInput").ap()
        tril_d = nc.dram_tensor("trild", [128, 128], F32, kind="ExternalInput").ap()
        iota_q_d = nc.dram_tensor("iota_q", [2, 1536], F32, kind="ExternalInput").ap()
        iota_kv_d = nc.dram_tensor("iota_kv", [2, 1536], F32, kind="ExternalInput").ap()
        xout_d = nc.dram_tensor("xout", [T, D], F32, kind="ExternalOutput").ap()
        y_d = nc.dram_tensor("yout", [T, D], F32, kind="ExternalOutput").ap()
        fgt0_d = nc.dram_tensor("fgt0", [128, 12], F32, kind="ExternalOutput").ap()
        fgt1_d = nc.dram_tensor("fgt1", [128, 12], F32, kind="ExternalOutput").ap()
        mkvt0_d = nc.dram_tensor("mkvt0", [128, 12], F32, kind="ExternalOutput").ap()
        mkvt1_d = nc.dram_tensor("mkvt1", [128, 12], F32, kind="ExternalOutput").ap()
        consts = load_consts(cx, p.sem("ld"))
        phase_l0ffn_a1(nc, p, consts, x_d, lng_d, lnb_d, wql_d, wkl_d, wvl_d, wlo_d,
                       rtq_d, rtkv_d, cos_d, ss_d, tril_d, iota_q_d, iota_kv_d,
                       xout_d, y_d, fgt0_d, fgt1_d, mkvt0_d, mkvt1_d)
        block = es.enter_context(nc.Block())
        p.finalize(block)
    return nc



def emit_attn_grouped(cx, qT, kT, vp, ctxg, tag, Fg, mkvc, toks):
    p = cx.p
    identf = cx.consts["identf"]
    NS = 4
    psS = cx.ps(tag + "_psS", [128, NS, 512], F32)
    psO = cx.ps(tag + "_psO", [65, 2, 512], F32)
    psT = cx.ps(tag + "_psT", [128, 2, 4, 128], F32)
    pT = cx.sb(tag + "_pT", [128, NS, 512], BF16)
    o65 = cx.sb(tag + "_o65", [65, 2, 512], F32)
    ctxp = cx.sb(tag + "_ctxp", [128, 4, NH, 65], F32)
    rec = cx.sb(tag + "_rec", [128, 16], F32)
    scl = cx.sb(tag + "_scl", [128, 1], F32)
    bia = cx.sb(tag + "_bia", [128, 1], F32)
    mb = cx.sb(tag + "_mb", [128, 13], F32)
    s_mm = p.sem(tag + "mm")
    p.on("vector", lambda v: v.memset(scl[:, :], 0.125))
    c2 = p.on("vector", lambda v: v.memset(bia[:, :], -EXP_B))
    c3 = p.on("vector", lambda v: v.tensor_scalar(
        mb[:, 0:12], mkvc, 300.0, -(300.0 + EXP_B), op0=ALU.mult, op1=ALU.add))
    c4 = p.on("vector", lambda v: v.memset(mb[:, 12:13], -EXP_B))
    p.wait("tensor", toks)
    p.wait("scalar", toks + [c2, c4])
    st = {
        "prev_exp": [None] * NS,
        "prev_pv": [None] * NS,
        "prev_tev_b": [None, None],
        "prev_tts_b": [None, None],
        "prev_scatter_b": [None, None],
        "ctx_done": [],
    }
    items = []
    tails = {}
    for w in (1, 2, 3):
        kts = list(range(4 * w)) + [12] if w < 3 else list(range(13))
        q0 = (w - 1) * 512
        for h in range(NH):
            hc, roff = h // 2, (h % 2) * 64
            ob = h % 2
            for j, kt in enumerate(kts):
                items.append(dict(hc=hc, roff=roff, ob=ob, kt=kt, h=h, q0=q0, w=w,
                                  first=(j == 0), last=(j == len(kts) - 1),
                                  r=len(items) % NS))
            tails[len(items) - 1] = dict(h=h, ob=ob, w=w, q_end=(h == NH - 1))

    def emit_S(i):
        it = items[i]
        r = it["r"]
        tmm = p.on("tensor", lambda te, it=it, r=r: te.matmul(
            psS[:, r, :],
            kT[it["roff"]:it["roff"] + 64, it["hc"], it["kt"] * 128:(it["kt"] + 1) * 128],
            qT[it["roff"]:it["roff"] + 64, it["hc"], it["q0"]:it["q0"] + 512],
            start=True, stop=True), inc=s_mm, wait=[st["prev_exp"][r]])
        use_mb = (it["w"] == 3 and it["kt"] < 12)
        tx = p.on("scalar", lambda s, it=it, r=r, use_mb=use_mb: s.activation(
            pT[:, r, :], psS[:, r, :], AF.Exp,
            bias=(mb[:, it["kt"]:it["kt"] + 1] if use_mb else bia[:, 0:1]),
            scale=scl[:, 0:1]),
            wait=[tmm, st["prev_pv"][r]])
        st["prev_exp"][r] = tx
        it["ready"] = tx

    def emit_PV(i):
        it = items[i]
        r = it["r"]
        ob = it["ob"]
        extra_w = [st["prev_tev_b"][ob]] if it["first"] else []
        tpv = p.on("tensor", lambda te, it=it, r=r, ob=ob: te.matmul(
            psO[:, ob, :], vp[:, it["kt"], it["h"], :], pT[:, r, :],
            start=it["first"], stop=it["last"]), inc=s_mm,
            wait=[it["ready"]] + extra_w)
        st["prev_pv"][r] = tpv
        if it["last"]:
            tev = p.on("vector", lambda v, ob=ob: v.tensor_copy(o65[:, ob, :], psO[:, ob, :]),
                       wait=[tpv, st["prev_tts_b"][ob]])
            st["prev_tev_b"][ob] = tev
            tails[i]["tev"] = tev

    def emit_tail(i):
        td = tails[i]
        h, ob, w = td["h"], td["ob"], td["w"]
        tev = td["tev"]
        ctxp_guard = st["ctx_done"][-1] if st["ctx_done"] else None
        tts = []
        for s4 in range(4):
            tts.append(p.on("tensor", lambda te, s4=s4, ob=ob: te.transpose(
                psT[:, ob, s4, 0:65], o65[:, ob, s4 * 128:(s4 + 1) * 128], identf[0:65, 0:65]),
                inc=s_mm, wait=[tev, st["prev_scatter_b"][ob]] if s4 == 0 else None))
        st["prev_tts_b"][ob] = tts[-1]
        st["prev_scatter_b"][ob] = p.on("vector", lambda v, h=h, ob=ob: v.tensor_copy(
            ctxp[:, :, h, :], psT[:, ob, :, 0:65]),
            wait=[tts[-1], ctxp_guard])
        if td["q_end"]:
            ctok = None
            for s4 in range(4):
                ti = (w - 1) * 4 + s4
                tr = p.on("vector", lambda v, s4=s4: v.reciprocal(
                    rec[:, :], ctxp[:, s4, :, 64]),
                    wait=[st["prev_scatter_b"][0], st["prev_scatter_b"][1]])
                tr2 = p.on("vector", lambda v, ti=ti: v.tensor_scalar(
                    rec[:, :], rec[:, :], Fg[:, ti:ti + 1], None, op0=ALU.mult))
                ctok = p.on("vector", lambda v, s4=s4, ti=ti: v.tensor_tensor(
                    ctxg[:, ti, :].rearrange("p (h d) -> p h d", d=64),
                    ctxp[:, s4, :, 0:64],
                    rec[:, :].unsqueeze(2).broadcast_to((128, 16, 64)), op=ALU.mult),
                    wait=[tr2])
            st["ctx_done"].append(ctok)

    N = len(items)
    LA = 3
    TD = 2
    for i in range(N):
        emit_S(i)
        if i >= LA:
            emit_PV(i - LA)
        if i - LA - TD in tails and i >= LA + TD:
            emit_tail(i - LA - TD)
    for i in range(N - LA, N):
        emit_PV(i)
    for e in sorted(tails):
        if e > N - 1 - LA - TD:
            emit_tail(e)
    return st["ctx_done"][-1]


def phase_l0ffn_a2(nc, p, consts, xp_d, y_d, fgt_d, mkvt_d, wq_d, wkv_d, wo_d,
                   qnT_d, cnT_d, nullk_d, nullv_d, xout_d):
    es = contextlib.ExitStack()
    with es:
        cx = Ctx(nc, es, p)
        cx.consts = consts
        zT = cx.sb("zT", [128, NC, T], BF16)
        Fg_sb = cx.sb("Fg_sb", [128, 2, 12], F32)
        mkv_sb = cx.sb("mkv_sb", [128, 2, 12], F32)
        qn_sb = cx.sb("qn_sb", [128, 2, NC], F32)
        cn_sb = cx.sb("cn_sb", [128, 2, NC], F32)
        s_lf = p.sem("lf")
        tmf = []
        for g in range(2):
            tmf.append(p.on("sync", lambda e, g=g: e.dma_start(out=Fg_sb[:, g, :], in_=fgt_d[g]), inc=s_lf, inc_by=16, kind="dma"))
            tmf.append(p.on("sync", lambda e, g=g: e.dma_start(out=mkv_sb[:, g, :], in_=mkvt_d[g]), inc=s_lf, inc_by=16, kind="dma"))
            tmf.append(p.on("sync", lambda e, g=g: e.dma_start(out=qn_sb[:, g, :], in_=qnT_d[g]), inc=s_lf, inc_by=16, kind="dma"))
            tmf.append(p.on("sync", lambda e, g=g: e.dma_start(out=cn_sb[:, g, :], in_=cnT_d[g]), inc=s_lf, inc_by=16, kind="dma"))
        tload_all = tmf[-1]
        esZ = es.enter_context(contextlib.ExitStack())
        cxZ = Ctx(nc, esZ, p); cxZ.consts = cx.consts
        yb = cxZ.sb("ybZ", [128, 2, D], F32)
        zb = cxZ.sb("zbZ", [128, 2, D], BF16)
        sqs = cxZ.sb("sqZ", [128, D], F32)
        stz = cxZ.sb("stZ", [128, NT, 4], F32)
        pstz = cxZ.ps("pstZ", [128, D], BF16)
        s_ly = [p.sem("lyZ0"), p.sem("lyZ1")]
        s_tz = p.sem("tZ")
        yg = [None, None]
        zg = [None, None]
        ev_prev = None
        for t in range(NT):
            r = t % 2
            ty = p.on("sync", lambda e, t=t, r=r: e.dma_start(
                out=yb[:, r, :], in_=y_d.rearrange("(t p) f -> p t f", p=128)[:, t, :]),
                inc=s_ly[r], inc_by=16, kind="dma", wait=[yg[r]])
            S_ = lambda i: stz[:, t, i:i + 1]
            ts1 = p.on("scalar", lambda s, r=r, s0=S_(0): s.activation(
                sqs[:, :], yb[:, r, :], AF.Square, accum_out=s0),
                wait=[ty] + ([cx.consts["tok"], tload_all] if t == 0 else []))
            ts2 = p.on("scalar", lambda s, s0=S_(0), s1=S_(1): s.activation(s1, s0, AF.Sqrt), wait=[ts1])
            p.on("vector", lambda v, s1=S_(1), s2=S_(2): v.tensor_scalar(s2, s1, 1e-12, None, op0=ALU.max),
                 wait=[ts2])
            tv2 = p.on("vector", lambda v, s2=S_(2), s3=S_(3): v.reciprocal(s3, s2))
            tv3 = p.on("vector", lambda v, s3=S_(3): v.tensor_scalar(s3, s3, 32.0, None, op0=ALU.mult))
            tz = p.on("scalar", lambda s, r=r, s3=S_(3): s.activation(
                zb[:, r, :], yb[:, r, :], AF.Identity, scale=s3), wait=[tv3, zg[r]])
            yg[r] = tz
            tts = []
            for c in range(NC):
                tts.append(p.on("tensor", lambda te, r=r, c=c: te.transpose(
                    pstz[:, c * 128:(c + 1) * 128], zb[:, r, c * 128:(c + 1) * 128],
                    cx.consts["identb"][:, :]), inc=s_tz,
                    wait=[tz, ev_prev] if c == 0 else None))
            zg[r] = tts[-1]
            ev_prev = p.on("vector", lambda v, t=t: v.tensor_copy(
                zT[:, :, t * 128:(t + 1) * 128],
                pstz[:, :].rearrange("p (c t) -> p c t", t=128)), wait=[tts[-1]])
        p.barrier()
        esZ.close()
        esG = es.enter_context(contextlib.ExitStack())
        cxG = Ctx(nc, esG, p); cxG.consts = cx.consts
        qTg = cxG.sb("qTg", [128, NC, 3 * 512], BF16)
        kTg = cxG.sb("kTg", [128, NC, 13 * 128], BF16)
        vpg = cxG.sb("vpg", [128, 13, NH, 65], BF16)
        ctxg = cxG.sb("ctxg", [128, 12, D], BF16)
        st_final = []
        for g in range(2):
            esW = es.enter_context(contextlib.ExitStack())
            cxW = Ctx(nc, esW, p); cxW.consts = cx.consts
            wstage = cxW.sb(f"wst{g}", [128, 2, 2 * D], F32)
            wqb = cxW.sb(f"wqb{g}", [128, NC, D], BF16)
            wkvb = cxW.sb(f"wkvb{g}", [128, NC, 2 * D], BF16)
            psp = cxW.ps(f"psp{g}", [128, 512], F32)
            psv = cxW.ps(f"psv{g}", [128, 2, 512], F32)
            s_lw = [p.sem(f"lw{g}0"), p.sem(f"lw{g}1")]
            s_mp = p.sem(f"mp{g}")
            cast_g = [None, None]
            wq_toks = []
            for c in range(NC):
                r = c % 2
                tl = p.on("sync", lambda e, g=g, c=c, r=r: e.dma_start(
                    out=wstage[:, r, 0:D], in_=wq_d[g, c]),
                    inc=s_lw[r], inc_by=16, kind="dma", wait=[cast_g[r]])
                tc = p.on("vector", lambda v, g=g, c=c, r=r: v.tensor_scalar(
                    wqb[:, c, :], wstage[:, r, 0:D], qn_sb[:, g, c:c + 1], None, op0=ALU.mult),
                    wait=[tl])
                cast_g[r] = tc
                wq_toks.append(tc)
            for c in range(NC):
                r = c % 2
                tl = p.on("sync", lambda e, g=g, c=c, r=r: e.dma_start(
                    out=wstage[:, r, :], in_=wkv_d[g, c]),
                    inc=s_lw[r], inc_by=16, kind="dma", wait=[cast_g[r]])
                tc = p.on("vector", lambda v, g=g, c=c, r=r: v.tensor_scalar(
                    wkvb[:, c, :], wstage[:, r, :], cn_sb[:, g, c:c + 1], None, op0=ALU.mult),
                    wait=[tl])
                cast_g[r] = tc
                wq_toks.append(tc)
            wtok = wq_toks[-1]
            ev_tok = None
            for m in range(NC):
                for n in range(3):
                    mm = None
                    for c in range(NC):
                        w = [wtok, ev_tok] if c == 0 else None
                        mm = p.on("tensor", lambda te, m=m, n=n, c=c: te.matmul(
                            psp[:, :], wqb[:, c, m * 128:(m + 1) * 128],
                            zT[:, c, 512 + n * 512:512 + (n + 1) * 512],
                            start=(c == 0), stop=(c == NC - 1)), inc=s_mp, wait=w)
                    ev_tok = p.on("vector", lambda v, m=m, n=n: v.tensor_copy(
                        qTg[:, m, n * 512:(n + 1) * 512], psp[:, :]), wait=[mm])
            tz1 = p.on("vector", lambda v: v.memset(kTg[:, :, 1537:], 0.0))
            s_nk = p.sem(f"nk{g}")
            tnk = p.on("gpsimd", lambda e, g=g: e.dma_start(
                out=kTg[:, :, 1536:1537], in_=nullk_d[g].unsqueeze(2), allow_slow_non_contiguous=True),
                inc=s_nk, inc_by=16, kind="dma")
            for m in range(NC):
                for n in range(3):
                    mm = None
                    for c in range(NC):
                        w = [wtok, ev_tok] if c == 0 else None
                        mm = p.on("tensor", lambda te, m=m, n=n, c=c: te.matmul(
                            psp[:, :], wkvb[:, c, m * 128:(m + 1) * 128],
                            zT[:, c, n * 512:(n + 1) * 512],
                            start=(c == 0), stop=(c == NC - 1)), inc=s_mp, wait=w)
                    ev_tok = p.on("vector", lambda v, m=m, n=n: v.tensor_copy(
                        kTg[:, m, n * 512:(n + 1) * 512], psp[:, :]), wait=[mm])
            tm1 = p.on("vector", lambda v: v.memset(vpg[:, 0:12, :, :], 1.0))
            tm2 = p.on("vector", lambda v: v.memset(vpg[:, 12, :, :], 0.0))
            tm3 = p.on("vector", lambda v: v.memset(vpg[0:1, 12, :, 64:65], 1.0))
            s_nv = p.sem(f"nv{g}")
            tnv = p.on("gpsimd", lambda e, g=g: e.dma_start(
                out=vpg[0:1, 12, :, 0:64], in_=nullv_d[g].rearrange("p (h d) -> p h d", d=64)),
                inc=s_nv, inc_by=16, kind="dma", wait=[tm3])
            last_ev = {}
            for t in range(12):
                mm_toks = [None, None]
                for half in range(2):
                    for c in range(NC):
                        w = [wtok, tm1] if c == 0 else None
                        if c == 0 and t >= 1:
                            w = [wtok, tm1, last_ev.get((t - 1, half))]
                        mm_toks[half] = p.on("tensor", lambda te, t=t, c=c, half=half: te.matmul(
                            psv[:, half, :], zT[:, c, t * 128:(t + 1) * 128],
                            wkvb[:, c, D + half * 512:D + (half + 1) * 512],
                            start=(c == 0), stop=(c == NC - 1)), inc=s_mp, wait=w)
                for half in range(2):
                    last_ev[(t, half)] = p.on("vector", lambda v, t=t, half=half: v.tensor_copy(
                        vpg[:, t, half * 8:(half + 1) * 8, 0:64],
                        psv[:, half, :].rearrange("p (h d) -> p h d", d=64)),
                        wait=[mm_toks[half]])
            p.wait("vector", [tnk, tnv])
            p.barrier()
            esW.close()
            esA = es.enter_context(contextlib.ExitStack())
            cxA2 = Ctx(nc, esA, p); cxA2.consts = cx.consts
            ctx_tok = emit_attn_grouped(cxA2, qTg, kTg, vpg, ctxg, f"ga{g}",
                                        Fg_sb[:, g, :], mkv_sb[:, g, :], [])
            p.barrier()
            esA.close()
            esO = es.enter_context(contextlib.ExitStack())
            cxO = Ctx(nc, esO, p); cxO.consts = cx.consts
            xsrc = xp_d if g == 0 else xout_d
            st_toks = emit_outproj_grouped(cxO, ctxg, wo_d[g], xsrc, xp_d, xout_d,
                                           f"go{g}", ctx_tok, first=(g == 0))
            st_final.append(st_toks)
            p.wait("vector", st_toks)
            p.barrier()
            esO.close()
        p.wait("sync", st_final[-1])
        p.wait("gpsimd", st_final[-1])
        st_ret = st_final[-1]
    return st_ret


def build_prog_l0ffn_a2():
    nc = bass.Bass("TRN2", target_bir_lowering=False, debug=False)
    es = contextlib.ExitStack()
    with es:
        p = Prog(nc, es)
        cx = Ctx(nc, es, p)
        xp_d = nc.dram_tensor("xp", [T, D], F32, kind="ExternalInput").ap()
        y_d = nc.dram_tensor("y", [T, D], F32, kind="ExternalInput").ap()
        fgt_d = [nc.dram_tensor(f"fgt{g}", [128, 12], F32, kind="ExternalInput").ap() for g in range(2)]
        mkvt_d = [nc.dram_tensor(f"mkvt{g}", [128, 12], F32, kind="ExternalInput").ap() for g in range(2)]
        wq_d = nc.dram_tensor("wqg", [2, NC, 128, D], F32, kind="ExternalInput").ap()
        wkv_d = nc.dram_tensor("wkvg", [2, NC, 128, 2 * D], F32, kind="ExternalInput").ap()
        wo_d = nc.dram_tensor("wog", [2, NC, 128, D], F32, kind="ExternalInput").ap()
        qnT_d = nc.dram_tensor("qnT", [2, 128, NC], F32, kind="ExternalInput").ap()
        cnT_d = nc.dram_tensor("cnT", [2, 128, NC], F32, kind="ExternalInput").ap()
        nullk_d = nc.dram_tensor("nullk", [2, NC, 128], F32, kind="ExternalInput").ap()
        nullv_d = nc.dram_tensor("nullv", [2, 1, D], F32, kind="ExternalInput").ap()
        xout_d = nc.dram_tensor("xout", [T, D], F32, kind="ExternalOutput").ap()
        consts = load_consts(cx, p.sem("ld"))
        phase_l0ffn_a2(nc, p, consts, xp_d, y_d, fgt_d, mkvt_d, wq_d, wkv_d, wo_d,
                       qnT_d, cnT_d, nullk_d, nullv_d, xout_d)
        block = es.enter_context(nc.Block())
        p.finalize(block)
    return nc


def emit_outproj_grouped(cx, ctxg, wo_dram, xsrc_d, xp_d, xout_d, tag, ctx_tok, first):
    p = cx.p
    wb = cx.sb(tag + "_wb", [128, NC, D], BF16)
    ctxT = cx.sb(tag + "_ctxT", [128, NC, 128], BF16)
    xr = cx.sb(tag + "_xr", [128, 2, D], F32)
    xo = cx.sb(tag + "_xo", [128, 2, D], F32)
    psT = cx.ps(tag + "_psT", [128, D], BF16)
    pso = cx.ps(tag + "_pso", [128, 2, 512], F32)
    s_l = [p.sem(tag + "l0"), p.sem(tag + "l1")]
    s_m = p.sem(tag + "m")
    s_st = [p.sem(tag + "st0"), p.sem(tag + "st1")]
    tc = p.on("gpsimd", lambda e: e.dma_start(out=wb[:, :, :], in_=wo_dram.transpose([1, 0, 2])),
              inc=p.sem(tag + "gl"), inc_by=16, kind="dma")
    cp_st = [None, None]
    if first:
        for t in range(4):
            r = t % 2
            tld = p.on("sync", lambda e, t=t, r=r: e.dma_start(
                out=xr[:, r, :], in_=xp_d.rearrange("(t p) f -> p t f", p=128)[:, t, :]),
                inc=s_l[r], inc_by=16, kind="dma", wait=[cp_st[r]])
            cp_st[r] = p.on("sync", lambda e, t=t, r=r: e.dma_start(
                out=xout_d.rearrange("(t p) f -> p t f", p=128)[:, t, :], in_=xr[:, r, :]),
                inc=s_st[r], inc_by=16, kind="dma", wait=[tld])
    fma_hist = {}
    st_hist = {}
    evT_prev = None
    mm_prev = [None, None]
    last_st = None
    for i in range(12):
        t = i + 4
        r = t % 2
        txr = p.on("sync", lambda e, t=t, r=r: e.dma_start(
            out=xr[:, r, :], in_=xsrc_d.rearrange("(t p) f -> p t f", p=128)[:, t, :]),
            inc=s_l[r], inc_by=16, kind="dma",
            wait=[fma_hist.get((i - 2, 0)), fma_hist.get((i - 2, 1)), cp_st[(i + 4) % 2]] if i >= 2 else [cp_st[(i + 4) % 2]])
        tts = []
        for c in range(NC):
            tts.append(p.on("tensor", lambda te, i=i, c=c: te.transpose(
                psT[:, c * 128:(c + 1) * 128], ctxg[:, i, c * 128:(c + 1) * 128],
                cx.consts["identb"][:, :]), inc=s_m,
                wait=[ctx_tok, evT_prev] if c == 0 else None))
        evT = p.on("vector", lambda v: v.tensor_copy(
            ctxT[:, :, :], psT[:, :].rearrange("p (c t) -> p c t", t=128)),
            wait=[tts[-1], mm_prev[0], mm_prev[1]])
        evT_prev = evT
        for half in range(2):
            mm = None
            for c in range(NC):
                w = [evT, tc, fma_hist.get((i - 1, half))] if c == 0 else None
                mm = p.on("tensor", lambda te, c=c, half=half: te.matmul(
                    pso[:, half, :], ctxT[:, c, :], wb[:, c, half * 512:(half + 1) * 512],
                    start=(c == 0), stop=(c == NC - 1)), inc=s_m, wait=w)
            mm_prev[half] = mm
            fma_hist[(i, half)] = p.on("vector", lambda v, r=r, half=half: v.tensor_tensor(
                xo[:, r, half * 512:(half + 1) * 512], xr[:, r, half * 512:(half + 1) * 512],
                pso[:, half, :], op=ALU.add),
                wait=[mm, txr, st_hist.get((i - 2,))])
        st_hist[(i,)] = p.on("sync", lambda e, t=t, r=r: e.dma_start(
            out=xout_d.rearrange("(t p) f -> p t f", p=128)[:, t, :], in_=xo[:, r, :]),
            inc=s_st[r], inc_by=16, kind="dma", wait=[fma_hist[(i, 0)], fma_hist[(i, 1)]])
    return [st_hist[(10,)], st_hist[(11,)]]



def build_prog_fused():
    nc = bass.Bass("TRN2", target_bir_lowering=False, debug=False)
    es = contextlib.ExitStack()
    with es:
        p = Prog(nc, es)
        cx = Ctx(nc, es, p)
        EI = lambda n, s: nc.dram_tensor(n, list(s), F32, kind="ExternalInput").ap()
        IT = lambda n, s: nc.dram_tensor(n, list(s), F32, kind="Internal").ap()
        x_d = EI("x", [T, D])
        xout_d = nc.dram_tensor("xout", [T, D], F32, kind="ExternalOutput").ap()
        lng_d, lnb_d = EI("lng", [1, D]), EI("lnb", [1, D])
        wql_d, wkl_d, wvl_d = EI("wql", [NC, 128, D]), EI("wkl", [NC, 128, D]), EI("wvl", [NC, 128, D])
        wlo_d = EI("wlo", [NC, 128, D])
        rtq_d, rtkv_d = EI("rtq", [NC, 128, 2]), EI("rtkv", [NC, 128, 2])
        cosl_d, ssl_d = EI("cosl", [NT, 128, 64]), EI("ssl", [NT, 128, 64])
        tril_d = EI("trild", [128, 128])
        iota_q_d, iota_kv_d = EI("iota_q", [2, 1536]), EI("iota_kv", [2, 1536])
        wqg_d = EI("wqg", [2, NC, 128, D])
        wkvg_d = EI("wkvg", [2, NC, 128, 2 * D])
        wog_d = EI("wog", [2, NC, 128, D])
        qnT_d, cnT_d = EI("qnT", [2, 128, NC]), EI("cnT", [2, 128, NC])
        nullk_d, nullv_d = EI("nullk", [2, NC, 128]), EI("nullv", [2, 1, D])
        bwq_d, bwk_d, bwv_d, bwo_d = (EI(n, [NC, 128, D]) for n in ("bwq", "bwk", "bwv", "bwo"))
        bbq_d, bbk_d, bbv_d, bbd_d = (EI(n, [1, D]) for n in ("bbq", "bbk", "bbv", "bbd"))
        cosa_d, ssa_d = EI("cosa", [NT, 128, 64]), EI("ssa", [NT, 128, 32])
        wg_d, wu_d = EI("wg", [NIC, 128, D]), EI("wu", [NIC, 128, D])
        wd_d = EI("wd", [NC, 128, II])
        dwq_d, dwk_d, dwv_d, dwo_d = (EI(n, [NC, 128, D]) for n in ("dwq", "dwk", "dwv", "dwo"))
        y_i = IT("y_i", [T, D])
        fgt_i = [IT(f"fgt{g}_i", [128, 12]) for g in range(2)]
        mkvt_i = [IT(f"mkvt{g}_i", [128, 12]) for g in range(2)]
        xs_i = IT("xs_i", [T, D])
        xa_i = IT("xa_i", [T, D])
        xb_i = IT("xb_i", [T, D])
        xc_i = IT("xc_i", [T, D])
        xd_i = IT("xd_i", [T, D])
        xe_i = IT("xe_i", [T, D])
        wcache = {
            "g": nc.dram_tensor("wgq_i", [128, NC, II], BF16, kind="Internal").ap(),
            "u": nc.dram_tensor("wuq_i", [128, NC, II], BF16, kind="Internal").ap(),
            "d": nc.dram_tensor("wdq_i", [128, NIC, D], BF16, kind="Internal").ap(),
        }
        consts = load_consts(cx, p.sem("ld"))
        a1_w = (lng_d, lnb_d, wql_d, wkl_d, wvl_d, wlo_d, rtq_d, rtkv_d,
                cosl_d, ssl_d, tril_d, iota_q_d, iota_kv_d)
        a2_w = (wqg_d, wkvg_d, wog_d, qnT_d, cnT_d, nullk_d, nullv_d)
        st = phase_l0ffn_a1(nc, p, consts, x_d, *a1_w, xs_i, y_i,
                            fgt_i[0], fgt_i[1], mkvt_i[0], mkvt_i[1])
        p.barrier(wait=st)
        st = phase_l0ffn_a2(nc, p, consts, xs_i, y_i, fgt_i, mkvt_i, *a2_w, xa_i)
        p.barrier(wait=st)
        st = phase_l0attn(nc, p, consts, xa_i, bwq_d, bwk_d, bwv_d, bwo_d,
                          bbq_d, bbk_d, bbv_d, bbd_d, cosa_d, ssa_d, xb_i)
        p.barrier(wait=st)
        st = phase_l0ffn_a1(nc, p, consts, xb_i, *a1_w, xs_i, y_i,
                            fgt_i[0], fgt_i[1], mkvt_i[0], mkvt_i[1])
        p.barrier(wait=st)
        st = phase_l0ffn_a2(nc, p, consts, xs_i, y_i, fgt_i, mkvt_i, *a2_w, xc_i)
        p.barrier(wait=st)
        st = phase_l1ffn(nc, p, consts, xc_i, wg_d, wu_d, wd_d, xd_i,
                         wcache=wcache, first=True)
        p.barrier(wait=st)
        st = phase_l1attn(nc, p, consts, xd_i, dwq_d, dwk_d, dwv_d, dwo_d, xe_i)
        p.barrier(wait=st)
        st = phase_l1ffn(nc, p, consts, xe_i, wg_d, wu_d, wd_d, xout_d,
                         wcache=wcache, first=False)
        p.wait("sync", st)
        p.wait("gpsimd", st)
        block = es.enter_context(nc.Block())
        p.finalize(block)
    return nc



_PROG_CACHE = {}


def _get_prog(name):
    if name not in _PROG_CACHE:
        _PROG_CACHE[name] = {"fused": build_prog_fused}[name]()
    return _PROG_CACHE[name]


def _run(name, in_maps):
    from concourse.bass_utils import run_bass_kernel_spmd
    nc = _get_prog(name)
    res = run_bass_kernel_spmd(nc, in_maps, core_ids=list(range(8)))
    return res.results


def kernel(L0_attn_hidden_states, L0_attn_Wq, L0_attn_bq, L0_attn_Wk, L0_attn_bk,
           L0_attn_Wv, L0_attn_bv, L0_attn_Wd, L0_attn_bd, L0_ffn_ln_g, L0_ffn_ln_b,
           L0_ffn_w_qkv, L0_ffn_w_lo, L0_ffn_rt_q, L0_ffn_rt_kv, L0_ffn_qnorm_g,
           L0_ffn_cnorm_g, L0_ffn_null_kv, L0_ffn_w_q, L0_ffn_w_kv, L0_ffn_w_out,
           L1_attn_w_qkv, L1_attn_w_o, L1_ffn_w_gate, L1_ffn_w_up, L1_ffn_w_down):
    f32 = np.float32
    cont = np.ascontiguousarray
    B = 8
    xs = [cont(np.asarray(L0_attn_hidden_states[b], f32)) for b in range(B)]
    consts = host_consts()
    identf, tri = consts["identf"], consts["tri"]
    tril = (np.arange(128)[None, :] <= np.arange(128)[:, None]).astype(f32)
    iq, ikv = router_iotas()

    wt = cont(np.asarray(L0_ffn_w_qkv, f32).T)
    mk = lambda Wm: cont(np.asarray(Wm, f32).reshape(NC, 128, -1))
    COSl, SSl = rope_tables_local()
    a1_common = {
        "lng": cont(np.asarray(L0_ffn_ln_g, f32)[None, :]),
        "lnb": cont(np.asarray(L0_ffn_ln_b, f32)[None, :]),
        "wql": mk(wt[:, 0:D]), "wkl": mk(wt[:, D:2 * D]), "wvl": mk(wt[:, 2 * D:]),
        "wlo": mk(np.asarray(L0_ffn_w_lo, f32).T),
        "rtq": cont(np.asarray(L0_ffn_rt_q, f32).T.reshape(NC, 128, 2)),
        "rtkv": cont(np.asarray(L0_ffn_rt_kv, f32).T.reshape(NC, 128, 2)),
        "cosd": COSl, "ssd": SSl, "trild": tril,
        "identf": identf, "tri": tri, "iota_q": iq, "iota_kv": ikv,
    }
    w_q = np.asarray(L0_ffn_w_q, f32)
    w_kv = np.asarray(L0_ffn_w_kv, f32)
    w_out = np.asarray(L0_ffn_w_out, f32)
    qn = np.asarray(L0_ffn_qnorm_g, f32)
    cn = np.asarray(L0_ffn_cnorm_g, f32)
    nkv = np.asarray(L0_ffn_null_kv, f32)
    a2_common = {
        "wqg": np.stack([mk(w_q[g].T) for g in range(2)]),
        "wkvg": np.stack([mk(w_kv[g].T) for g in range(2)]),
        "wog": np.stack([mk(w_out[g].T) for g in range(2)]),
        "qnT": cont(np.stack([qn[g].reshape(NC, 128).T for g in range(2)])),
        "cnT": cont(np.stack([cn[g].reshape(NC, 128).T for g in range(2)])),
        "nullk": cont(np.stack([nkv[0, g].reshape(D).reshape(NC, 128) for g in range(2)])),
        "nullv": cont(np.stack([nkv[1, g].reshape(1, D) for g in range(2)])),
        "identf": identf, "tri": tri,
    }
    COSa, SSa = rope_tables_l0a()
    b_common = {
        "wq": mk(np.asarray(L0_attn_Wq, f32)), "wk": mk(np.asarray(L0_attn_Wk, f32)),
        "wv": mk(np.asarray(L0_attn_Wv, f32)), "wo": mk(np.asarray(L0_attn_Wd, f32)),
        "bq": cont(np.asarray(L0_attn_bq, f32)[None, :]),
        "bk": cont(np.asarray(L0_attn_bk, f32)[None, :]),
        "bv": cont(np.asarray(L0_attn_bv, f32)[None, :]),
        "bd": cont(np.asarray(L0_attn_bd, f32)[None, :]),
        "cosd": COSa, "ssd": SSa, "identf": identf, "tri": tri,
    }
    c_common = {
        "wg": cont(np.asarray(L1_ffn_w_gate, f32).reshape(NIC, 128, D)),
        "wu": cont(np.asarray(L1_ffn_w_up, f32).reshape(NIC, 128, D)),
        "wd": cont(np.asarray(L1_ffn_w_down, f32).reshape(NC, 128, II)),
        "identf": identf, "tri": tri,
    }
    dq, dk, dv = split_qkv_cols(np.asarray(L1_attn_w_qkv, f32))
    d_common = {
        "wq": dq, "wk": dk, "wv": dv,
        "wo": cont(np.asarray(L1_attn_w_o, f32).reshape(NC, 128, D)),
        "identf": identf, "tri": tri,
    }

    common = {
        "identf": identf, "tri": tri,
        "lng": a1_common["lng"], "lnb": a1_common["lnb"],
        "wql": a1_common["wql"], "wkl": a1_common["wkl"], "wvl": a1_common["wvl"],
        "wlo": a1_common["wlo"], "rtq": a1_common["rtq"], "rtkv": a1_common["rtkv"],
        "cosl": a1_common["cosd"], "ssl": a1_common["ssd"], "trild": a1_common["trild"],
        "iota_q": iq, "iota_kv": ikv,
        "wqg": a2_common["wqg"], "wkvg": a2_common["wkvg"], "wog": a2_common["wog"],
        "qnT": a2_common["qnT"], "cnT": a2_common["cnT"],
        "nullk": a2_common["nullk"], "nullv": a2_common["nullv"],
        "bwq": b_common["wq"], "bwk": b_common["wk"], "bwv": b_common["wv"],
        "bwo": b_common["wo"], "bbq": b_common["bq"], "bbk": b_common["bk"],
        "bbv": b_common["bv"], "bbd": b_common["bd"],
        "cosa": b_common["cosd"], "ssa": b_common["ssd"],
        "wg": c_common["wg"], "wu": c_common["wu"], "wd": c_common["wd"],
        "dwq": d_common["wq"], "dwk": d_common["wk"], "dwv": d_common["wv"],
        "dwo": d_common["wo"],
    }
    r = _run("fused", [dict(common, x=x) for x in xs])
    return np.stack([r[b]["xout"] for b in range(B)]).astype(np.float32)
